# version 10 (final)
# baseline (speedup 1.0000x reference)
import numpy as np
import concourse.bass as bass
import concourse.bacc as bacc
import concourse.mybir as mybir
from concourse.bass_utils import run_bass_kernel_spmd
from concourse.library_config import mlp as MLP_LIB

F32 = mybir.dt.float32
BF16 = mybir.dt.bfloat16
I16 = mybir.dt.int16
U32 = mybir.dt.uint32
AF = mybir.ActivationFunctionType
ALU = mybir.AluOpType
AX = mybir.AxisListType

D_MODEL = 1024
HD = 64
NUM_LEVELS = 4
BLOCK_SIZES = [4, 8, 16, 32]
RANKS = [4, 8, 8, 8]
NUM_NODES = [8, 4, 2, 1]
E = 8
CH = 128
EPS = 1e-5

def _build_topc():
    comps = []
    for l in range(NUM_LEVELS):
        bs, nn, r = BLOCK_SIZES[l], NUM_NODES[l], RANKS[l]
        for n in range(nn):
            base = n * 2 * bs
            comps.append(dict(l=l, n=n, r=r, bs=bs,
                              L=slice(base, base + bs), R=slice(base + bs, base + 2 * bs)))
    return comps

TOPC = _build_topc()
PACKG = [[0, 1, 2, 3], [4, 5, 6, 7], [8, 9, 10, 11], [12, 13, 14]]
LEAFG = [[0, 1, 2, 3], [4, 5, 6, 7], [8, 9, 10, 11], [12, 13, 14, 15]]
OUTCOL = {}
_ob = 0
for _ci, _c in enumerate(TOPC):
    OUTCOL[_ci] = _ob
    _ob += _c['bs']

BANDIX = {}
_bi = 0
for _fam in range(3):
    for _grp in range(4):
        for _si in range(len((LEAFG if _fam == 0 else PACKG)[_grp])):
            BANDIX[(_fam, _grp, _si)] = _bi
            _bi += 1
N_BANDS = _bi


def build_canvases(UL, VR, UR, VL):
    z = np.zeros
    cv = {}
    cv['phi_vr'] = [z((64, 128), np.float32) for _ in range(4)]
    cv['phi_ul'] = [z((64, 128), np.float32) for _ in range(4)]
    cv['phi_vl'] = [z((64, 128), np.float32) for _ in range(4)]
    cv['phi_ult_out'] = [z((128, 128), np.float32) for _ in range(4)]
    cv['phi_urt_out'] = [z((128, 128), np.float32) for _ in range(4)]
    cv['phi_ult_y'] = [z((128, 64), np.float32) for _ in range(4)]
    cv['phi_urt_y'] = [z((128, 64), np.float32) for _ in range(4)]
    cv['wmask'] = [z((128, 128), np.float32) for _ in range(4)]
    for g, members in enumerate(PACKG):
        for si, ci in enumerate(members):
            c = TOPC[ci]
            l, r, bs = c['l'], c['r'], c['bs']
            slot = 32 * si
            cv['phi_vr'][g][c['R'], slot:slot + r] = VR[l]
            cv['phi_ul'][g][c['L'], slot:slot + r] = UL[l]
            cv['phi_vl'][g][c['L'], slot:slot + r] = VL[l]
            cb = OUTCOL[ci]
            cv['phi_ult_out'][g][slot:slot + r, cb:cb + bs] = UL[l].T
            cv['phi_urt_out'][g][slot:slot + r, cb:cb + bs] = UR[l].T
            cv['phi_ult_y'][g][slot:slot + r, c['L']] = UL[l].T
            cv['phi_urt_y'][g][slot:slot + r, c['R']] = UR[l].T
            cv['wmask'][g][slot:slot + r, slot:slot + r] = 1.0
    cv['phi_lf'] = [z((64, 128), np.float32) for _ in range(4)]
    for g, members in enumerate(LEAFG):
        for si, n in enumerate(members):
            slot = 32 * si
            cv['phi_lf'][g][4 * n:4 * n + 4, slot:slot + 4] = np.eye(4, dtype=np.float32)
    m = z((64, 64), np.float32)
    for n in range(16):
        m[4 * n:4 * n + 4, 4 * n:4 * n + 4] = 1.0
    cv['leafmask'] = m
    return cv


def pure_constants(NTT):
    c = {}
    c['ident'] = np.eye(128, dtype=np.float32)
    triu = (np.arange(128)[:, None] <= np.arange(128)[None, :]).astype(np.float32)
    c['triu4'] = np.tile(triu, (1, 4))
    c['tri64s'] = (np.arange(64)[:, None] < np.arange(64)[None, :]).astype(np.float32)
    c['iota8'] = np.broadcast_to(np.arange(8, dtype=np.float32), (128, 8)).copy()
    c['onesc'] = np.ones((1, 128), np.float32)
    c['ones128'] = np.ones((128, 1), np.float32)
    T = 128 * NTT
    ts = np.zeros((128, NTT), np.float32)
    for t in range(T):
        ts[t % 128, t // 128] = t + 1
    c['toksrc'] = ts
    c['cidx0'] = ((np.arange(128) % 16) * (T // 16) + np.arange(128) // 16).astype(np.int32).reshape(128, 1)
    return c


class Prog:
    ENGS = ("tensor", "vector", "scalar", "sync", "gpsimd")

    def __init__(self, cnt, wm=None, gate=None):
        self.ops = {e: [] for e in self.ENGS}
        self.cnt = cnt
        self.wm = wm if wm is not None else {}
        self.gate = gate or {}
        self._gated = set()

    def op(self, eng, fn, waits=(), incs=()):
        toks = {}
        winc = []
        for s, k in incs:
            self.cnt[s] = self.cnt.get(s, 0) + k
            toks[s] = self.cnt[s]
            winc.append((s, k))
        wl = {}
        def add(t):
            if t is None:
                return
            if isinstance(t, dict):
                for s, v in t.items():
                    wl[s] = max(wl.get(s, 0), v)
            elif isinstance(t, (tuple, list)):
                for u in t:
                    add(u)
            else:
                raise TypeError(t)
        add(list(waits))
        if self.gate and eng not in self._gated:
            add(self.gate)
            self._gated.add(eng)
        for s in list(wl):
            if s[0] in ('d', 'g'):
                wl[s] = self.cnt.get(s, 0)
                self.wm[s] = max(self.wm.get(s, 0), wl[s])
        for s, k in incs:
            if s[0] in ('d', 'g'):
                w = self.wm.get(s, 0) - k and self.wm.get(s, 0)
                w = self.wm.get(s, 0)
                prev = self.cnt[s] - k
                if 0 < w <= prev:
                    wl[s] = max(wl.get(s, 0), w)
        self.ops[eng].append((fn, list(wl.items()), winc))
        return toks

    def emit(self, nc, sems):
        with nc.Block() as block:
            def mk(eng_name):
                def inner(eng):
                    for fn, waits, incs in self.ops[eng_name]:
                        for s, v in waits:
                            eng.wait_ge(sems[s], v)
                        inst = fn(eng)
                        for s, k in incs:
                            if inst is None:
                                inst = eng.nop()
                            inst.then_inc(sems[s], k)
                return inner
            for name, deco in (("tensor", block.tensor), ("vector", block.vector),
                               ("scalar", block.scalar), ("sync", block.sync),
                               ("gpsimd", block.gpsimd)):
                if self.ops[name]:
                    deco(mk(name))


def build_program(cfg):
    T = cfg['T']; NB = cfg['NB']; TB = T // NB
    NCH = TB // CH; NTT = T // 128
    SPAN = 512
    CAP = cfg['CAP']
    NEXP = cfg['NEXP']; HF = cfg['H_FFN']
    do_moe = cfg.get('do_moe', True)
    n_cores = cfg.get('n_cores', 8)
    debug_dump = cfg.get('debug_dump', None)
    NTILES = (2 * T + NEXP * 511 + 511) // 512
    GL = NTILES * 512
    GS16 = GL // 16

    NHB = HF // 128
    CPS = SPAN // CH

    nc = bacc.Bacc("TRN2", target_bir_lowering=False, debug=False)
    dtr = nc.dram_tensor

    X = dtr("x", [T, D_MODEL], F32, kind="ExternalInput").ap()
    WQ = dtr("wq", [D_MODEL, HD], F32, kind="ExternalInput").ap()
    WK = dtr("wk", [D_MODEL, HD], F32, kind="ExternalInput").ap()
    WV = dtr("wv", [D_MODEL, HD], F32, kind="ExternalInput").ap()
    WO = dtr("wo", [HD, D_MODEL], F32, kind="ExternalInput").ap()
    WOT = dtr("wot", [D_MODEL, HD], F32, kind="ExternalInput").ap()
    WR = dtr("wr", [D_MODEL, E], F32, kind="ExternalInput").ap()
    BR = dtr("br", [1, E], F32, kind="ExternalInput").ap()
    PHI_VR = dtr("phi_vr", [4, 64, 128], F32, kind="ExternalInput").ap()
    PHI_UL = dtr("phi_ul", [4, 64, 128], F32, kind="ExternalInput").ap()
    PHI_VL = dtr("phi_vl", [4, 64, 128], F32, kind="ExternalInput").ap()
    PHI_LF = dtr("phi_lf", [4, 64, 128], F32, kind="ExternalInput").ap()
    PHI_ULT_OUT = dtr("phi_ult_out", [4, 128, 128], F32, kind="ExternalInput").ap()
    PHI_URT_OUT = dtr("phi_urt_out", [4, 128, 128], F32, kind="ExternalInput").ap()
    PHI_ULT_Y = dtr("phi_ult_y", [4, 128, 64], F32, kind="ExternalInput").ap()
    PHI_URT_Y = dtr("phi_urt_y", [4, 128, 64], F32, kind="ExternalInput").ap()
    WMASKC = dtr("wmaskc", [4, 128, 128], F32, kind="ExternalInput").ap()
    LEAFM = dtr("leafm", [64, 64], F32, kind="ExternalInput").ap()
    IDENT = dtr("ident", [128, 128], F32, kind="ExternalInput").ap()
    TRIU4 = dtr("triu4", [128, 512], F32, kind="ExternalInput").ap()
    TRI64S = dtr("tri64s", [64, 64], F32, kind="ExternalInput").ap()
    IOTA8 = dtr("iota8", [128, E], F32, kind="ExternalInput").ap()
    ONESC = dtr("onesc", [1, 128], F32, kind="ExternalInput").ap()
    ONES128 = dtr("ones128", [128, 1], F32, kind="ExternalInput").ap()
    EPSV = dtr("epsv", [128, 1], F32, kind="ExternalInput").ap()
    PREFM = dtr("prefm", [8, 1], F32, kind="ExternalInput").ap()
    TOKSRC = dtr("toksrc", [128, NTT], F32, kind="ExternalInput").ap()
    SLOTM = dtr("slotm", [128, 4], F32, kind="ExternalInput").ap()
    IOT16 = dtr("iot16", [128, 256], F32, kind="ExternalInput").ap()
    TILV = dtr("tilv", [128, 1], F32, kind="ExternalInput").ap()
    CIDX0 = dtr("cidx0", [128, 1], mybir.dt.int32, kind="ExternalInput").ap()
    W1 = dtr("w1", [NEXP, D_MODEL, HF], F32, kind="ExternalInput").ap()
    W2 = dtr("w2", [NEXP, HF, D_MODEL], F32, kind="ExternalInput").ap()
    B1R = dtr("b1r", [NEXP * 128, 64], F32, kind="ExternalInput").ap()
    B2 = dtr("b2", [NEXP, 1, D_MODEL], F32, kind="ExternalInput").ap()
    W1B = dtr("w1b", [NEXP * D_MODEL, HF], BF16).ap()
    W2B = dtr("w2b", [NEXP * HF, D_MODEL], BF16).ap()

    OUT = dtr("out", [T, D_MODEL], F32, kind="ExternalOutput").ap()

    XBF = dtr("xbf", [T, D_MODEL], BF16).ap()
    LN1F = dtr("ln1f", [T, D_MODEL], F32).ap()
    LN1B = dtr("ln1b", [T, D_MODEL], BF16).ap()
    OUT2 = dtr("out2", [GL, D_MODEL], F32).ap()
    LISTS_F = dtr("lists_f", [GL, 1], F32).ap()
    SCR = dtr("scr", [4096], F32).ap()
    SCR2 = dtr("scr2", [2 * T, 1], F32).ap()
    CC_IN = dtr("cc_in", [1, E], F32)
    CC_OUT = dtr("cc_out", [n_cores, E], F32)

    SEMS = ["sPE", "sDV", "sAC", "sSP", "sCC",
            "dC", "dX0", "dX1", "dS0", "dS1", "dT0", "dT1", "dR", "dL1", "dL2",
            "dP", "dQ", "dO2", "dLN0", "dLN1", "dOUT", "dDBG",
            "gW", "gSC", "gW1a", "gW1b", "gW2", "gXG0", "gXG1", "gG0", "gG1",
            "gLX0", "gLX1", "dZ", "gWB", "gB1", "gB2", "gW2b"]
    cnt = {s: 0 for s in SEMS}

    import contextlib
    stack = contextlib.ExitStack()
    sems = {s: stack.enter_context(nc.semaphore(s)) for s in SEMS}
    sb = lambda *a: stack.enter_context(nc.sbuf_tensor(*a))
    psm = lambda *a: stack.enter_context(nc.psum_tensor(*a))

    with stack:
        c_identf = sb("c_identf", [128, 128], F32)
        c_triub = sb("c_triub", [128, 128], BF16)
        c_tri64b = sb("c_tri64b", [64, 64], BF16)
        c_iota8 = sb("c_iota8", [128, E], F32)
        c_onescf = sb("c_onescf", [1, 128], F32)
        c_ones128f = sb("c_ones128f", [128, 1], F32)
        c_ones128b = sb("c_ones128b", [128, 1], BF16)
        c_prefm = sb("c_prefm", [8, 1], F32)
        c_eps = sb("c_eps", [128, 1], F32)
        logits = sb("logits", [128, NTT, E], F32)
        wgt = sb("wgt", [128, NTT, 2], F32)

        PS = [psm(f"PS{i}", [128, 512], F32) for i in range(3)]
        PS3B = psm("PS3B", [128, 1024], BF16)
        PS = PS + [None] + [psm(f"PS{i}", [128, 512], F32) for i in range(4, 8)]

        with contextlib.ExitStack() as st1:
            sb1 = lambda *a: st1.enter_context(nc.sbuf_tensor(*a))
            c_identb = sb1("c_identb", [128, 128], BF16)
            c_slotm = sb1("c_slotm", [128, 4], F32)
            featQx = sb1("featQx", [128, N_BANDS, 128], BF16)
            c_triu4 = sb1("c_triu4", [128, 512], F32)
            c_tri64f = sb1("c_tri64f", [64, 64], F32)
            c_leafm = sb1("c_leafm", [64, 64], F32)
            srow = sb1("srow", [1, 64], F32)
            phiK = sb1("phiK", [64, 12, 128], BF16)
            phiQ = sb1("phiQ", [64, 12, 128], BF16)
            phiO = sb1("phiO", [128, 8, 128], BF16)
            phiY = sb1("phiY", [128, 8, 64], BF16)
            c_wmask = sb1("c_wmask", [128, 4, 128], F32)
            w_qkvr = sb1("w_qkvr", [128, 8, 3 * HD + E], BF16)
            w_o = sb1("w_o", [64, D_MODEL], BF16)
            w_ot = sb1("w_ot", [128, 8, HD], BF16)
            wowr = sb1("wowr", [64, E], BF16)
            cswr_bc = sb1("cswr_bc", [128, E], F32)
            br_bc = sb1("br_bc", [128, E], F32)
            xbrow = sb1("xbrow", [128, 2, 4, D_MODEL], BF16)
            xres = sb1("xres", [128, 2, D_MODEL], F32)
            xt = sb1("xt", [128, 2, 8, SPAN], BF16)
            qb = sb1("qb", [64, T], BF16)
            kb = sb1("kb", [64, T], BF16)
            vb = sb1("vb", [64, T], BF16)
            xwrt = sb1("xwrt", [8, T], BF16)
            t_elu = sb1("t_elu", [64, 3, SPAN], F32)
            featK = sb1("featK", [128, 12, SPAN], BF16)
            featQ = sb1("featQ", [128, 12, SPAN], BF16)
            sig = sb1("sig", [128, 48, 128], BF16)
            wts = sb1("wts", [128, 2, 128], BF16)
            kvqt = sb1("kvqt", [128, 3, HD], BF16)
            dsc = sb1("dsc", [128, HD], F32)
            ytb = sb1("ytb", [128, HD], BF16)
            ytT = sb1("ytT", [64, 128], BF16)
            zrow = sb1("zrow", [1, HD], F32)
            den = sb1("den", [128, 1], F32)
            den2 = sb1("den2", [128, 1], F32)
            rden = sb1("rden", [128, 1], F32)
            cs_f = sb1("cs_f", [64, 64], F32)
            ct_f = sb1("ct_f", [64, 64], F32)
            csb = sb1("csb", [64, 64], BF16)
            ctb = sb1("ctb", [64, 64], BF16)
            ctm = sb1("ctm", [64, 64], BF16)
            t2b = sb1("t2b", [64, 4, 128], BF16)
            t1b = sb1("t1b", [64, 4, 128], BF16)
            wgb = sb1("wgb", [128, 4, 128], BF16)
            wtgb = sb1("wtgb", [128, 4, 128], BF16)
            rtop = sb1("rtop", [128, 4, 64], BF16)
            rbot = sb1("rbot", [128, 4, 64], BF16)
            r1 = sb1("r1", [128, D_MODEL], F32)
            ln1t = sb1("ln1t", [128, D_MODEL], F32)
            ln1tb = sb1("ln1tb", [128, D_MODEL], BF16)
            bnst = sb1("bnst", [128, 12], F32)
            bnag = sb1("bnag", [128, 2], F32)
            rstd = sb1("rstd", [128, 1], F32)
            nmr = sb1("nmr", [128, 1], F32)
            sqv = sb1("sqv", [128, 1], F32)
            rawt8 = sb1("rawt8", [8, 128], F32)
            lraw = sb1("lraw", [128, E], F32)

            P = Prog(cnt)
            d16 = lambda s="dC": [(s, 16)]

            cl = []
            for dst, src in ((c_identf[:], IDENT[:, :]), (c_triu4[:], TRIU4[:, :]),
                             (c_iota8[:], IOTA8[:, :]), (c_onescf[:], ONESC[:, :]),
                             (c_ones128f[:], ONES128[:, :]), (c_prefm[:], PREFM[:, :]),
                             (c_leafm[:], LEAFM[:, :]), (c_wmask[:], WMASKC[:, :, :]),
                             (c_tri64f[:], TRI64S[:, :]), (c_eps[:, :], EPSV[:, :]),
                             (c_slotm[:, :], SLOTM[:, :])):
                cl.append(P.op("sync", lambda e, d=dst, s=src: e.dma_start(out=d, in_=s),
                               incs=d16("dC")))
            cc = []
            for dst, src in ((c_identb[:], c_identf[:]), (c_triub[:], c_triu4[:, 0:128]),
                             (c_ones128b[:], c_ones128f[:]), (c_tri64b[:], c_tri64f[:])):
                cc.append(P.op("vector", lambda e, d=dst, s=src: e.tensor_copy(d, s),
                               waits=cl, incs=[("sDV", 1)]))

            nc.const_aps.register(F32, float(EPS), c_eps[:, 0:1])
            wlo = []
            def gload(dst, src):
                wlo.append(P.op("gpsimd", lambda e, d=dst, s=src: e.dma_start(out=d, in_=s),
                                incs=d16("gW")))
            gload(w_qkvr[:, :, 0:HD], WQ.rearrange("(k p) h -> p k h", p=128))
            gload(w_qkvr[:, :, HD:2 * HD], WK.rearrange("(k p) h -> p k h", p=128))
            gload(w_qkvr[:, :, 2 * HD:3 * HD], WV.rearrange("(k p) h -> p k h", p=128))
            gload(w_qkvr[:, :, 3 * HD:3 * HD + E], WR.rearrange("(k p) h -> p k h", p=128))
            gload(w_o[:, :], WO[:, :])
            gload(w_ot[:, :, :], WOT.rearrange("(k p) h -> p k h", p=128))
            for g in range(4):
                gload(phiK[:, 0 + g, :], PHI_LF[g])
                gload(phiK[:, 4 + g, :], PHI_VR[g])
                gload(phiK[:, 8 + g, :], PHI_UL[g])
                gload(phiQ[:, 0 + g, :], PHI_LF[g])
                gload(phiQ[:, 4 + g, :], PHI_VR[g])
                gload(phiQ[:, 8 + g, :], PHI_VL[g])
                gload(phiO[:, 0 + g, :], PHI_ULT_OUT[g])
                gload(phiO[:, 4 + g, :], PHI_URT_OUT[g])
                gload(phiY[:, 0 + g, :], PHI_ULT_Y[g])
                gload(phiY[:, 4 + g, :], PHI_URT_Y[g])
            wtok = P.op("sync", lambda e: e.nop(), waits=wlo, incs=[("sSP", 1)])
            ctok = P.op("sync", lambda e: e.nop(), waits=cc, incs=[("sSP", 1)])

            def mm_wowr(e):
                i = None
                for k in range(8):
                    i = e.matmul(PS[6][0:64, 0:E], w_ot[:, k, :],
                                 w_qkvr[:, k, 3 * HD:3 * HD + E], start=(k == 0), stop=(k == 7))
                return i
            m1 = P.op("tensor", mm_wowr, waits=[wtok], incs=[("sPE", 1)])
            def mm_cswr(e):
                i = None
                for k in range(8):
                    i = e.matmul(PS[6][0:1, 64:64 + E], c_ones128b[:, :],
                                 w_qkvr[:, k, 3 * HD:3 * HD + E], start=(k == 0), stop=(k == 7))
                return i
            m2 = P.op("tensor", mm_cswr, waits=[wtok, ctok], incs=[("sPE", 1)])
            e1 = P.op("vector", lambda e: e.tensor_copy(wowr[:], PS[6][0:64, 0:E]),
                      waits=[m1], incs=[("sDV", 1)])
            e2 = P.op("vector", lambda e: e.tensor_copy(srow[0:1, 0:E], PS[6][0:1, 64:64 + E]),
                      waits=[m2], incs=[("sDV", 1)])
            brl = P.op("sync", lambda e: e.dma_start(out=srow[0:1, 8:8 + E], in_=BR[:, :]),
                       incs=d16("dC"))
            def mm_bc(e):
                e.matmul(PS[6][:, 128:128 + E], c_onescf[:, :], srow[0:1, 0:E],
                         start=True, stop=True)
                return e.matmul(PS[6][:, 192:192 + E], c_onescf[:, :], srow[0:1, 8:8 + E],
                                start=True, stop=True)
            m3 = P.op("tensor", mm_bc, waits=[e2, brl, cl], incs=[("sPE", 1)])
            e3 = P.op("vector", lambda e: e.tensor_copy(cswr_bc[:], PS[6][:, 128:128 + E]),
                      waits=[m3], incs=[("sDV", 1)])
            e4 = P.op("vector", lambda e: e.tensor_copy(br_bc[:], PS[6][:, 192:192 + E]),
                      waits=[m3], incs=[("sDV", 1)])
            prep = P.op("sync", lambda e: e.nop(), waits=[e1, e3, e4], incs=[("sSP", 1)])

            prev = {}
            for g in range(T // SPAN):
                buf = g % 2
                r0 = g * SPAN
                lx = P.op("gpsimd", lambda e, r0=r0, buf=buf: e.dma_start(
                    out=xbrow[:, buf], in_=X[r0:r0 + SPAN, :].rearrange("(j p) d -> p j d", p=128)),
                    waits=[prev.get(('st', buf))], incs=d16("gLX%d" % buf))
                st = P.op("sync", lambda e, r0=r0, buf=buf: e.dma_start(
                    out=XBF[r0:r0 + SPAN, :].rearrange("(j p) d -> p j d", p=128),
                    in_=xbrow[:, buf]), waits=[lx], incs=d16("dS%d" % buf))
                prev[('st', buf)] = st
                xts = []
                for d in range(8):
                    xts.append(P.op("sync", lambda e, r0=r0, buf=buf, d=d: e.dma_start(
                        out=xt[:, buf, d, :], in_=XBF[r0:r0 + SPAN, 128 * d:128 * (d + 1)],
                        transpose=True),
                        waits=[st, prev.get(('mmq', buf))], incs=d16("dT%d" % buf)))
                def mm_qkv(e, buf=buf):
                    last = None
                    for p in range(3):
                        for k in range(8):
                            last = e.matmul(PS[p][0:64, 0:SPAN],
                                            w_qkvr[:, k, HD * p:HD * (p + 1)], xt[:, buf, k, :],
                                            start=(k == 0), stop=(k == 7))
                    for k in range(8):
                        last = e.matmul(PS[4][0:8, 0:SPAN], w_qkvr[:, k, 3 * HD:3 * HD + E],
                                        xt[:, buf, k, :], start=(k == 0), stop=(k == 7))
                    return last
                mmq = P.op("tensor", mm_qkv,
                           waits=xts + [wtok, prev.get('evq'), prep if g == 0 else None],
                           incs=[("sPE", 1)])
                prev[('mmq', buf)] = mmq
                sl = slice(g * SPAN, (g + 1) * SPAN)
                o0 = P.op("vector", lambda e: e.tensor_scalar_min(t_elu[:, 0], PS[0][0:64, 0:SPAN], 0.0),
                          waits=[mmq], incs=[("sDV", 1)])
                o1 = P.op("scalar", lambda e: e.activation(t_elu[:, 1], t_elu[:, 0], AF.Exp),
                          waits=[o0], incs=[("sAC", 1)])
                o2 = P.op("scalar", lambda e: e.activation(t_elu[:, 2], PS[0][0:64, 0:SPAN], AF.Relu),
                          waits=[mmq], incs=[("sAC", 1)])
                o3 = P.op("vector", lambda e, sl=sl: e.tensor_add(qb[:, sl], t_elu[:, 1], t_elu[:, 2]),
                          waits=[o1, o2], incs=[("sDV", 1)])
                o4 = P.op("vector", lambda e: e.tensor_scalar_min(t_elu[:, 0], PS[1][0:64, 0:SPAN], 0.0),
                          waits=[mmq, o3], incs=[("sDV", 1)])
                o5 = P.op("scalar", lambda e: e.activation(t_elu[:, 1], t_elu[:, 0], AF.Exp),
                          waits=[o4, o3], incs=[("sAC", 1)])
                o6 = P.op("scalar", lambda e: e.activation(t_elu[:, 2], PS[1][0:64, 0:SPAN], AF.Relu),
                          waits=[mmq, o4, o3], incs=[("sAC", 1)])
                o7 = P.op("vector", lambda e, sl=sl: e.tensor_add(kb[:, sl], t_elu[:, 1], t_elu[:, 2]),
                          waits=[o5, o6], incs=[("sDV", 1)])
                o8 = P.op("vector", lambda e, sl=sl: e.tensor_copy(vb[:, sl], PS[2][0:64, 0:SPAN]),
                          waits=[mmq], incs=[("sDV", 1)])
                o9 = P.op("vector", lambda e, sl=sl: e.tensor_copy(xwrt[:, sl], PS[4][0:8, 0:SPAN]),
                          waits=[mmq], incs=[("sDV", 1)])
                evq = P.op("sync", lambda e: e.nop(), waits=[o3, o7, o8, o9], incs=[("sSP", 1)])
                prev['evq'] = evq
            p0done = prev['evq']

            if do_moe:
                for ex in range(NEXP):
                    P.op("gpsimd", lambda e, ex=ex: e.dma_start(
                        out=W1B[ex * D_MODEL:(ex + 1) * D_MODEL, :], in_=W1[ex]),
                        incs=d16("gWB"))
                    P.op("gpsimd", lambda e, ex=ex: e.dma_start(
                        out=W2B[ex * HF:(ex + 1) * HF, :], in_=W2[ex]),
                        incs=d16("gWB"))

            NW = 12
            batch_carry = None
            pc = {'xrow_used': p0done, 'oproj_used': p0done}
            for b in range(NB):
                ms1 = P.op("vector", lambda e: e.memset(cs_f[:, :], 0.0),
                           waits=[batch_carry], incs=[("sDV", 1)])
                ms2 = P.op("vector", lambda e: e.memset(ct_f[:, :], 0.0),
                           waits=[batch_carry], incs=[("sDV", 1)])
                pc['cadd_tok'] = P.op("sync", lambda e: e.nop(), waits=[ms1, ms2],
                                      incs=[("sSP", 1)])
                for ci in range(NCH):
                    if b * NCH + ci >= cfg.get('max_chunks', 1 << 30):
                        break
                    t0 = b * TB + ci * CH
                    csl = slice(t0, t0 + CH)
                    first = (ci == 0)
                    last_chunk = (ci == NCH - 1)
                    spi = ci % CPS
                    if spi == 0:
                        ssl = slice(t0, t0 + SPAN)
                        fev = []
                        srcs = [vb, vb, kb, qb, qb, qb]
                        phz = [phiK, phiK, phiK, phiQ, phiQ, phiQ]
                        dsts = [featK, featK, featK, featQ, featQ, featQ]
                        for fi in range(6):
                            for g2 in range(4):
                                jj = 4 * fi + g2
                                bank = PS[4] if jj % 2 == 0 else PS[5]
                                mm = P.op("tensor",
                                          lambda e, fi=fi, g2=g2, bank=bank, ssl=ssl:
                                          e.matmul(bank[:, 0:SPAN],
                                                   phz[fi][:, 4 * (fi % 3) + g2, :],
                                                   srcs[fi][:, ssl], start=True, stop=True),
                                          waits=[p0done, ctok, wtok,
                                                 pc.get(('fev', jj % 2)),
                                                 pc.get('feats_used'),
                                                 pc.get('inter_ps_free'),
                                                 pc.get('wts_psum_free'),
                                                 pc.get('lraw_psum_free')],
                                          incs=[("sPE", 1)])
                                if jj % 2 == 0:
                                    ev = P.op("vector",
                                              lambda e, fi=fi, g2=g2, bank=bank:
                                              e.tensor_copy(dsts[fi][:, 4 * (fi % 3) + g2, :],
                                                            bank[:, 0:SPAN]),
                                              waits=[mm], incs=[("sDV", 1)])
                                else:
                                    ev = P.op("scalar",
                                              lambda e, fi=fi, g2=g2, bank=bank:
                                              e.activation(dsts[fi][:, 4 * (fi % 3) + g2, :],
                                                           bank[:, 0:SPAN], AF.Copy),
                                              waits=[mm], incs=[("sAC", 1)])
                                pc[('fev', jj % 2)] = ev
                                fev.append(ev)
                        pc['feats'] = P.op("sync", lambda e: e.nop(), waits=fev, incs=[("sSP", 1)])
                    feats_tok = pc['feats']

                    if cfg.get('p1_stage', 99) < 2:
                        continue
                    trs = []
                    trm = []
                    for j, srcb in enumerate((kb, vb, qb)):
                        mm = P.op("tensor", lambda e, srcb=srcb, j=j, csl=csl:
                                  e.transpose(PS3B[:, 64 * j:64 * (j + 1)], srcb[:, csl],
                                              c_identb[0:64, 0:64]),
                                  waits=[p0done, ctok, pc.get('kvqt_used'), pc.get('ps3_free')],
                                  incs=[("sPE", 1)])
                        trm.append(mm)
                        ev = P.op("scalar", lambda e, j=j: e.activation(
                            kvqt[:, j, :], PS3B[:, 64 * j:64 * (j + 1)], AF.Copy),
                            waits=[mm], incs=[("sAC", 1)])
                        trs.append(ev)
                    tr_tok = P.op("sync", lambda e: e.nop(), waits=trs, incs=[("sSP", 1)])

                    if cfg.get('p1_stage', 99) < 3:
                        continue
                    def mm_cumk(e, first=first):
                        if first:
                            e.matmul(PS[2][0:1, 256:320], c_ones128b[:, :], kvqt[:, 0, :],
                                     start=True, stop=True)
                            return e.matmul(PS[2][:, 64:128], c_triub[:, :], kvqt[:, 0, :],
                                            start=True, stop=True)
                        e.matmul(PS[2][0:1, 256:320], c_ones128b[:, :], kvqt[:, 0, :],
                                 start=True, stop=False)
                        e.matmul(PS[2][0:1, 256:320], c_onescf[0:1, 0:1], zrow[:, :],
                                 start=False, stop=True)
                        e.matmul(PS[2][:, 64:128], c_onescf[:, :], zrow[:, :],
                                 start=True, stop=False)
                        return e.matmul(PS[2][:, 64:128], c_triub[:, :], kvqt[:, 0, :],
                                        start=False, stop=True)
                    mmk = P.op("tensor", mm_cumk,
                               waits=[tr_tok, pc.get('den_done'), pc.get('zrow_done')],
                               incs=[("sPE", 1)])
                    zc = P.op("vector", lambda e: e.tensor_copy(zrow[:, :],
                                                                PS[2][0:1, 256:320]),
                              waits=[mmk], incs=[("sDV", 1)])
                    pc['zrow_done'] = zc

                    if cfg.get('p1_stage', 99) < 4:
                        continue
                    wave_ev = {}
                    wave_mm = {}
                    for w in range(NW):
                        grp = w % 4
                        fam = w // 4
                        bank = PS[w % 2]
                        Kf = featK[:, 4 * fam + grp, 128 * spi:128 * (spi + 1)]
                        Qf = featQ[:, 4 * fam + grp, 128 * spi:128 * (spi + 1)]
                        members = (LEAFG if fam == 0 else PACKG)[grp]
                        mcs = []
                        for si in range(len(members)):
                            gm = BANDIX[(fam, grp, si)]
                            if gm % 2 == 0:
                                mc = P.op("vector", lambda e, gm=gm, si=si, Qf=Qf:
                                          e.tensor_scalar(featQx[:, gm, :], Qf,
                                                          c_slotm[:, si:si + 1], None,
                                                          ALU.mult),
                                          waits=[feats_tok, ctok, pc.get('featqx_used')],
                                          incs=[("sDV", 1)])
                            else:
                                mc = P.op("scalar", lambda e, gm=gm, si=si, Qf=Qf:
                                          e.activation(featQx[:, gm, :], Qf, AF.Copy,
                                                       scale=c_slotm[:, si:si + 1]),
                                          waits=[feats_tok, ctok, pc.get('featqx_used')],
                                          incs=[("sAC", 1)])
                            mcs.append(mc)
                        def mm_sc(e, Kf=Kf, members=members, bank=bank, fam=fam, grp=grp):
                            last = None
                            for si in range(len(members)):
                                gm = BANDIX[(fam, grp, si)]
                                last = e.matmul(bank[:, 128 * si:128 * (si + 1)],
                                                Kf, featQx[:, gm, :],
                                                start=True, stop=True)
                            return last
                        scw = [feats_tok, wave_ev.get(w - 2), pc.get(('scwave', w))] + mcs
                        if w < 2:
                            scw.append(pc.get('oproj_used'))
                        mm = P.op("tensor", mm_sc, waits=scw, incs=[("sPE", 1)])
                        wave_mm[w] = mm
                        ev = P.op("vector", lambda e, w=w, bank=bank: e.tensor_tensor(
                            sig[:, 4 * w:4 * w + 4, :].rearrange("p a b -> p (a b)"),
                            bank[:, 0:512], c_triu4[:, :], ALU.mult),
                            waits=[mm, pc.get('sig_used')], incs=[("sDV", 1)])
                        wave_ev[w] = ev
                        pc[('scwave', w)] = ev
                    pc['featqx_used'] = list(wave_mm.values())
                    sc_tok = P.op("sync", lambda e: e.nop(), waits=list(wave_ev.values()),
                                  incs=[("sSP", 1)])

                    if cfg.get('p1_stage', 99) < 5:
                        continue
                    def mm_wt(e, spi=spi):
                        last = None
                        for g2 in range(4):
                            last = e.matmul(PS[4][:, 0:128],
                                            featK[:, 8 + g2, 128 * spi:128 * (spi + 1)],
                                            phiO[:, 0 + g2, :], start=(g2 == 0), stop=(g2 == 3))
                        for g2 in range(4):
                            last = e.matmul(PS[4][:, 128:256],
                                            featK[:, 4 + g2, 128 * spi:128 * (spi + 1)],
                                            phiO[:, 4 + g2, :], start=(g2 == 0), stop=(g2 == 3))
                        return last
                    mmw = P.op("tensor", mm_wt, waits=[feats_tok, pc.get('wts_used'), trm, pc.get('inter_ps_free'), pc.get(('fev', 0)), pc.get(('fev', 1))],
                               incs=[("sPE", 1)])
                    evw = P.op("scalar", lambda e: e.activation(
                        wts[:, :, :].rearrange("p a b -> p (a b)"), PS[4][:, 0:256], AF.Copy),
                        waits=[mmw], incs=[("sAC", 1)])
                    pc['wts_psum_free'] = evw

                    if cfg.get('p1_stage', 99) < 6:
                        continue
                    if not first:
                        evc = P.op("vector", lambda e: e.tensor_copy(csb[:], cs_f[:, :]),
                                   waits=[pc.get('cadd_tok'), pc.get('inter_used')], incs=[("sDV", 1)])
                        evct = P.op("scalar", lambda e: e.activation(ctb[:], ct_f[:, :], AF.Copy),
                                    waits=[pc.get('cadd_tok'), pc.get('inter_used')], incs=[("sAC", 1)])
                        evcm = P.op("vector", lambda e: e.tensor_tensor(ctm[:], ct_f[:, :],
                                                                        c_leafm[:], ALU.mult),
                                    waits=[pc.get('cadd_tok'), pc.get('inter_used')], incs=[("sDV", 1)])
                        def mm_t2(e):
                            last = None
                            for g2 in range(4):
                                last = e.matmul(PS[4][0:64, 128 * g2:128 * (g2 + 1)], csb[:, :],
                                                phiK[:, 8 + g2, :], start=True, stop=True)
                            return last
                        def mm_t1(e):
                            last = None
                            for g2 in range(4):
                                last = e.matmul(PS[5][0:64, 128 * g2:128 * (g2 + 1)], ctb[:, :],
                                                phiK[:, 4 + g2, :], start=True, stop=True)
                            return last
                        mt2 = P.op("tensor", mm_t2, waits=[evc, feats_tok, evw],
                                   incs=[("sPE", 1)])
                        mt1 = P.op("tensor", mm_t1, waits=[evct, feats_tok, evw, pc.get('lraw_psum_free')],
                                   incs=[("sPE", 1)])
                        et2 = P.op("vector", lambda e: e.tensor_copy(
                            t2b[:, :, :].rearrange("p a b -> p (a b)"), PS[4][0:64, 0:512]),
                            waits=[mt2], incs=[("sDV", 1)])
                        et1 = P.op("scalar", lambda e: e.activation(
                            t1b[:, :, :].rearrange("p a b -> p (a b)"), PS[5][0:64, 0:512], AF.Copy),
                            waits=[mt1], incs=[("sAC", 1)])
                        def mm_wg(e):
                            last = None
                            for g2 in range(4):
                                last = e.matmul(PS[4][:, 128 * g2:128 * (g2 + 1)], t2b[:, g2, :],
                                                phiK[:, 4 + g2, :], start=True, stop=True)
                            return last
                        def mm_wtg(e):
                            last = None
                            for g2 in range(4):
                                last = e.matmul(PS[5][:, 128 * g2:128 * (g2 + 1)], t1b[:, g2, :],
                                                phiK[:, 8 + g2, :], start=True, stop=True)
                            return last
                        mwg = P.op("tensor", mm_wg, waits=[et2, et1], incs=[("sPE", 1)])
                        mwtg = P.op("tensor", mm_wtg, waits=[et1, et2], incs=[("sPE", 1)])
                        ewg = P.op("vector", lambda e: e.tensor_tensor(
                            wgb[:, :, :].rearrange("p a b -> p (a b)"), PS[4][:, 0:512],
                            c_wmask[:, :, :].rearrange("p a b -> p (a b)"), ALU.mult),
                            waits=[mwg], incs=[("sDV", 1)])
                        ewtg = P.op("vector", lambda e: e.tensor_tensor(
                            wtgb[:, :, :].rearrange("p a b -> p (a b)"), PS[5][:, 0:512],
                            c_wmask[:, :, :].rearrange("p a b -> p (a b)"), ALU.mult),
                            waits=[mwtg, ewg], incs=[("sDV", 1)])
                        def mm_rtop(e):
                            last = None
                            for g2 in range(4):
                                last = e.matmul(PS[4][:, 64 * g2:64 * (g2 + 1)], wgb[:, g2, :],
                                                phiY[:, 0 + g2, :], start=True, stop=True)
                            return last
                        def mm_rbot(e):
                            last = None
                            for g2 in range(4):
                                last = e.matmul(PS[5][:, 64 * g2:64 * (g2 + 1)], wtgb[:, g2, :],
                                                phiY[:, 4 + g2, :], start=True, stop=True)
                            return last
                        mrt = P.op("tensor", mm_rtop, waits=[ewg], incs=[("sPE", 1)])
                        mrb = P.op("tensor", mm_rbot, waits=[ewtg], incs=[("sPE", 1)])
                        ert = P.op("vector", lambda e: e.tensor_copy(
                            rtop[:, :, :].rearrange("p a b -> p (a b)"), PS[4][:, 0:256]),
                            waits=[mrt], incs=[("sDV", 1)])
                        erb = P.op("scalar", lambda e: e.activation(
                            rbot[:, :, :].rearrange("p a b -> p (a b)"), PS[5][:, 0:256], AF.Copy),
                            waits=[mrb], incs=[("sAC", 1)])
                        pc['inter_ready'] = P.op("sync", lambda e: e.nop(),
                                                 waits=[ert, erb, evcm], incs=[("sSP", 1)])
                        pc['inter_ps_free'] = [ert, erb]

                    if cfg.get('p1_stage', 99) < 7:
                        continue
                    def mm_apply(e, spi=spi, first=first, csl=csl):
                        last = None
                        started = False
                        for g2 in range(4):
                            for si, n in enumerate(LEAFG[g2]):
                                w = g2
                                last = e.matmul(PS[2][:, 4 * n:4 * n + 4],
                                                sig[:, 4 * w + si, :],
                                                kvqt[:, 0, 4 * n:4 * n + 4],
                                                start=not started, stop=False)
                                started = True
                        for g2 in range(4):
                            for si, cidx in enumerate(PACKG[g2]):
                                c = TOPC[cidx]
                                w = 4 + g2
                                cb = OUTCOL[cidx]
                                last = e.matmul(PS[2][:, c['L']], sig[:, 4 * w + si, :],
                                                wts[:, 0, cb:cb + c['bs']],
                                                start=False, stop=False)
                        for g2 in range(4):
                            for si, cidx in enumerate(PACKG[g2]):
                                c = TOPC[cidx]
                                w = 8 + g2
                                cb = OUTCOL[cidx]
                                st_ = False
                                stp = (not first) is False and g2 == 3 and si == len(PACKG[3]) - 1
                                last = e.matmul(PS[2][:, c['R']], sig[:, 4 * w + si, :],
                                                wts[:, 1, cb:cb + c['bs']],
                                                start=False, stop=stp)
                        if not first:
                            e.matmul(PS[2][:, 0:64], qb[:, csl], ctm[:, :],
                                     start=False, stop=False)
                            for g2 in range(4):
                                e.matmul(PS[2][:, 0:64],
                                         featQ[:, 4 + g2, 128 * spi:128 * (spi + 1)],
                                         rtop[:, g2, :], start=False, stop=False)
                            for g2 in range(3):
                                e.matmul(PS[2][:, 0:64],
                                         featQ[:, 8 + g2, 128 * spi:128 * (spi + 1)],
                                         rbot[:, g2, :], start=False, stop=False)
                            last = e.matmul(PS[2][:, 0:64],
                                            featQ[:, 8 + 3, 128 * spi:128 * (spi + 1)],
                                            rbot[:, 3, :], start=False, stop=True)
                        return last
                    apw = [sc_tok, evw, tr_tok, pc.get('y_done')]
                    if not first:
                        apw.append(pc.get('inter_ready'))
                    mma = P.op("tensor", mm_apply, waits=apw, incs=[("sPE", 1)])
                    pc['sig_used'] = mma
                    pc['wts_used'] = mma
                    pc['feats_used'] = mma
                    pc['inter_used'] = mma

                    if cfg.get('p1_stage', 99) < 8:
                        continue
                    def mm_cu(e):
                        e.matmul(PS[6][0:64, 0:64], kvqt[:, 0, :], kvqt[:, 1, :],
                                 start=True, stop=True)
                        return e.matmul(PS[7][0:64, 0:64], kvqt[:, 1, :], kvqt[:, 0, :],
                                        start=True, stop=True)
                    cuw = [tr_tok, pc.get('cu_psum_free')]
                    mcu = P.op("tensor", mm_cu, waits=cuw, incs=[("sPE", 1)])
                    addw = [mcu, pc.get('cadd_tok')]
                    if not first:
                        addw.append(pc.get('inter_ready'))
                    ca1 = P.op("vector", lambda e: e.tensor_add(cs_f[:, :], cs_f[:, :],
                                                                PS[6][0:64, 0:64]),
                               waits=addw, incs=[("sDV", 1)])
                    ca2 = P.op("vector", lambda e: e.tensor_add(ct_f[:, :], ct_f[:, :],
                                                                PS[7][0:64, 0:64]),
                               waits=addw, incs=[("sDV", 1)])
                    pc['cu_psum_free'] = [ca1, ca2]
                    pc['cadd_tok'] = P.op("sync", lambda e: e.nop(), waits=[ca1, ca2],
                                          incs=[("sSP", 1)])

                    if cfg.get('p1_stage', 99) < 9:
                        continue
                    dden = P.op("vector", lambda e: e.scalar_tensor_tensor(
                        dsc[:, :], PS[2][:, 64:128], 1.0, kvqt[:, 2, :],
                        ALU.mult, ALU.mult, accum_out=den[:, :]),
                        waits=[mmk, tr_tok], incs=[("sDV", 1)])
                    pc['den_done'] = dden
                    b1_ = P.op("sync", lambda e: e.nop(), waits=[dden], incs=[("sSP", 1)])
                    dmax = P.op("vector", lambda e: e.tensor_scalar_max(den2[:, :], den[:, :], 1e-6),
                                waits=[b1_], incs=[("sDV", 1)])
                    b2_ = P.op("sync", lambda e: e.nop(), waits=[dmax], incs=[("sSP", 1)])
                    rcd = P.op("vector", lambda e: e.reciprocal(rden[:, :], den2[:, :]),
                               waits=[b2_], incs=[("sDV", 1)])
                    b2c = P.op("sync", lambda e: e.nop(), waits=[rcd], incs=[("sSP", 1)])
                    ydv = P.op("vector", lambda e: e.tensor_scalar(
                        ytb[:, :], PS[2][:, 0:64], rden[:, 0:1], None, ALU.mult),
                        waits=[b2c, mma], incs=[("sDV", 1)])
                    pc['y_done'] = ydv

                    if cfg.get('p1_stage', 99) < 10:
                        continue
                    myt = P.op("tensor", lambda e: e.transpose(PS3B[0:64, 0:128], ytb[:, :],
                                                               c_identb[:, :]),
                               waits=[ydv, evw], incs=[("sPE", 1)])
                    eyt = P.op("scalar", lambda e: e.activation(ytT[:, :], PS3B[0:64, 0:128],
                                                                AF.Copy),
                               waits=[myt], incs=[("sAC", 1)])
                    def mm_op(e, csl=csl):
                        e.matmul(PS[0][:, 0:512], ytT[:, :], w_o[:, 0:512], start=True, stop=True)
                        e.matmul(PS[1][:, 0:512], ytT[:, :], w_o[:, 512:1024], start=True, stop=True)
                        e.matmul(PS[2][0:8, 128:256], wowr[:, :], ytT[:, :],
                                 start=True, stop=False)
                        return e.matmul(PS[2][0:8, 128:256], c_identb[0:8, 0:8],
                                        xwrt[:, csl], start=False, stop=True)
                    mop = P.op("tensor", mm_op,
                               waits=[eyt, pc.get('oproj_used'), wave_ev[NW - 1], wave_ev[NW - 2],
                                      ydv],
                               incs=[("sPE", 1)])
                    lxr = P.op("sync", lambda e, t0=t0: e.dma_start(
                        out=xres[:, 0, :], in_=X[t0:t0 + CH, :]),
                        waits=[pc.get('xrow_used')], incs=d16("dR"))
                    ra = P.op("vector", lambda e: e.tensor_add(r1[:, 0:512], PS[0][:, 0:512],
                                                               xres[:, 0, 0:512]),
                              waits=[mop, lxr, pc.get('r1_used')], incs=[("sDV", 1)])
                    rb_ = P.op("vector", lambda e: e.tensor_add(r1[:, 512:1024], PS[1][:, 0:512],
                                                                xres[:, 0, 512:1024]),
                               waits=[mop, lxr, pc.get('r1_used')], incs=[("sDV", 1)])
                    pc['oproj_used'] = [ra, rb_]
                    pc['xrow_used'] = [ra, rb_]
                    erw = P.op("vector", lambda e: e.tensor_copy(
                        rawt8[:, :], PS[2][0:8, 128:256]),
                        waits=[mop, pc.get('rawt8_used')], incs=[("sDV", 1)])

                    if cfg.get('p1_stage', 99) < 11:
                        continue
                    bs1 = P.op("vector", lambda e: e.bn_stats(bnst[:, 0:6], r1[:, 0:512]),
                               waits=[ra, rb_], incs=[("sDV", 1)])
                    bs2 = P.op("vector", lambda e: e.bn_stats(bnst[:, 6:12], r1[:, 512:1024]),
                               waits=[ra, rb_], incs=[("sDV", 1)])
                    b3_ = P.op("sync", lambda e: e.nop(), waits=[bs1, bs2], incs=[("sSP", 1)])
                    ag = P.op("vector", lambda e: e.bn_aggr(bnag[:, :], bnst[:, :]),
                              waits=[b3_], incs=[("sDV", 1)])
                    sq = P.op("scalar", lambda e: e.activation(sqv[:, :], bnag[:, 1:2], AF.Sqrt,
                                                               bias=float(EPS)),
                              waits=[ag, ctok], incs=[("sAC", 1)])
                    rc = P.op("vector", lambda e: e.reciprocal(rstd[:, :], sqv[:, :]),
                              waits=[sq], incs=[("sDV", 1)])
                    b4_ = P.op("sync", lambda e: e.nop(), waits=[rc], incs=[("sSP", 1)])
                    nm = P.op("vector", lambda e: e.scalar_tensor_tensor(
                        nmr[:, :], bnag[:, 0:1], -1.0, rstd[:, :], ALU.mult, ALU.mult),
                        waits=[b4_], incs=[("sDV", 1)])
                    b5_ = P.op("sync", lambda e: e.nop(), waits=[nm], incs=[("sSP", 1)])
                    ln = P.op("vector", lambda e: e.tensor_scalar(ln1t[:, :], r1[:, :],
                                                                  rstd[:, 0:1], nmr[:, 0:1],
                                                                  ALU.mult, ALU.add),
                              waits=[b5_, pc.get('ln1_used')], incs=[("sDV", 1)])
                    lnb = P.op("scalar", lambda e: e.activation(ln1tb[:, :], ln1t[:, :], AF.Copy),
                               waits=[ln], incs=[("sAC", 1)])
                    st1_ = P.op("sync", lambda e, t0=t0: e.dma_start(out=LN1F[t0:t0 + CH, :],
                                                                    in_=ln1t[:, :]),
                                waits=[ln], incs=d16("dL1"))
                    st2_ = P.op("sync", lambda e, t0=t0: e.dma_start(out=LN1B[t0:t0 + CH, :],
                                                                    in_=ln1tb[:, :]),
                                waits=[lnb], incs=d16("dL2"))
                    pc['ln1_used'] = [st1_, st2_]
                    pc['r1_used'] = ln

                    if cfg.get('p1_stage', 99) < 12:
                        continue
                    mlt = P.op("tensor", lambda e: e.transpose(PS[5][:, 256:264], rawt8[:, :],
                                                               c_identf[0:8, 0:8]),
                               waits=[erw, eyt,
                                      pc.get('inter_ready') if not first else None],
                               incs=[("sPE", 1)])
                    pc['rawt8_used'] = mlt
                    elt = P.op("vector", lambda e: e.tensor_copy(lraw[:, :], PS[5][:, 256:264]),
                               waits=[mlt, pc.get('lraw_used')], incs=[("sDV", 1)])
                    pc['lraw_psum_free'] = elt
                    lg1 = P.op("scalar", lambda e: e.activation(lraw[:, :], lraw[:, :], AF.Copy,
                                                                scale=rstd[:, 0:1]),
                               waits=[elt, b5_], incs=[("sAC", 1)])
                    tt = t0 // 128
                    lg2_ = P.op("vector", lambda e, tt=tt: e.scalar_tensor_tensor(
                        logits[:, tt, :], cswr_bc[:, :], nmr[:, 0:1], lraw[:, :],
                        ALU.mult, ALU.add),
                        waits=[lg1], incs=[("sDV", 1)])
                    pc['lraw_used'] = lg2_
                    b6_ = P.op("sync", lambda e: e.nop(), waits=[lg2_], incs=[("sSP", 1)])
                    lg3 = P.op("vector", lambda e, tt=tt: e.tensor_add(
                        logits[:, tt, :], logits[:, tt, :], br_bc[:, :]),
                        waits=[b6_], incs=[("sDV", 1)])

                    pc['ps3_free'] = [elt, eyt, evw]
                    pc['kvqt_used'] = [mmk, mma, mcu, ca1, ca2, dden, ydv]
                    if last_chunk:
                        batch_carry = [pc.get('cadd_tok'), pc.get('inter_ready'), lg3]
            scan_done = P.op("sync", lambda e: e.nop(),
                             waits=[{s: cnt.get(s, 0) for s in SEMS if s != "sSP"}],
                             incs=[("sSP", 1)])
            P.emit(nc, sems)

        if do_moe:
            emit_moe(nc, cnt, sems, dict(
                T=T, NTT=NTT, CAP=CAP, NEXP=NEXP, HF=HF,
                NTILES=NTILES, GL=GL, GS16=GS16,
                n_cores=n_cores,
                logits=logits, wgt=wgt, PS=PS,
                c_iota8=c_iota8, c_tri64b=c_tri64b, c_triub=c_triub, c_onescf=c_onescf,
                c_identf=c_identf, c_identb=c_identb, c_prefm=c_prefm,
                c_ones128b=c_ones128b, c_ones128f=c_ones128f, SCR=SCR,
                LN1F=LN1F, LN1B=LN1B, OUT2=OUT2, LISTS_F=LISTS_F,
                CC_IN=CC_IN, CC_OUT=CC_OUT, OUT=OUT, TOKSRC=TOKSRC,
                CIDX0=CIDX0, SCR2=SCR2, IOT16=IOT16, TILV=TILV,
                dbg_p2=cfg.get('dbg_p2'), dbg_p3=cfg.get('dbg_p3'), dbg_wb=cfg.get('dbg_wb'), dbg_g1=cfg.get('dbg_g1'),
                W1B=W1B, W2B=W2B, B1R=B1R, B2=B2, nc=nc))
        else:
            P = Prog(cnt)
            src = LN1F if debug_dump in (None, 'ln1') else LN1F
            sts = []
            for i in range(T // 128):
                sts.append(P.op("sync", lambda e, i=i: e.dma_start(
                    out=OUT[128 * i:128 * (i + 1), :], in_=src[128 * i:128 * (i + 1), :]),
                    incs=[("dDBG", 16)]))
            P.op("sync", lambda e: e.nop(), waits=[{"dDBG": cnt["dDBG"]}], incs=[("sSP", 1)])
            P.emit(nc, sems)

    nc.compile()
    return nc


def emit_moe(nc, cnt, sems, env):
    import contextlib
    T = env['T']; NTT = env['NTT']; CAP = env['CAP']
    NEXP = env['NEXP']; HF = env['HF']
    NTILES = env['NTILES']; GL = env['GL']; GS16 = env['GS16']
    n_cores = env['n_cores']
    logits = env['logits']; wgt = env['wgt']; PS = env['PS']
    c_iota8 = env['c_iota8']; c_tri64b = env['c_tri64b']; c_onescf = env['c_onescf']
    c_identf = env['c_identf']; c_ones128b = env['c_ones128b']; c_prefm = env['c_prefm']
    LN1F = env['LN1F']; LN1B = env['LN1B']; OUT2 = env['OUT2']
    LISTS_F = env['LISTS_F']
    CC_IN = env['CC_IN']; CC_OUT = env['CC_OUT']; OUT = env['OUT']; TOKSRC = env['TOKSRC']
    W1B = env['W1B']; W2B = env['W2B']; B1R = env['B1R']; B2 = env['B2']
    IOT16 = env['IOT16']; TILV = env['TILV']
    SEMS = list(cnt.keys())
    HQ = HF // 4
    NHBQ = HF // 512

    with contextlib.ExitStack() as st2:
        sbP = lambda *a: st2.enter_context(nc.sbuf_tensor(*a))
        cbidx = sbP("cbidx", [128, 2 * (T // 16) + 16], I16)
        stF = st2.enter_context(contextlib.ExitStack())
        sb2 = lambda *a: stF.enter_context(nc.sbuf_tensor(*a))
        top8 = sb2("top8", [128, E], F32)
        idx8 = sb2("idx8", [128, E], U32)
        e01s = sb2("e01s", [128, NTT, 2], F32)
        g01s = sb2("g01s", [128, NTT, 2], F32)
        dgap = sb2("dgap", [128, 1], F32)
        e01f = sb2("e01f", [128, 2], F32)
        s0b = sb2("s0b", [128, E], F32)
        s1b = sb2("s1b", [128, E], F32)
        selsum = sb2("selsum", [128, E], F32)
        selbf = sb2("selbf", [128, E], BF16)
        cums = sb2("cums", [128, NTT, E], F32)
        sels = sb2("sels", [128, NTT, E], F32)
        csrow = sb2("csrow", [1, NTT * E], F32)
        colsf = sb2("colsf", [NTT, E], F32)
        colsb = sb2("colsb", [NTT, E], BF16)
        toffs = sb2("toffs", [NTT, E], F32)
        trow = sb2("trow", [1, NTT * E], F32)
        countsr = sb2("countsr", [1, E], F32)
        agc = sb2("agc", [n_cores, E], F32)
        agm = sb2("agm", [n_cores, E], F32)
        corow = sb2("corow", [1, E], F32)
        cobc = sb2("cobc", [128, E], F32)
        rloc = sb2("rloc", [128, E], F32)
        lrank8 = sb2("lrank8", [128, E], F32)
        grank8 = sb2("grank8", [128, E], F32)
        keep8 = sb2("keep8", [128, E], F32)
        skb = sb2("skb", [128, 2, E], F32)
        junk8 = sb2("junk8", [128, 6, E], F32)
        lr01 = sb2("lr01", [128, 2], F32)
        kp01 = sb2("kp01", [128, 2], F32)
        li32 = sb2("li32", [128, 2], mybir.dt.int32)
        lm16 = sb2("lm16", [128, 2], mybir.dt.int32)
        ld16 = sb2("ld16", [128, 2], mybir.dt.int32)
        lu32 = sb2("lu32", [128, 2], mybir.dt.int32)
        sdi32 = sb2("sdi32", [128, 2], mybir.dt.int32)
        sdall = sb2("sdall", [128, 2 * NTT], mybir.dt.int32)
        cball = sb2("cball", [128, 2 * NTT], F32)
        cidx0 = sb2("cidx0_sb", [128, 1], mybir.dt.int32)
        cidxt = sb2("cidxt", [128, 2, 2], mybir.dt.int32)
        cb16f = sb2("cb16f", [128, T // 16], F32)
        lst16 = sb2("lst16", [128, GS16 // 4], F32)
        TOKSRC_sb = sb2("toksrc_sb", [128, NTT], F32)
        zeros = sb2("zeros", [128, GL // 128], F32)
        glist = sb2("glist", [128, GS16 + 16], I16)
        I32 = mybir.dt.int32
        c_tilv = sb2("c_tilv", [128, 1], F32)
        iotsb = sb2("iotsb", [128, 256], F32)
        ccol = sb2("ccol", [8, 1], F32)
        pca = sb2("pca", [8, 1], F32)
        pci = sb2("pci", [8, 1], I32)
        pcc = sb2("pcc", [8, 1], F32)
        pcnt = sb2("pcnt", [8, 1], F32)
        pcntb = sb2("pcntb", [8, 1], BF16)
        tbcol = sb2("tbcol", [8, 1], F32)
        tbrow = sb2("tbrow", [1, 8], F32)
        tbbc = sb2("tbbc", [128, 8], F32)
        til8 = sb2("til8", [128, 8], F32)
        etcol = sb2("etcol", [128, 1], F32)
        etrow = sb2("etrow", [1, NTILES], F32)
        etbc = sb2("etbc", [128, NTILES], F32)
        etw1f = sb2("etw1f", [128, NTILES], F32)
        etw2f = sb2("etw2f", [128, NTILES], F32)
        etb1f = sb2("etb1f", [128, NTILES], F32)
        tbt = sb2("tbt", [128, 2], F32)
        tbt32 = sb2("tbt32", [128, 2], I32)
        gi32 = sb2("gi32", [128, 2], I32)
        iw1x = sb2("iw1x", [128, 2, 64], I16)
        iw2x = sb2("iw2x", [128, 2, 256], I16)
        ib1x = sb2("ib1x", [128, 2, 8], I16)
        ib2x = sb2("ib2x", [128, 2, 8], I16)
        iw1f = sb2("iw1f", [128, 2, 64], F32)
        iw2f = sb2("iw2f", [128, 2, 256], F32)
        ib1f = sb2("ib1f", [128, 2, 8], F32)
        ib2f = sb2("ib2f", [128, 2, 8], F32)
        w1h = sb2("w1h", [128, 2, 8, HF // 4], BF16)
        w2f = sb2("w2f", [128, HF // 128, D_MODEL], BF16)
        xg = sb2("xg", [128, 2, 8, 512], BF16)
        hsb = sb2("hsb", [128, HF // 128, 512], BF16)
        b1t = sb2("b1t", [128, 2, 1, 64], F32)
        b2g = sb2("b2g", [128, 2, 1, D_MODEL], F32)
        o2r = sb2("o2r", [128, 2, D_MODEL], F32)

        P = Prog(cnt, gate={"sSP": cnt["sSP"]})
        d16 = lambda s="dC": [(s, 16)]
        pcm = {}
        msi2 = P.op("vector", lambda e: e.memset(cbidx[:, :], 0),
                    incs=[("sDV", 1)])
        msi3 = P.op("vector", lambda e: e.memset(glist[:, :], 0), incs=[("sDV", 1)])
        msi_tok = P.op("sync", lambda e: e.nop(), waits=[msi2, msi3], incs=[("sSP", 1)])

        tl = P.op("sync", lambda e: e.dma_start(out=TOKSRC_sb[:, :], in_=TOKSRC[:, :]),
                  incs=d16("dP"))
        tvl = P.op("sync", lambda e: e.dma_start(out=c_tilv[:, :], in_=TILV[:, :]),
                   incs=d16("dP"))
        iol = P.op("sync", lambda e: e.dma_start(out=iotsb[:, :], in_=IOT16[:, :]),
                   incs=d16("dP"))
        tile_toks = []
        for tt in range(NTT):
            mx = P.op("vector", lambda e, tt=tt: e.max(top8[:, :], logits[:, tt, :]),
                      waits=[pcm.get('tb')], incs=[("sDV", 1)])
            b1 = P.op("sync", lambda e: e.nop(), waits=[mx], incs=[("sSP", 1)])
            mi = P.op("vector", lambda e, tt=tt: e.max_index(idx8[:, :], top8[:, :],
                                                             logits[:, tt, :]),
                      waits=[b1], incs=[("sDV", 1)])
            d01 = P.op("vector", lambda e: e.tensor_sub(dgap[:, :], top8[:, 0:1], top8[:, 1:2]),
                       waits=[b1], incs=[("sDV", 1)])
            g0 = P.op("scalar", lambda e, tt=tt: e.activation(g01s[:, tt, 0:1], dgap[:, :],
                                                              AF.Sigmoid),
                      waits=[d01], incs=[("sAC", 1)])
            g1 = P.op("scalar", lambda e, tt=tt: e.activation(g01s[:, tt, 1:2], dgap[:, :],
                                                              AF.Sigmoid, scale=-1.0),
                      waits=[d01], incs=[("sAC", 1)])
            b2_ = P.op("sync", lambda e: e.nop(), waits=[mi, d01], incs=[("sSP", 1)])
            ef = P.op("vector", lambda e, tt=tt: e.tensor_copy(e01s[:, tt, :], idx8[:, 0:2]),
                      waits=[b2_], incs=[("sDV", 1)])
            b3_ = P.op("sync", lambda e: e.nop(), waits=[ef], incs=[("sSP", 1)])
            sl0 = P.op("vector", lambda e, tt=tt: e.tensor_scalar(
                s0b[:, :], c_iota8[:, :], e01s[:, tt, 0:1], None, ALU.is_equal),
                waits=[b3_], incs=[("sDV", 1)])
            sl1 = P.op("vector", lambda e, tt=tt: e.tensor_scalar(
                s1b[:, :], c_iota8[:, :], e01s[:, tt, 1:2], None, ALU.is_equal),
                waits=[b3_], incs=[("sDV", 1)])
            b4_ = P.op("sync", lambda e: e.nop(), waits=[sl0, sl1], incs=[("sSP", 1)])
            sm = P.op("vector", lambda e, tt=tt: e.tensor_add(sels[:, tt, :], s0b[:, :], s1b[:, :]),
                      waits=[b4_], incs=[("sDV", 1)])
            sc = P.op("scalar", lambda e, tt=tt: e.activation(selbf[:, :], sels[:, tt, :], AF.Copy),
                      waits=[sm, pcm.get('selbf_used')], incs=[("sAC", 1)])
            mcs = P.op("tensor", lambda e, tt=tt: e.matmul(PS[4][:, 0:E], env['c_triub'][:, :],
                                                           selbf[:, :], start=True, stop=True),
                       waits=[sc, pcm.get('cums_psum_free')], incs=[("sPE", 1)])
            mcol = P.op("tensor", lambda e, tt=tt: e.matmul(PS[5][0:1, E * tt:E * (tt + 1)],
                                                            c_ones128b[:, :], selbf[:, :],
                                                            start=True, stop=True),
                        waits=[sc], incs=[("sPE", 1)])
            pcm['selbf_used'] = [mcs, mcol]
            ecs = P.op("vector", lambda e, tt=tt: e.tensor_copy(cums[:, tt, :], PS[4][:, 0:E]),
                       waits=[mcs], incs=[("sDV", 1)])
            pcm['cums_psum_free'] = ecs
            pcm['tb'] = [ecs]
            tile_toks.append([ecs, mcol, g0, g1, sm])
        ecr = P.op("vector", lambda e: e.tensor_copy(csrow[:, :], PS[5][0:1, 0:NTT * E]),
                   waits=tile_toks, incs=[("sDV", 1)])
        SCR = env['SCR']
        sc1 = P.op("sync", lambda e: e.dma_start(
            out=SCR[0:NTT * E].rearrange("(a b) -> a b", a=1), in_=csrow[:, :]),
            waits=[ecr], incs=d16("dQ"))
        sc2 = P.op("sync", lambda e: e.dma_start(
            out=colsf[:, :], in_=SCR[0:NTT * E].rearrange("(a b) -> a b", a=NTT)),
            waits=[sc1], incs=d16("dP"))
        cbf = P.op("vector", lambda e: e.tensor_copy(colsb[:, :], colsf[:, :]),
                   waits=[sc2], incs=[("sDV", 1)])
        mto = P.op("tensor", lambda e: e.matmul(PS[4][0:NTT, 0:E], c_tri64b[0:NTT, 0:NTT],
                                                colsb[:, :], start=True, stop=True),
                   waits=[cbf, pcm.get('cums_psum_free')], incs=[("sPE", 1)])
        mcnt = P.op("tensor", lambda e: e.matmul(PS[4][0:1, 64:64 + E], c_ones128b[0:NTT, :],
                                                 colsb[:, :], start=True, stop=True),
                    waits=[cbf], incs=[("sPE", 1)])
        eto = P.op("vector", lambda e: e.tensor_copy(toffs[:, :], PS[4][0:NTT, 0:E]),
                   waits=[mto], incs=[("sDV", 1)])
        ecn = P.op("vector", lambda e: e.tensor_copy(countsr[:, :], PS[4][0:1, 64:64 + E]),
                   waits=[mcnt], incs=[("sDV", 1)])
        st5 = P.op("sync", lambda e: e.dma_start(
            out=SCR[1024:1032].rearrange("(a b) -> a b", a=1), in_=countsr[:, :]),
            waits=[ecn], incs=d16("dQ"))
        lt5 = P.op("sync", lambda e: e.dma_start(
            out=ccol[:, :], in_=SCR[1024:1032].rearrange("(a b) -> a b", a=8)),
            waits=[st5], incs=d16("dP"))
        pc1 = P.op("vector", lambda e: e.tensor_scalar_add(pca[:, :], ccol[:, :], 511.0),
                   waits=[lt5], incs=[("sDV", 1)])
        pb1 = P.op("sync", lambda e: e.nop(), waits=[pc1], incs=[("sSP", 1)])
        pc3 = P.op("vector", lambda e: e.tensor_copy(pci[:, :], pca[:, :]),
                   waits=[pb1], incs=[("sDV", 1)])
        pb2 = P.op("sync", lambda e: e.nop(), waits=[pc3], incs=[("sSP", 1)])
        pc3b = P.op("vector", lambda e: e.tensor_scalar(pci[:, :], pci[:, :], ~511, None,
                                                        ALU.bitwise_and),
                    waits=[pb2], incs=[("sDV", 1)])
        pb3 = P.op("sync", lambda e: e.nop(), waits=[pc3b], incs=[("sSP", 1)])
        pc4 = P.op("vector", lambda e: e.tensor_copy(pcnt[:, :], pci[:, :]),
                   waits=[pb3], incs=[("sDV", 1)])
        pcb = P.op("vector", lambda e: e.tensor_copy(pcntb[:, :], pcnt[:, :]),
                   waits=[pc4], incs=[("sDV", 1)])
        mtb = P.op("tensor", lambda e: e.matmul(PS[6][0:8, 8:9], c_tri64b[0:8, 0:8],
                                                pcntb[:, :], start=True, stop=True),
                   waits=[pcb], incs=[("sPE", 1)])
        etb = P.op("vector", lambda e: e.tensor_copy(tbcol[:, :], PS[6][0:8, 8:9]),
                   waits=[mtb], incs=[("sDV", 1)])
        st6 = P.op("sync", lambda e: e.dma_start(
            out=SCR[1040:1048].rearrange("(a b) -> a b", a=8), in_=tbcol[:, :]),
            waits=[etb], incs=d16("dQ"))
        lt6 = P.op("sync", lambda e: e.dma_start(
            out=tbrow[:, :], in_=SCR[1040:1048].rearrange("(a b) -> a b", a=1)),
            waits=[st6], incs=d16("dP"))
        mtbb = P.op("tensor", lambda e: e.matmul(PS[6][:, 16:24], c_onescf[:, :],
                                                 tbrow[:, :], start=True, stop=True),
                    waits=[lt6], incs=[("sPE", 1)])
        etbb = P.op("vector", lambda e: e.tensor_copy(tbbc[:, :], PS[6][:, 16:24]),
                    waits=[mtbb], incs=[("sDV", 1)])
        tle = P.op("vector", lambda e: e.tensor_scalar(til8[:, :], tbbc[:, :],
                                                       c_tilv[:, 0:1], None, ALU.is_le),
                   waits=[etbb, tvl], incs=[("sDV", 1)])
        tb0 = P.op("sync", lambda e: e.nop(), waits=[tle], incs=[("sSP", 1)])
        ets = P.op("scalar", lambda e: e.activation(til8[:, :], til8[:, :], AF.Copy,
                                                    accum_out=etcol[:, :]),
                   waits=[tb0], incs=[("sAC", 1)])
        tb1 = P.op("sync", lambda e: e.nop(), waits=[ets], incs=[("sSP", 1)])
        ets2 = P.op("vector", lambda e: e.tensor_scalar_add(etcol[:, :], etcol[:, :], -1.0),
                    waits=[tb1], incs=[("sDV", 1)])
        st7 = P.op("sync", lambda e: e.dma_start(
            out=SCR[1056:1056 + NTILES].rearrange("(a b) -> a b", b=1),
            in_=etcol[0:NTILES, :]),
            waits=[ets2], incs=d16("dQ"))
        lt7 = P.op("sync", lambda e: e.dma_start(
            out=etrow[:, :], in_=SCR[1056:1056 + NTILES].rearrange("(a b) -> a b", a=1)),
            waits=[st7], incs=d16("dP"))
        metb = P.op("tensor", lambda e: e.matmul(PS[6][:, 64:64 + NTILES], c_onescf[:, :],
                                                 etrow[:, :], start=True, stop=True),
                    waits=[lt7], incs=[("sPE", 1)])
        eetb = P.op("vector", lambda e: e.tensor_copy(etbc[:, :], PS[6][:, 64:64 + NTILES]),
                    waits=[metb], incs=[("sDV", 1)])
        tb2 = P.op("sync", lambda e: e.nop(), waits=[eetb], incs=[("sSP", 1)])
        eiw1 = P.op("vector", lambda e: e.tensor_scalar(etw1f[:, :], etbc[:, :],
                                                        float(D_MODEL), None, ALU.mult),
                    waits=[tb2], incs=[("sDV", 1)])
        eiw2 = P.op("vector", lambda e: e.tensor_scalar(etw2f[:, :], etbc[:, :],
                                                        float(HF), None, ALU.mult),
                    waits=[tb2], incs=[("sDV", 1)])
        eib1 = P.op("vector", lambda e: e.tensor_scalar(etb1f[:, :], etbc[:, :],
                                                        128.0, None, ALU.mult),
                    waits=[tb2], incs=[("sDV", 1)])
        tb_tok = P.op("sync", lambda e: e.nop(), waits=[eiw1, eiw2, eib1],
                      incs=[("sSP", 1)])
        st3 = P.op("sync", lambda e: e.dma_start(
            out=SCR[0:NTT * E].rearrange("(a b) -> a b", a=NTT), in_=toffs[:, :]),
            waits=[eto, sc2], incs=d16("dQ"))
        lt3 = P.op("sync", lambda e: e.dma_start(
            out=trow[:, :], in_=SCR[0:NTT * E].rearrange("(a b) -> a b", a=1)),
            waits=[st3], incs=d16("dP"))
        st4 = P.op("sync", lambda e: e.dma_start(out=CC_IN.ap()[:, :], in_=countsr[:, :]),
                   waits=[ecn], incs=d16("dQ"))
        ccw = {"dQ": cnt["dQ"]}
        cc = P.op("gpsimd", lambda e: e.collective_compute(
            "AllGather", ALU.bypass, replica_groups=[list(range(n_cores))],
            ins=[CC_IN.ap().opt()], outs=[CC_OUT.ap().opt()]),
            waits=[st4, ccw], incs=[("sCC", 1)])
        lag = P.op("sync", lambda e: e.dma_start(out=agc[:, :], in_=CC_OUT.ap()[:, :]),
                   waits=[cc], incs=d16("dP"))
        mskc = P.op("vector", lambda e: e.tensor_scalar(agm[:, :], agc[:, :],
                                                        c_prefm[0:n_cores, 0:1], None, ALU.mult),
                    waits=[lag], incs=[("sDV", 1)])
        c_ones128f = env['c_ones128f']
        mco = P.op("tensor", lambda e: e.matmul(PS[4][0:1, 128:128 + E],
                                                c_ones128f[0:n_cores, 0:1],
                                                agm[:, :], start=True, stop=True),
                   waits=[mskc, eto, ecn], incs=[("sPE", 1)])
        eco = P.op("vector", lambda e: e.tensor_copy(corow[:, :], PS[4][0:1, 128:128 + E]),
                   waits=[mco], incs=[("sDV", 1)])
        mcob = P.op("tensor", lambda e: e.matmul(PS[4][:, 192:192 + E], c_onescf[:, :],
                                                 corow[:, :], start=True, stop=True),
                    waits=[eco], incs=[("sPE", 1)])
        ecob = P.op("vector", lambda e: e.tensor_copy(cobc[:, :], PS[4][:, 192:192 + E]),
                    waits=[mcob], incs=[("sDV", 1)])
        p2mid = P.op("sync", lambda e: e.nop(), waits=[lt3, ecob], incs=[("sSP", 1)])

        CIDX0 = env['CIDX0']; SCR2 = env['SCR2']
        lci = P.op("sync", lambda e: e.dma_start(out=cidx0[:, :], in_=CIDX0[:, :]),
                   incs=d16("dP"))
        zz = P.op("vector", lambda e: e.memset(zeros[:, :], 0.0), incs=[("sDV", 1)])
        zs = P.op("sync", lambda e: e.dma_start(
            out=LISTS_F[:, 0].rearrange("(p c) -> p c", p=128), in_=zeros[:, :]),
            waits=[zz], incs=d16("dQ"))
        zs_tok = {"dQ": cnt["dQ"]}
        sc_toks = []
        for tt in range(NTT):
            ofb = P.op("tensor", lambda e, tt=tt: e.matmul(
                PS[6][:, 0:E], c_onescf[:, :], trow[0:1, E * tt:E * (tt + 1)],
                start=True, stop=True),
                waits=[p2mid, pcm.get('ofb_free')], incs=[("sPE", 1)])
            rk = P.op("vector", lambda e, tt=tt: e.tensor_sub(rloc[:, :], cums[:, tt, :],
                                                              sels[:, tt, :]),
                      waits=[pcm.get('rk_free'), p2mid], incs=[("sDV", 1)])
            lra = P.op("vector", lambda e: e.tensor_add(lrank8[:, :], rloc[:, :], PS[6][:, 0:E]),
                       waits=[ofb, rk], incs=[("sDV", 1)])
            pcm['ofb_free'] = lra
            gra = P.op("vector", lambda e: e.tensor_add(grank8[:, :], lrank8[:, :], cobc[:, :]),
                       waits=[lra], incs=[("sDV", 1)])
            kp = P.op("vector", lambda e: e.tensor_scalar(keep8[:, :], grank8[:, :],
                                                          float(CAP), None, ALU.is_lt),
                      waits=[gra], incs=[("sDV", 1)])
            sk0 = P.op("vector", lambda e, tt=tt: e.tensor_scalar(
                skb[:, 0, :], c_iota8[:, :], e01s[:, tt, 0:1], None, ALU.is_equal),
                waits=[p2mid, pcm.get('skb_free')], incs=[("sDV", 1)])
            sk1 = P.op("vector", lambda e, tt=tt: e.tensor_scalar(
                skb[:, 1, :], c_iota8[:, :], e01s[:, tt, 1:2], None, ALU.is_equal),
                waits=[p2mid, pcm.get('skb_free')], incs=[("sDV", 1)])
            b5_ = P.op("sync", lambda e: e.nop(), waits=[kp, sk0, sk1], incs=[("sSP", 1)])
            ex = []
            for k2 in range(2):
                ex.append(P.op("vector", lambda e, k2=k2: e.scalar_tensor_tensor(
                    junk8[:, k2, :], skb[:, k2, :], 1.0, lrank8[:, :], ALU.mult, ALU.mult,
                    accum_out=lr01[:, k2:k2 + 1]),
                    waits=[b5_], incs=[("sDV", 1)]))
            for k2 in range(2):
                ex.append(P.op("vector", lambda e, k2=k2: e.scalar_tensor_tensor(
                    junk8[:, 2 + k2, :], skb[:, k2, :], 1.0, keep8[:, :], ALU.mult, ALU.mult,
                    accum_out=kp01[:, k2:k2 + 1]),
                    waits=[b5_, ex[-1]], incs=[("sDV", 1)]))
            for k2 in range(2):
                ex.append(P.op("vector", lambda e, k2=k2: e.scalar_tensor_tensor(
                    junk8[:, 4 + k2, :], skb[:, k2, :], 1.0, tbbc[:, :], ALU.mult, ALU.mult,
                    accum_out=tbt[:, k2:k2 + 1]),
                    waits=[b5_, etbb, pcm.get('tbt_used'), ex[-1]], incs=[("sDV", 1)]))
            pcm['rk_free'] = ex
            pcm['skb_free'] = ex
            b6_ = P.op("sync", lambda e: e.nop(), waits=ex, incs=[("sSP", 1)])
            wv = P.op("vector", lambda e, tt=tt: e.tensor_mul(
                wgt[:, tt, :], kp01[:, :], g01s[:, tt, :]),
                waits=[b6_], incs=[("sDV", 1)])
            cbv = P.op("vector", lambda e, tt=tt: e.scalar_tensor_tensor(
                cball[:, tt::NTT], lr01[:, :], 1.0, tbt[:, :], ALU.mult, ALU.add),
                waits=[b6_, pcm.get(('cbsc', tt - 2))], incs=[("sDV", 1)])
            li = P.op("vector", lambda e: e.tensor_copy(li32[:, :], lr01[:, :]),
                      waits=[b6_], incs=[("sDV", 1)])
            tbi = P.op("vector", lambda e: e.tensor_copy(tbt32[:, :], tbt[:, :]),
                      waits=[b6_], incs=[("sDV", 1)])
            pcm['tbt_used'] = [cbv, tbi]
            b7_ = P.op("sync", lambda e: e.nop(), waits=[li, tbi], incs=[("sSP", 1)])
            gq = P.op("vector", lambda e: e.tensor_add(gi32[:, :], li32[:, :], tbt32[:, :]),
                      waits=[b7_], incs=[("sDV", 1)])
            b7b = P.op("sync", lambda e: e.nop(), waits=[gq], incs=[("sSP", 1)])
            m2_ = P.op("vector", lambda e: e.tensor_scalar(lm16[:, :], gi32[:, :], 15, None,
                                                           ALU.bitwise_and),
                       waits=[b7b], incs=[("sDV", 1)])
            d2_ = P.op("vector", lambda e: e.tensor_scalar(ld16[:, :], gi32[:, :], 4, None,
                                                           ALU.logical_shift_right),
                       waits=[b7b], incs=[("sDV", 1)])
            b8_ = P.op("sync", lambda e: e.nop(), waits=[m2_, d2_], incs=[("sSP", 1)])
            u_ = P.op("vector", lambda e: e.scalar_tensor_tensor(
                lu32[:, :], lm16[:, :], GS16, ld16[:, :], ALU.mult, ALU.add),
                waits=[b8_], incs=[("sDV", 1)])
            b9_ = P.op("sync", lambda e: e.nop(), waits=[u_], incs=[("sSP", 1)])
            sdq = P.op("vector", lambda e, tt=tt: e.tensor_copy(
                sdall[:, tt::NTT], lu32[:, :]),
                waits=[b9_, pcm.get(('lsc', tt - 2))], incs=[("sDV", 1)])
            for k2 in range(2):
                t1 = P.op("gpsimd", lambda e, tt=tt, k2=k2: e.indirect_dma_start(
                    LISTS_F[:, :], bass.IndirectOffsetOnAxis(sdall[:, k2 * NTT + tt:k2 * NTT + tt + 1], 0),
                    TOKSRC_sb[:, tt:tt + 1], None),
                    waits=[sdq, zs_tok, tl], incs=d16("gSC"))
                pcm[('lsc', tt)] = t1 if k2 == 0 else [pcm[('lsc', tt)], t1]
                cia = P.op("vector", lambda e, tt=tt, k2=k2: e.tensor_scalar_add(
                    cidxt[:, tt % 2, k2:k2 + 1], cidx0[:, :], k2 * T + 8 * tt),
                    waits=[lci, pcm.get(('cbsc', tt - 2))], incs=[("sDV", 1)])
                t2 = P.op("gpsimd", lambda e, tt=tt, k2=k2: e.indirect_dma_start(
                    SCR2[:, :], bass.IndirectOffsetOnAxis(cidxt[:, tt % 2, k2:k2 + 1], 0),
                    cball[:, k2 * NTT + tt:k2 * NTT + tt + 1], None),
                    waits=[cbv, cia], incs=d16("gSC"))
                pcm[('cbsc', tt)] = t2 if k2 == 0 else [pcm[('cbsc', tt)], t2]
            sc_toks.append([pcm[('lsc', tt)], pcm[('cbsc', tt)], wv])
        scat_tok = {"gSC": cnt["gSC"]}
        cbc_ = []
        prev_cp = None
        for k2 in range(2):
            lds = [P.op("sync", lambda e, k2=k2, g=g: e.dma_start(
                out=cb16f[16 * g:16 * (g + 1), :],
                in_=SCR2[k2 * T:(k2 + 1) * T, 0].rearrange("(p c) -> p c", p=16)),
                waits=[scat_tok, prev_cp], incs=d16("dP")) for g in range(8)]
            cp = P.op("vector", lambda e, k2=k2: e.tensor_copy(
                cbidx[:, k2 * (T // 16):(k2 + 1) * (T // 16)], cb16f[:, :]),
                waits=[lds, msi_tok], incs=[("sDV", 1)])
            cbc_.append(cp)
            prev_cp = cp
        GQ = GS16 // 4
        lm3s = []
        prev_lm3 = None
        for k2 in range(4):
            llf = [P.op("sync", lambda e, k2=k2, g=g: e.dma_start(
                out=lst16[16 * g:16 * (g + 1), :],
                in_=bass.AP(LISTS_F.tensor, k2 * GQ, [[GS16, 16], [1, GQ]])),
                waits=[scat_tok, prev_lm3], incs=d16("dP")) for g in range(8)]
            lm1 = P.op("vector", lambda e: e.tensor_scalar_add(lst16[:, :], lst16[:, :], -1.0),
                       waits=[llf], incs=[("sDV", 1)])
            lm2 = P.op("scalar", lambda e: e.activation(lst16[:, :], lst16[:, :], AF.Relu),
                       waits=[lm1], incs=[("sAC", 1)])
            lm3 = P.op("vector", lambda e, k2=k2: e.tensor_copy(
                glist[:, k2 * GQ:(k2 + 1) * GQ], lst16[:, :]),
                waits=[lm2, msi_tok], incs=[("sDV", 1)])
            lm3s.append(lm3)
            prev_lm3 = lm3
        p2done = P.op("sync", lambda e: e.nop(), waits=[lm3s, cbc_], incs=[("sSP", 1)])

        wbtok = {"gWB": cnt.get("gWB", 0)}
        if env.get('dbg_wb'):
            for r0 in range(0, 512, 512):
                P.op("gpsimd", lambda e, r0=r0: e.dma_start(
                    out=OUT[r0:r0 + 512, 0:HF], in_=W1B[r0:r0 + 512, :]),
                    waits=[wbtok], incs=d16("gSC"))
                P.op("gpsimd", lambda e, r0=r0: e.dma_start(
                    out=OUT[1024 + r0:1024 + r0 + 512, :], in_=W2B[r0:r0 + 512, :]),
                    waits=[wbtok], incs=d16("gSC"))
            P.op("sync", lambda e: e.nop(),
                 waits=[{s: cnt.get(s, 0) for s in SEMS if s != "sSP"}], incs=[("sSP", 1)])
            P.emit(nc, sems)
            return
        if env.get('dbg_p2'):
            dbg_rows = [(countsr, 8),
                        (tbrow, 8), (etrow, NTILES), (trow, NTT * E),
                        (corow, 8)]
            dops = []
            for ri, (src, w) in enumerate(dbg_rows):
                dops.append(P.op("sync", lambda e, ri=ri, src=src, w=w: e.dma_start(
                    out=OUT[ri:ri + 1, 0:w], in_=src[0:1, 0:w]),
                    waits=[p2done, tb_tok], incs=d16("dOUT")))
            gls = sb2("glsdbg", [128, 512], F32)
            gc_ = P.op("vector", lambda e: e.tensor_copy(gls[:, :], glist[:, 0:512]),
                       waits=[p2done], incs=[("sDV", 1)])
            dops.append(P.op("sync", lambda e: e.dma_start(
                out=OUT[8:136, 0:512], in_=gls[:, :]), waits=[gc_], incs=d16("dOUT")))
            cbs = sb2("cbsdbg", [128, 2 * (T // 16)], F32)
            cc2 = P.op("vector", lambda e: e.tensor_copy(cbs[:, :], cbidx[:, 0:2 * (T // 16)]),
                       waits=[p2done], incs=[("sDV", 1)])
            dops.append(P.op("sync", lambda e: e.dma_start(
                out=OUT[136:264, 0:2 * (T // 16)], in_=cbs[:, :]),
                waits=[cc2], incs=d16("dOUT")))
            P.op("sync", lambda e: e.nop(),
                 waits=[{s: cnt.get(s, 0) for s in SEMS if s != "sSP"}], incs=[("sSP", 1)])
            P.emit(nc, sems)
            return
        for t in range(NTILES):
            xbuf = t % 2
            ib = t % 2
            iwd = [tb_tok, iol, pcm.get(('idx_used', ib))]
            fw1 = P.op("vector", lambda e, t=t, ib=ib: e.tensor_scalar(
                iw1f[:, ib, :], iotsb[:, 0:64], etw1f[:, t:t + 1], None, ALU.add),
                waits=iwd, incs=[("sDV", 1)])
            fw2 = P.op("vector", lambda e, t=t, ib=ib: e.tensor_scalar(
                iw2f[:, ib, 0:HF // 16], iotsb[:, 0:HF // 16], etw2f[:, t:t + 1], None, ALU.add),
                waits=iwd, incs=[("sDV", 1)])
            fb1 = P.op("vector", lambda e, t=t, ib=ib: e.tensor_scalar(
                ib1f[:, ib, :], iotsb[:, 0:8], etb1f[:, t:t + 1], None, ALU.add),
                waits=iwd, incs=[("sDV", 1)])
            fb2 = P.op("vector", lambda e, t=t, ib=ib: e.tensor_scalar(
                ib2f[:, ib, :], iotsb[:, 0:8], 0.0, etbc[:, t:t + 1], ALU.mult, ALU.add),
                waits=iwd, incs=[("sDV", 1)])
            jbb = P.op("sync", lambda e: e.nop(), waits=[fw1, fw2, fb1, fb2],
                       incs=[("sSP", 1)])
            jw1 = P.op("vector", lambda e, ib=ib: e.tensor_copy(iw1x[:, ib, :],
                                                                iw1f[:, ib, :]),
                       waits=[jbb], incs=[("sDV", 1)])
            jw2 = P.op("vector", lambda e, ib=ib: e.tensor_copy(iw2x[:, ib, 0:HF // 16],
                                                                iw2f[:, ib, 0:HF // 16]),
                       waits=[jbb], incs=[("sDV", 1)])
            jb1 = P.op("vector", lambda e, ib=ib: e.tensor_copy(ib1x[:, ib, :],
                                                                ib1f[:, ib, :]),
                       waits=[jbb], incs=[("sDV", 1)])
            jb2 = P.op("vector", lambda e, ib=ib: e.tensor_copy(ib2x[:, ib, :],
                                                                ib2f[:, ib, :]),
                       waits=[jbb], incs=[("sDV", 1)])
            gx = P.op("gpsimd", lambda e, t=t, xbuf=xbuf: e.dma_gather(
                xg[:, xbuf], LN1B[:, :], glist[:, 32 * t:32 * (t + 1)],
                num_idxs=512, num_idxs_reg=512, elem_size=D_MODEL, transpose=True),
                waits=[p2done, pcm.get(('xg_used', xbuf))],
                incs=d16("gXG%d" % xbuf))
            gb1 = P.op("gpsimd", lambda e, ib=ib: e.dma_gather(
                b1t[:, ib], B1R[:, :], ib1x[:, ib, :],
                num_idxs=128, num_idxs_reg=128, elem_size=64),
                waits=[jb1, pcm.get(('b1_used', ib))], incs=d16("gB1"))
            gb2 = P.op("gpsimd", lambda e, ib=ib: e.dma_gather(
                b2g[:, ib], B2[:, 0, :], ib2x[:, ib, :],
                num_idxs=128, num_idxs_reg=128, elem_size=D_MODEL),
                waits=[jb2, pcm.get(('b2_used', ib))], incs=d16("gB2"))
            NW2 = HF // 4
            HT4 = HF // 512
            if env.get('dbg_g1') and t == 0:
                ixf = sb2("ixfdbg", [128, 256], F32)
                c1_ = P.op("vector", lambda e: e.tensor_copy(ixf[:, 0:64], iw1x[:, 0, :]),
                           waits=[jw1], incs=[("sDV", 1)])
                c2_ = P.op("vector", lambda e: e.tensor_copy(ixf[:, 64:64 + HF // 16],
                                                             iw2x[:, 0, 0:HF // 16]),
                           waits=[jw2], incs=[("sDV", 1)])
                c3_ = P.op("vector", lambda e: e.tensor_copy(ixf[:, 240:248], ib1x[:, 0, :]),
                           waits=[jb1], incs=[("sDV", 1)])
                c4_ = P.op("vector", lambda e: e.tensor_copy(ixf[:, 248:256], ib2x[:, 0, :]),
                           waits=[jb2], incs=[("sDV", 1)])
                d1_ = P.op("sync", lambda e: e.dma_start(out=OUT[0:128, 0:256],
                                                         in_=ixf[:, :]),
                           waits=[c1_, c2_, c3_, c4_], incs=d16("dOUT"))
                gxx = P.op("gpsimd", lambda e: e.dma_gather(
                    xg[:, 0], LN1B[:, :], glist[:, 0:32],
                    num_idxs=512, num_idxs_reg=512, elem_size=D_MODEL, transpose=True),
                    waits=[p2done], incs=d16("gXG0"))
                gw1d = P.op("gpsimd", lambda e: e.dma_gather(
                    w1h[:, 0], bass.AP(W1B.tensor, 0, [[HF, NEXP * D_MODEL], [1, HQ]]),
                    iw1x[:, 0, :],
                    num_idxs=D_MODEL, num_idxs_reg=D_MODEL, elem_size=HQ, elem_step=HF),
                    waits=[jw1, wbtok], incs=d16("gW1a"))
                gw2d = P.op("gpsimd", lambda e: e.dma_gather(
                    w2f[:, 0:HF // 512, :], W2B[:, :], iw2x[:, 0, 0:HF // 64],
                    num_idxs=HF // 4, num_idxs_reg=HF // 4, elem_size=D_MODEL),
                    waits=[jw2, wbtok], incs=d16("gW2"))
                gb1d = P.op("gpsimd", lambda e: e.dma_gather(
                    b1t[:, 0], B1R[:, :], ib1x[:, 0, :],
                    num_idxs=128, num_idxs_reg=128, elem_size=64),
                    waits=[jb1], incs=d16("gB1"))
                gb2d = P.op("gpsimd", lambda e: e.dma_gather(
                    b2g[:, 0], B2[:, 0, :], ib2x[:, 0, :],
                    num_idxs=128, num_idxs_reg=128, elem_size=D_MODEL),
                    waits=[jb2], incs=d16("gB2"))
                w1f_ = sb2("w1fdbg", [128, 8, HQ], F32)
                cw1 = P.op("vector", lambda e: e.tensor_copy(w1f_[:, :, :], w1h[:, 0]),
                           waits=[gw1d], incs=[("sDV", 1)])
                d2_ = P.op("sync", lambda e: e.dma_start(
                    out=OUT[128:256, 0:4 * HQ],
                    in_=w1f_[:, 0:4, :].rearrange("p a b -> p (a b)")),
                    waits=[cw1], incs=d16("dOUT"))
                w2f_ = sb2("w2fdbg", [128, HF // 512, D_MODEL], F32)
                cw2 = P.op("vector", lambda e: e.tensor_copy(w2f_[:, :, :],
                                                             w2f[:, 0:HF // 512, :]),
                           waits=[gw2d], incs=[("sDV", 1)])
                d3_ = P.op("sync", lambda e: e.dma_start(
                    out=OUT[256:384, 0:D_MODEL],
                    in_=w2f_[:, 0:1, :].rearrange("p a b -> p (a b)")),
                    waits=[cw2], incs=d16("dOUT"))
                d4_ = P.op("sync", lambda e: e.dma_start(
                    out=OUT[384:512, 0:64], in_=b1t[:, 0, 0, :]),
                    waits=[gb1d], incs=d16("dOUT"))
                d5_ = P.op("sync", lambda e: e.dma_start(
                    out=OUT[512:640, 0:D_MODEL], in_=b2g[:, 0, 0, :]),
                    waits=[gb2d], incs=d16("dOUT"))
                xgf_ = sb2("xgfdbg", [128, 8, 512], F32)
                cxg = P.op("vector", lambda e: e.tensor_copy(xgf_[:, :, :], xg[:, 0]),
                           waits=[gxx], incs=[("sDV", 1)])
                d6_ = P.op("sync", lambda e: e.dma_start(
                    out=OUT[640:768, 0:1024],
                    in_=xgf_[:, 0:2, :].rearrange("p a b -> p (a b)")),
                    waits=[cxg], incs=d16("dOUT"))
                P.op("sync", lambda e: e.nop(),
                     waits=[{s: cnt.get(s, 0) for s in SEMS if s != "sSP"}],
                     incs=[("sSP", 1)])
                P.emit(nc, sems)
                return
            lw2 = [P.op("gpsimd", lambda e, ib=ib, k=k: e.dma_gather(
                w2f[:, HT4 * k:HT4 * (k + 1), :], W2B[:, :],
                iw2x[:, ib, (NW2 // 16) * k:(NW2 // 16) * (k + 1)],
                num_idxs=NW2, num_idxs_reg=NW2, elem_size=D_MODEL),
                waits=[jw2, wbtok, pcm.get('w2_used')], incs=d16("gW2"))
                for k in range(4)]
            hbtoks = []
            for q in range(4):
                wbuf = q % 2
                lw1 = P.op("gpsimd", lambda e, q=q, ib=ib, wbuf=wbuf: e.dma_gather(
                    w1h[:, wbuf], bass.AP(W1B.tensor, q * HQ, [[HF, NEXP * D_MODEL], [1, HQ]]),
                    iw1x[:, ib, :],
                    num_idxs=D_MODEL, num_idxs_reg=D_MODEL, elem_size=HQ, elem_step=HF),
                    waits=[jw1, wbtok, pcm.get(('w1_used', wbuf))],
                    incs=d16("gW1a" if wbuf == 0 else "gW1b"))
                w1toks = []
                for hb in range(NHBQ):
                    ghb = q * NHBQ + hb
                    def mmf1(e, hb=hb, ghb=ghb, xbuf=xbuf, wbuf=wbuf):
                        last = None
                        for d in range(8):
                            last = e.matmul(PS[ghb % 2][:, 0:512],
                                            w1h[:, wbuf, d, 128 * hb:128 * (hb + 1)],
                                            xg[:, xbuf, d, :],
                                            start=(d == 0), stop=(d == 7))
                        return last
                    mmh = P.op("tensor", mmf1,
                               waits=[gx, lw1, pcm.get(('f1ps', ghb % 2)),
                                      pcm.get('ffn2_done')],
                               incs=[("sPE", 1)])
                    gel = P.op("scalar", lambda e, ghb=ghb, ib=ib:
                               e.activation(hsb[:, ghb, :], PS[ghb % 2][:, 0:512], AF.Gelu,
                                            bias=b1t[:, ib, 0, ghb:ghb + 1]),
                               waits=[mmh, gb1], incs=[("sAC", 1)])
                    pcm[('f1ps', ghb % 2)] = gel
                    w1toks.append(gel)
                    hbtoks.append(gel)
                pcm[('w1_used', wbuf)] = w1toks
            pcm[('b1_used', ib)] = hbtoks
            pcm[('idx_used', ib)] = [gb1, gb2] + lw2
            f2toks = []
            b2toks = []
            for sub in range(4):
                ppair = (PS[2], PS[4]) if sub % 2 == 0 else (PS[5], PS[6])
                def mmf2(e, sub=sub, ppair=ppair):
                    last = None
                    for ht in range(HF // 128):
                        last = e.matmul(ppair[0][:, 0:512],
                                        hsb[:, ht, 128 * sub:128 * (sub + 1)],
                                        w2f[:, ht, 0:512],
                                        start=(ht == 0), stop=(ht == HF // 128 - 1))
                        last = e.matmul(ppair[1][:, 0:512],
                                        hsb[:, ht, 128 * sub:128 * (sub + 1)],
                                        w2f[:, ht, 512:1024],
                                        start=(ht == 0), stop=(ht == HF // 128 - 1))
                    return last
                mm2 = P.op("tensor", mmf2,
                           waits=hbtoks + lw2 + [pcm.get(('f2ps', sub % 2))],
                           incs=[("sPE", 1)])
                obuf = sub % 2
                ev1 = P.op("vector", lambda e, obuf=obuf, ppair=ppair, ib=ib:
                           e.tensor_add(o2r[:, obuf, 0:512],
                                        ppair[0][:, 0:512], b2g[:, ib, 0, 0:512]),
                           waits=[mm2, gb2, pcm.get(('o2r_used', obuf))],
                           incs=[("sDV", 1)])
                ev2 = P.op("vector", lambda e, obuf=obuf, ppair=ppair, ib=ib:
                           e.tensor_add(o2r[:, obuf, 512:1024],
                                        ppair[1][:, 0:512], b2g[:, ib, 0, 512:1024]),
                           waits=[mm2, gb2, pcm.get(('o2r_used', obuf))],
                           incs=[("sDV", 1)])
                pcm[('f2ps', sub % 2)] = [ev1, ev2]
                b2toks += [ev1, ev2]
                so = P.op("sync", lambda e, t=t, sub=sub, obuf=obuf:
                          e.dma_start(out=OUT2[512 * t + 128 * sub:
                                               512 * t + 128 * (sub + 1), :],
                                      in_=o2r[:, obuf, :]),
                          waits=[ev1, ev2], incs=d16("dO2"))
                pcm[('o2r_used', obuf)] = so
                f2toks.append(mm2)
            pcm['ffn2_done'] = f2toks
            pcm['w2_used'] = f2toks
            pcm[('xg_used', xbuf)] = f2toks
            pcm[('b2_used', ib)] = b2toks
        ffn_done = P.op("sync", lambda e: e.nop(),
                        waits=[{s: cnt.get(s, 0) for s in SEMS if s != "sSP"}], incs=[("sSP", 1)])
        if env.get('dbg_p3'):
            nr = min(T, GL)
            for r0 in range(0, nr, 512):
                P.op("sync", lambda e, r0=r0: e.dma_start(
                    out=OUT[r0:r0 + 512, :], in_=OUT2[r0:r0 + 512, :]),
                    waits=[ffn_done], incs=d16("dOUT"))
            P.op("sync", lambda e: e.nop(),
                 waits=[{s: cnt.get(s, 0) for s in SEMS if s != "sSP"}], incs=[("sSP", 1)])
            P.emit(nc, sems)
            return
        P.emit(nc, sems)
        wmA = P.wm
        stF.close()

        sbC = lambda *a: st2.enter_context(nc.sbuf_tensor(*a))
        g0r = sbC("g0r", [128, 1, D_MODEL], F32)
        g1r = sbC("g1r", [128, 1, D_MODEL], F32)
        fin = sbC("fin", [128, 2, D_MODEL], F32)
        bnst2 = sbC("bnst2", [128, 12], F32)
        bnag2 = sbC("bnag2", [128, 2], F32)
        rstd2 = sbC("rstd2", [128, 1], F32)
        nmr2 = sbC("nmr2", [128, 1], F32)
        sqv2 = sbC("sqv2", [128, 1], F32)
        outr = sbC("outr", [128, 2, D_MODEL], F32)
        P = Prog(cnt, wm=wmA, gate=dict(ffn_done))
        for tt in range(NTT):
            fb = tt % 2
            lr_ = P.op("sync", lambda e, tt=tt, fb=fb: e.dma_start(
                out=outr[:, fb, :],
                in_=LN1F[128 * tt:128 * (tt + 1), :]),
                waits=[ffn_done, pcm.get(('outr_used', fb))], incs=d16("dLN%d" % fb))
            gg0 = P.op("gpsimd", lambda e, tt=tt: e.dma_gather(
                g0r[:, :, :], OUT2[:, :], cbidx[:, 8 * tt:8 * (tt + 1)],
                num_idxs=128, num_idxs_reg=128, elem_size=D_MODEL, transpose=False),
                waits=[ffn_done, pcm.get('g0_used')], incs=d16("gG0"))
            gg1 = P.op("gpsimd", lambda e, tt=tt: e.dma_gather(
                g1r[:, :, :], OUT2[:, :], cbidx[:, (T // 16) + 8 * tt:(T // 16) + 8 * (tt + 1)],
                num_idxs=128, num_idxs_reg=128, elem_size=D_MODEL, transpose=False),
                waits=[ffn_done, pcm.get('g1_used')], incs=d16("gG1"))
            c1 = P.op("vector", lambda e, tt=tt, fb=fb: e.scalar_tensor_tensor(
                fin[:, fb, :], g0r[:, 0, :], wgt[:, tt, 0:1], outr[:, fb, :],
                ALU.mult, ALU.add),
                waits=[lr_, gg0, pcm.get(('fin_used', fb))], incs=[("sDV", 1)])
            b7_ = P.op("sync", lambda e: e.nop(), waits=[c1], incs=[("sSP", 1)])
            c2 = P.op("vector", lambda e, tt=tt, fb=fb: e.scalar_tensor_tensor(
                fin[:, fb, :], g1r[:, 0, :], wgt[:, tt, 1:2], fin[:, fb, :],
                ALU.mult, ALU.add),
                waits=[b7_, gg1], incs=[("sDV", 1)])
            pcm['g0_used'] = c1
            pcm['g1_used'] = c2
            b8_ = P.op("sync", lambda e: e.nop(), waits=[c2], incs=[("sSP", 1)])
            bsa = P.op("vector", lambda e, fb=fb: e.bn_stats(bnst2[:, 0:6], fin[:, fb, 0:512]),
                       waits=[b8_], incs=[("sDV", 1)])
            bsb = P.op("vector", lambda e, fb=fb: e.bn_stats(bnst2[:, 6:12], fin[:, fb, 512:1024]),
                       waits=[b8_], incs=[("sDV", 1)])
            b9_ = P.op("sync", lambda e: e.nop(), waits=[bsa, bsb], incs=[("sSP", 1)])
            ag2 = P.op("vector", lambda e: e.bn_aggr(bnag2[:, :], bnst2[:, :]),
                       waits=[b9_], incs=[("sDV", 1)])
            sq2 = P.op("scalar", lambda e: e.activation(sqv2[:, :], bnag2[:, 1:2], AF.Sqrt,
                                                        bias=float(EPS)),
                       waits=[ag2], incs=[("sAC", 1)])
            rc2 = P.op("vector", lambda e: e.reciprocal(rstd2[:, :], sqv2[:, :]),
                       waits=[sq2], incs=[("sDV", 1)])
            ba_ = P.op("sync", lambda e: e.nop(), waits=[rc2], incs=[("sSP", 1)])
            nm2 = P.op("vector", lambda e: e.scalar_tensor_tensor(
                nmr2[:, :], bnag2[:, 0:1], -1.0, rstd2[:, :], ALU.mult, ALU.mult),
                waits=[ba_], incs=[("sDV", 1)])
            bb_ = P.op("sync", lambda e: e.nop(), waits=[nm2], incs=[("sSP", 1)])
            fo = P.op("vector", lambda e, fb=fb: e.tensor_scalar(
                outr[:, fb, :], fin[:, fb, :], rstd2[:, 0:1], nmr2[:, 0:1],
                ALU.mult, ALU.add),
                waits=[bb_], incs=[("sDV", 1)])
            pcm[('fin_used', fb)] = fo
            so_ = P.op("sync", lambda e, tt=tt, fb=fb: e.dma_start(
                out=OUT[128 * tt:128 * (tt + 1), :], in_=outr[:, fb, :]),
                waits=[fo], incs=d16("dOUT"))
            pcm[('outr_used', fb)] = [so_, c1]
        P.op("sync", lambda e: e.nop(),
             waits=[{s: cnt.get(s, 0) for s in SEMS if s != "sSP"}], incs=[("sSP", 1)])
        P.emit(nc, sems)


def b1r_pad(b1, HF):
    out = np.zeros((8 * 128, 64), np.float32)
    for e in range(8):
        out[e * 128:(e + 1) * 128, 0:HF // 128] = b1[e].reshape(HF // 128, 128).T
    return out


def iot16_const():
    a = np.zeros((16, 256), np.float32)
    for i in range(4096):
        a[i % 16, i // 16] = i
    return np.tile(a, (8, 1))


def kernel(**inputs):
    N_CORES = 8
    T = 8192
    HF = 4096
    cfg = dict(T=T, NB=2, CAP=16384, NEXP=8, H_FFN=HF,
               do_moe=True, n_cores=N_CORES)

    f32 = lambda a: np.ascontiguousarray(np.asarray(a), dtype=np.float32)
    x = f32(inputs['L0_attn_x']).reshape(16 * 4096, D_MODEL)
    Wq = f32(inputs['L0_attn_Wq']); Wk = f32(inputs['L0_attn_Wk'])
    Wv = f32(inputs['L0_attn_Wv']); Wo = f32(inputs['L0_attn_Wo'])
    UL = [f32(inputs['L0_attn_UL%d' % l]) for l in range(4)]
    VR = [f32(inputs['L0_attn_VR%d' % l]) for l in range(4)]
    UR = [f32(inputs['L0_attn_UR%d' % l]) for l in range(4)]
    VL = [f32(inputs['L0_attn_VL%d' % l]) for l in range(4)]
    Wr = f32(inputs['L0_ffn_Wr']); br = f32(inputs['L0_ffn_br']).reshape(1, 8)
    W1 = f32(inputs['L0_ffn_W1']); b1 = f32(inputs['L0_ffn_b1'])
    W2 = f32(inputs['L0_ffn_W2']); b2 = f32(inputs['L0_ffn_b2'])

    cv = build_canvases(UL, VR, UR, VL)
    pc = pure_constants(T // 128)
    slotm = np.zeros((128, 4), np.float32)
    for si in range(4):
        slotm[32 * si:32 * (si + 1), si] = 1.0

    common = {
        'wq': Wq, 'wk': Wk, 'wv': Wv, 'wo': Wo,
        'wot': np.ascontiguousarray(Wo.T), 'wr': Wr, 'br': br,
        'phi_vr': np.stack(cv['phi_vr']), 'phi_ul': np.stack(cv['phi_ul']),
        'phi_vl': np.stack(cv['phi_vl']), 'phi_lf': np.stack(cv['phi_lf']),
        'phi_ult_out': np.stack(cv['phi_ult_out']),
        'phi_urt_out': np.stack(cv['phi_urt_out']),
        'phi_ult_y': np.stack(cv['phi_ult_y']),
        'phi_urt_y': np.stack(cv['phi_urt_y']),
        'wmaskc': np.stack(cv['wmask']), 'leafm': cv['leafmask'],
        'ident': pc['ident'], 'triu4': pc['triu4'], 'tri64s': pc['tri64s'],
        'iota8': pc['iota8'], 'onesc': pc['onesc'], 'ones128': pc['ones128'],
        'toksrc': pc['toksrc'], 'cidx0': pc['cidx0'], 'slotm': slotm,
        'epsv': np.full((128, 1), 1e-5, np.float32),
        'w1': W1, 'w2': W2,
        'b1r': b1r_pad(b1, HF),
        'b2': np.ascontiguousarray(b2.reshape(8, 1, D_MODEL)),
        'iot16': iot16_const(),
        'tilv': (512.0 * np.arange(128, dtype=np.float32)).reshape(128, 1),
    }
    in_maps = []
    for c in range(N_CORES):
        prefm = np.zeros((8, 1), np.float32)
        prefm[:c, 0] = 1.0
        m = dict(common)
        m['x'] = x[T * c:T * (c + 1)]
        m['prefm'] = prefm
        in_maps.append(m)

    nc = build_program(cfg)
    res = run_bass_kernel_spmd(nc, in_maps, core_ids=list(range(N_CORES)))
    out = np.concatenate([np.asarray(res.results[c]['out']) for c in range(N_CORES)], axis=0)
    return out.reshape(16, 4096, D_MODEL).astype(np.float32, copy=False)


# version 11
# speedup vs baseline: 1.0087x; 1.0087x over previous
import numpy as np
import concourse.bass as bass
import concourse.bacc as bacc
import concourse.mybir as mybir
from concourse.bass_utils import run_bass_kernel_spmd
from concourse.library_config import mlp as MLP_LIB

F32 = mybir.dt.float32
BF16 = mybir.dt.bfloat16
I16 = mybir.dt.int16
U32 = mybir.dt.uint32
AF = mybir.ActivationFunctionType
ALU = mybir.AluOpType
AX = mybir.AxisListType

D_MODEL = 1024
HD = 64
NUM_LEVELS = 4
BLOCK_SIZES = [4, 8, 16, 32]
RANKS = [4, 8, 8, 8]
NUM_NODES = [8, 4, 2, 1]
E = 8
CH = 128
EPS = 1e-5

def _build_topc():
    comps = []
    for l in range(NUM_LEVELS):
        bs, nn, r = BLOCK_SIZES[l], NUM_NODES[l], RANKS[l]
        for n in range(nn):
            base = n * 2 * bs
            comps.append(dict(l=l, n=n, r=r, bs=bs,
                              L=slice(base, base + bs), R=slice(base + bs, base + 2 * bs)))
    return comps

TOPC = _build_topc()
PACKG = [[0, 1, 2, 3], [4, 5, 6, 7], [8, 9, 10, 11], [12, 13, 14]]
LEAFG = [[0, 1, 2, 3], [4, 5, 6, 7], [8, 9, 10, 11], [12, 13, 14, 15]]
OUTCOL = {}
_ob = 0
for _ci, _c in enumerate(TOPC):
    OUTCOL[_ci] = _ob
    _ob += _c['bs']

BANDIX = {}
_bi = 0
for _fam in range(3):
    for _grp in range(4):
        for _si in range(len((LEAFG if _fam == 0 else PACKG)[_grp])):
            BANDIX[(_fam, _grp, _si)] = _bi
            _bi += 1
N_BANDS = _bi


def build_canvases(UL, VR, UR, VL):
    z = np.zeros
    cv = {}
    cv['phi_vr'] = [z((64, 128), np.float32) for _ in range(4)]
    cv['phi_ul'] = [z((64, 128), np.float32) for _ in range(4)]
    cv['phi_vl'] = [z((64, 128), np.float32) for _ in range(4)]
    cv['phi_ult_out'] = [z((128, 128), np.float32) for _ in range(4)]
    cv['phi_urt_out'] = [z((128, 128), np.float32) for _ in range(4)]
    cv['phi_ult_y'] = [z((128, 64), np.float32) for _ in range(4)]
    cv['phi_urt_y'] = [z((128, 64), np.float32) for _ in range(4)]
    cv['wmask'] = [z((128, 128), np.float32) for _ in range(4)]
    for g, members in enumerate(PACKG):
        for si, ci in enumerate(members):
            c = TOPC[ci]
            l, r, bs = c['l'], c['r'], c['bs']
            slot = 32 * si
            cv['phi_vr'][g][c['R'], slot:slot + r] = VR[l]
            cv['phi_ul'][g][c['L'], slot:slot + r] = UL[l]
            cv['phi_vl'][g][c['L'], slot:slot + r] = VL[l]
            cb = OUTCOL[ci]
            cv['phi_ult_out'][g][slot:slot + r, cb:cb + bs] = UL[l].T
            cv['phi_urt_out'][g][slot:slot + r, cb:cb + bs] = UR[l].T
            cv['phi_ult_y'][g][slot:slot + r, c['L']] = UL[l].T
            cv['phi_urt_y'][g][slot:slot + r, c['R']] = UR[l].T
            cv['wmask'][g][slot:slot + r, slot:slot + r] = 1.0
    cv['phi_lf'] = [z((64, 128), np.float32) for _ in range(4)]
    for g, members in enumerate(LEAFG):
        for si, n in enumerate(members):
            slot = 32 * si
            cv['phi_lf'][g][4 * n:4 * n + 4, slot:slot + 4] = np.eye(4, dtype=np.float32)
    m = z((64, 64), np.float32)
    for n in range(16):
        m[4 * n:4 * n + 4, 4 * n:4 * n + 4] = 1.0
    cv['leafmask'] = m
    return cv


def pure_constants(NTT):
    c = {}
    c['ident'] = np.eye(128, dtype=np.float32)
    triu = (np.arange(128)[:, None] <= np.arange(128)[None, :]).astype(np.float32)
    c['triu4'] = np.tile(triu, (1, 4))
    c['tri64s'] = (np.arange(64)[:, None] < np.arange(64)[None, :]).astype(np.float32)
    c['iota8'] = np.broadcast_to(np.arange(8, dtype=np.float32), (128, 8)).copy()
    c['onesc'] = np.ones((1, 128), np.float32)
    c['ones128'] = np.ones((128, 1), np.float32)
    T = 128 * NTT
    ts = np.zeros((128, NTT), np.float32)
    for t in range(T):
        ts[t % 128, t // 128] = t + 1
    c['toksrc'] = ts
    c['cidx0'] = ((np.arange(128) % 16) * (T // 16) + np.arange(128) // 16).astype(np.int32).reshape(128, 1)
    return c


class Prog:
    ENGS = ("tensor", "vector", "scalar", "sync", "gpsimd")

    def __init__(self, cnt, wm=None, gate=None):
        self.ops = {e: [] for e in self.ENGS}
        self.cnt = cnt
        self.wm = wm if wm is not None else {}
        self.gate = gate or {}
        self._gated = set()

    def op(self, eng, fn, waits=(), incs=()):
        toks = {}
        winc = []
        for s, k in incs:
            self.cnt[s] = self.cnt.get(s, 0) + k
            toks[s] = self.cnt[s]
            winc.append((s, k))
        wl = {}
        def add(t):
            if t is None:
                return
            if isinstance(t, dict):
                for s, v in t.items():
                    wl[s] = max(wl.get(s, 0), v)
            elif isinstance(t, (tuple, list)):
                for u in t:
                    add(u)
            else:
                raise TypeError(t)
        add(list(waits))
        if self.gate and eng not in self._gated:
            add(self.gate)
            self._gated.add(eng)
        for s in list(wl):
            if s[0] in ('d', 'g'):
                wl[s] = self.cnt.get(s, 0)
                self.wm[s] = max(self.wm.get(s, 0), wl[s])
        for s, k in incs:
            if s[0] in ('d', 'g'):
                w = self.wm.get(s, 0) - k and self.wm.get(s, 0)
                w = self.wm.get(s, 0)
                prev = self.cnt[s] - k
                if 0 < w <= prev:
                    wl[s] = max(wl.get(s, 0), w)
        self.ops[eng].append((fn, list(wl.items()), winc))
        return toks

    def emit(self, nc, sems):
        with nc.Block() as block:
            def mk(eng_name):
                def inner(eng):
                    for fn, waits, incs in self.ops[eng_name]:
                        for s, v in waits:
                            eng.wait_ge(sems[s], v)
                        inst = fn(eng)
                        for s, k in incs:
                            if inst is None:
                                inst = eng.nop()
                            inst.then_inc(sems[s], k)
                return inner
            for name, deco in (("tensor", block.tensor), ("vector", block.vector),
                               ("scalar", block.scalar), ("sync", block.sync),
                               ("gpsimd", block.gpsimd)):
                if self.ops[name]:
                    deco(mk(name))


def build_program(cfg):
    T = cfg['T']; NB = cfg['NB']; TB = T // NB
    NCH = TB // CH; NTT = T // 128
    SPAN = 512
    CAP = cfg['CAP']
    NEXP = cfg['NEXP']; HF = cfg['H_FFN']
    do_moe = cfg.get('do_moe', True)
    n_cores = cfg.get('n_cores', 8)
    debug_dump = cfg.get('debug_dump', None)
    NTILES = (2 * T + NEXP * 511 + 511) // 512
    GL = NTILES * 512
    GS16 = GL // 16

    NHB = HF // 128
    CPS = SPAN // CH

    nc = bacc.Bacc("TRN2", target_bir_lowering=False, debug=False)
    dtr = nc.dram_tensor

    X = dtr("x", [T, D_MODEL], F32, kind="ExternalInput").ap()
    WQ = dtr("wq", [D_MODEL, HD], F32, kind="ExternalInput").ap()
    WK = dtr("wk", [D_MODEL, HD], F32, kind="ExternalInput").ap()
    WV = dtr("wv", [D_MODEL, HD], F32, kind="ExternalInput").ap()
    WO = dtr("wo", [HD, D_MODEL], F32, kind="ExternalInput").ap()
    WOT = dtr("wot", [D_MODEL, HD], F32, kind="ExternalInput").ap()
    WR = dtr("wr", [D_MODEL, E], F32, kind="ExternalInput").ap()
    BR = dtr("br", [1, E], F32, kind="ExternalInput").ap()
    PHI_VR = dtr("phi_vr", [4, 64, 128], F32, kind="ExternalInput").ap()
    PHI_UL = dtr("phi_ul", [4, 64, 128], F32, kind="ExternalInput").ap()
    PHI_VL = dtr("phi_vl", [4, 64, 128], F32, kind="ExternalInput").ap()
    PHI_LF = dtr("phi_lf", [4, 64, 128], F32, kind="ExternalInput").ap()
    PHI_ULT_OUT = dtr("phi_ult_out", [4, 128, 128], F32, kind="ExternalInput").ap()
    PHI_URT_OUT = dtr("phi_urt_out", [4, 128, 128], F32, kind="ExternalInput").ap()
    PHI_ULT_Y = dtr("phi_ult_y", [4, 128, 64], F32, kind="ExternalInput").ap()
    PHI_URT_Y = dtr("phi_urt_y", [4, 128, 64], F32, kind="ExternalInput").ap()
    WMASKC = dtr("wmaskc", [4, 128, 128], F32, kind="ExternalInput").ap()
    LEAFM = dtr("leafm", [64, 64], F32, kind="ExternalInput").ap()
    IDENT = dtr("ident", [128, 128], F32, kind="ExternalInput").ap()
    TRIU4 = dtr("triu4", [128, 512], F32, kind="ExternalInput").ap()
    TRI64S = dtr("tri64s", [64, 64], F32, kind="ExternalInput").ap()
    IOTA8 = dtr("iota8", [128, E], F32, kind="ExternalInput").ap()
    ONESC = dtr("onesc", [1, 128], F32, kind="ExternalInput").ap()
    ONES128 = dtr("ones128", [128, 1], F32, kind="ExternalInput").ap()
    EPSV = dtr("epsv", [128, 1], F32, kind="ExternalInput").ap()
    PREFM = dtr("prefm", [8, 1], F32, kind="ExternalInput").ap()
    TOKSRC = dtr("toksrc", [128, NTT], F32, kind="ExternalInput").ap()
    SLOTM = dtr("slotm", [128, 4], F32, kind="ExternalInput").ap()
    IOT16 = dtr("iot16", [128, 256], F32, kind="ExternalInput").ap()
    TILV = dtr("tilv", [128, 1], F32, kind="ExternalInput").ap()
    CIDX0 = dtr("cidx0", [128, 1], mybir.dt.int32, kind="ExternalInput").ap()
    W1 = dtr("w1", [NEXP, D_MODEL, HF], F32, kind="ExternalInput").ap()
    W2 = dtr("w2", [NEXP, HF, D_MODEL], F32, kind="ExternalInput").ap()
    B1R = dtr("b1r", [NEXP * 128, 64], F32, kind="ExternalInput").ap()
    B2 = dtr("b2", [NEXP, 1, D_MODEL], F32, kind="ExternalInput").ap()
    W1B = dtr("w1b", [NEXP * D_MODEL, HF], BF16).ap()
    W2B = dtr("w2b", [NEXP * HF, D_MODEL], BF16).ap()

    OUT = dtr("out", [T, D_MODEL], F32, kind="ExternalOutput").ap()

    XBF = dtr("xbf", [T, D_MODEL], BF16).ap()
    LN1F = dtr("ln1f", [T, D_MODEL], F32).ap()
    LN1B = dtr("ln1b", [T, D_MODEL], BF16).ap()
    OUT2 = dtr("out2", [GL, D_MODEL], F32).ap()
    LISTS_F = dtr("lists_f", [GL, 1], F32).ap()
    SCR = dtr("scr", [4096], F32).ap()
    SCR2 = dtr("scr2", [2 * T, 1], F32).ap()
    CC_IN = dtr("cc_in", [1, E], F32)
    CC_OUT = dtr("cc_out", [n_cores, E], F32)

    SEMS = ["sPE", "sDV", "sAC", "sSP", "sCC",
            "dC", "dX0", "dX1", "dS0", "dS1", "dT0", "dT1", "dR", "dL1", "dL2",
            "dP", "dQ", "dO2", "dLN0", "dLN1", "dOUT", "dDBG",
            "gW", "gSC", "gW1a", "gW1b", "gW2", "gXG0", "gXG1", "gG0", "gG1",
            "gLX0", "gLX1", "dZ", "gWB", "gB1", "gB2", "gW2b"]
    cnt = {s: 0 for s in SEMS}

    import contextlib
    stack = contextlib.ExitStack()
    sems = {s: stack.enter_context(nc.semaphore(s)) for s in SEMS}
    sb = lambda *a: stack.enter_context(nc.sbuf_tensor(*a))
    psm = lambda *a: stack.enter_context(nc.psum_tensor(*a))

    with stack:
        c_identf = sb("c_identf", [128, 128], F32)
        c_triub = sb("c_triub", [128, 128], BF16)
        c_tri64b = sb("c_tri64b", [64, 64], BF16)
        c_iota8 = sb("c_iota8", [128, E], F32)
        c_onescf = sb("c_onescf", [1, 128], F32)
        c_ones128f = sb("c_ones128f", [128, 1], F32)
        c_ones128b = sb("c_ones128b", [128, 1], BF16)
        c_prefm = sb("c_prefm", [8, 1], F32)
        c_eps = sb("c_eps", [128, 1], F32)
        logits = sb("logits", [128, NTT, E], F32)
        wgt = sb("wgt", [128, NTT, 2], F32)

        PS = [psm(f"PS{i}", [128, 512], F32) for i in range(3)]
        PS3B = psm("PS3B", [128, 1024], BF16)
        PS = PS + [None] + [psm(f"PS{i}", [128, 512], F32) for i in range(4, 8)]

        with contextlib.ExitStack() as st1:
            sb1 = lambda *a: st1.enter_context(nc.sbuf_tensor(*a))
            c_identb = sb1("c_identb", [128, 128], BF16)
            c_slotm = sb1("c_slotm", [128, 4], F32)
            featQx = sb1("featQx", [128, N_BANDS, 128], BF16)
            c_triu4 = sb1("c_triu4", [128, 512], F32)
            c_tri64f = sb1("c_tri64f", [64, 64], F32)
            c_leafm = sb1("c_leafm", [64, 64], F32)
            srow = sb1("srow", [1, 64], F32)
            phiK = sb1("phiK", [64, 12, 128], BF16)
            phiQ = sb1("phiQ", [64, 12, 128], BF16)
            phiO = sb1("phiO", [128, 8, 128], BF16)
            phiY = sb1("phiY", [128, 8, 64], BF16)
            c_wmask = sb1("c_wmask", [128, 4, 128], F32)
            w_qkvr = sb1("w_qkvr", [128, 8, 3 * HD + E], BF16)
            w_o = sb1("w_o", [64, D_MODEL], BF16)
            w_ot = sb1("w_ot", [128, 8, HD], BF16)
            wowr = sb1("wowr", [64, E], BF16)
            cswr_bc = sb1("cswr_bc", [128, E], F32)
            br_bc = sb1("br_bc", [128, E], F32)
            xbrow = sb1("xbrow", [128, 2, 4, D_MODEL], BF16)
            xres = sb1("xres", [128, 2, D_MODEL], F32)
            xt = sb1("xt", [128, 2, 8, SPAN], BF16)
            qb = sb1("qb", [64, T], BF16)
            kb = sb1("kb", [64, T], BF16)
            vb = sb1("vb", [64, T], BF16)
            xwrt = sb1("xwrt", [8, T], BF16)
            t_elu = sb1("t_elu", [64, 3, SPAN], F32)
            featK = sb1("featK", [128, 12, SPAN], BF16)
            featQ = sb1("featQ", [128, 12, SPAN], BF16)
            sig = sb1("sig", [128, 48, 128], BF16)
            wts = sb1("wts", [128, 2, 128], BF16)
            kvqt = sb1("kvqt", [128, 3, HD], BF16)
            dsc = sb1("dsc", [128, HD], F32)
            ytb = sb1("ytb", [128, HD], BF16)
            ytT = sb1("ytT", [64, 128], BF16)
            zrow = sb1("zrow", [1, HD], F32)
            den = sb1("den", [128, 1], F32)
            den2 = sb1("den2", [128, 1], F32)
            rden = sb1("rden", [128, 1], F32)
            cs_f = sb1("cs_f", [64, 64], F32)
            ct_f = sb1("ct_f", [64, 64], F32)
            csb = sb1("csb", [64, 64], BF16)
            ctb = sb1("ctb", [64, 64], BF16)
            ctm = sb1("ctm", [64, 64], BF16)
            t2b = sb1("t2b", [64, 4, 128], BF16)
            t1b = sb1("t1b", [64, 4, 128], BF16)
            wgb = sb1("wgb", [128, 4, 128], BF16)
            wtgb = sb1("wtgb", [128, 4, 128], BF16)
            rtop = sb1("rtop", [128, 4, 64], BF16)
            rbot = sb1("rbot", [128, 4, 64], BF16)
            r1 = sb1("r1", [128, D_MODEL], F32)
            ln1t = sb1("ln1t", [128, D_MODEL], F32)
            ln1tb = sb1("ln1tb", [128, D_MODEL], BF16)
            bnst = sb1("bnst", [128, 12], F32)
            bnag = sb1("bnag", [128, 2], F32)
            rstd = sb1("rstd", [128, 1], F32)
            nmr = sb1("nmr", [128, 1], F32)
            sqv = sb1("sqv", [128, 1], F32)
            rawt8 = sb1("rawt8", [8, 128], F32)
            lraw = sb1("lraw", [128, E], F32)

            P = Prog(cnt)
            d16 = lambda s="dC": [(s, 16)]

            cl = []
            for dst, src in ((c_identf[:], IDENT[:, :]), (c_triu4[:], TRIU4[:, :]),
                             (c_iota8[:], IOTA8[:, :]), (c_onescf[:], ONESC[:, :]),
                             (c_ones128f[:], ONES128[:, :]), (c_prefm[:], PREFM[:, :]),
                             (c_leafm[:], LEAFM[:, :]), (c_wmask[:], WMASKC[:, :, :]),
                             (c_tri64f[:], TRI64S[:, :]), (c_eps[:, :], EPSV[:, :]),
                             (c_slotm[:, :], SLOTM[:, :])):
                cl.append(P.op("sync", lambda e, d=dst, s=src: e.dma_start(out=d, in_=s),
                               incs=d16("dC")))
            cc = []
            for dst, src in ((c_identb[:], c_identf[:]), (c_triub[:], c_triu4[:, 0:128]),
                             (c_ones128b[:], c_ones128f[:]), (c_tri64b[:], c_tri64f[:])):
                cc.append(P.op("vector", lambda e, d=dst, s=src: e.tensor_copy(d, s),
                               waits=cl, incs=[("sDV", 1)]))

            nc.const_aps.register(F32, float(EPS), c_eps[:, 0:1])
            wlo = []
            def gload(dst, src):
                wlo.append(P.op("gpsimd", lambda e, d=dst, s=src: e.dma_start(out=d, in_=s),
                                incs=d16("gW")))
            gload(w_qkvr[:, :, 0:HD], WQ.rearrange("(k p) h -> p k h", p=128))
            gload(w_qkvr[:, :, HD:2 * HD], WK.rearrange("(k p) h -> p k h", p=128))
            gload(w_qkvr[:, :, 2 * HD:3 * HD], WV.rearrange("(k p) h -> p k h", p=128))
            gload(w_qkvr[:, :, 3 * HD:3 * HD + E], WR.rearrange("(k p) h -> p k h", p=128))
            gload(w_o[:, :], WO[:, :])
            gload(w_ot[:, :, :], WOT.rearrange("(k p) h -> p k h", p=128))
            for g in range(4):
                gload(phiK[:, 0 + g, :], PHI_LF[g])
                gload(phiK[:, 4 + g, :], PHI_VR[g])
                gload(phiK[:, 8 + g, :], PHI_UL[g])
                gload(phiQ[:, 0 + g, :], PHI_LF[g])
                gload(phiQ[:, 4 + g, :], PHI_VR[g])
                gload(phiQ[:, 8 + g, :], PHI_VL[g])
                gload(phiO[:, 0 + g, :], PHI_ULT_OUT[g])
                gload(phiO[:, 4 + g, :], PHI_URT_OUT[g])
                gload(phiY[:, 0 + g, :], PHI_ULT_Y[g])
                gload(phiY[:, 4 + g, :], PHI_URT_Y[g])
            wtok = P.op("sync", lambda e: e.nop(), waits=wlo, incs=[("sSP", 1)])
            ctok = P.op("sync", lambda e: e.nop(), waits=cc, incs=[("sSP", 1)])

            def mm_wowr(e):
                i = None
                for k in range(8):
                    i = e.matmul(PS[6][0:64, 0:E], w_ot[:, k, :],
                                 w_qkvr[:, k, 3 * HD:3 * HD + E], start=(k == 0), stop=(k == 7))
                return i
            m1 = P.op("tensor", mm_wowr, waits=[wtok], incs=[("sPE", 1)])
            def mm_cswr(e):
                i = None
                for k in range(8):
                    i = e.matmul(PS[6][0:1, 64:64 + E], c_ones128b[:, :],
                                 w_qkvr[:, k, 3 * HD:3 * HD + E], start=(k == 0), stop=(k == 7))
                return i
            m2 = P.op("tensor", mm_cswr, waits=[wtok, ctok], incs=[("sPE", 1)])
            e1 = P.op("vector", lambda e: e.tensor_copy(wowr[:], PS[6][0:64, 0:E]),
                      waits=[m1], incs=[("sDV", 1)])
            e2 = P.op("vector", lambda e: e.tensor_copy(srow[0:1, 0:E], PS[6][0:1, 64:64 + E]),
                      waits=[m2], incs=[("sDV", 1)])
            brl = P.op("sync", lambda e: e.dma_start(out=srow[0:1, 8:8 + E], in_=BR[:, :]),
                       incs=d16("dC"))
            def mm_bc(e):
                e.matmul(PS[6][:, 128:128 + E], c_onescf[:, :], srow[0:1, 0:E],
                         start=True, stop=True)
                return e.matmul(PS[6][:, 192:192 + E], c_onescf[:, :], srow[0:1, 8:8 + E],
                                start=True, stop=True)
            m3 = P.op("tensor", mm_bc, waits=[e2, brl, cl], incs=[("sPE", 1)])
            e3 = P.op("vector", lambda e: e.tensor_copy(cswr_bc[:], PS[6][:, 128:128 + E]),
                      waits=[m3], incs=[("sDV", 1)])
            e4 = P.op("vector", lambda e: e.tensor_copy(br_bc[:], PS[6][:, 192:192 + E]),
                      waits=[m3], incs=[("sDV", 1)])
            prep = P.op("sync", lambda e: e.nop(), waits=[e1, e3, e4], incs=[("sSP", 1)])

            prev = {}
            for g in range(T // SPAN):
                buf = g % 2
                r0 = g * SPAN
                lx = P.op("gpsimd", lambda e, r0=r0, buf=buf: e.dma_start(
                    out=xbrow[:, buf], in_=X[r0:r0 + SPAN, :].rearrange("(j p) d -> p j d", p=128)),
                    waits=[prev.get(('st', buf))], incs=d16("gLX%d" % buf))
                st = P.op("sync", lambda e, r0=r0, buf=buf: e.dma_start(
                    out=XBF[r0:r0 + SPAN, :].rearrange("(j p) d -> p j d", p=128),
                    in_=xbrow[:, buf]), waits=[lx], incs=d16("dS%d" % buf))
                prev[('st', buf)] = st
                xts = []
                for d in range(8):
                    xts.append(P.op("sync", lambda e, r0=r0, buf=buf, d=d: e.dma_start(
                        out=xt[:, buf, d, :], in_=XBF[r0:r0 + SPAN, 128 * d:128 * (d + 1)],
                        transpose=True),
                        waits=[st, prev.get(('mmq', buf))], incs=d16("dT%d" % buf)))
                def mm_qkv(e, buf=buf):
                    last = None
                    for p in range(3):
                        for k in range(8):
                            last = e.matmul(PS[p][0:64, 0:SPAN],
                                            w_qkvr[:, k, HD * p:HD * (p + 1)], xt[:, buf, k, :],
                                            start=(k == 0), stop=(k == 7))
                    for k in range(8):
                        last = e.matmul(PS[4][0:8, 0:SPAN], w_qkvr[:, k, 3 * HD:3 * HD + E],
                                        xt[:, buf, k, :], start=(k == 0), stop=(k == 7))
                    return last
                mmq = P.op("tensor", mm_qkv,
                           waits=xts + [wtok, prev.get('evq'), prep if g == 0 else None],
                           incs=[("sPE", 1)])
                prev[('mmq', buf)] = mmq
                sl = slice(g * SPAN, (g + 1) * SPAN)
                o0 = P.op("vector", lambda e: e.tensor_scalar_min(t_elu[:, 0], PS[0][0:64, 0:SPAN], 0.0),
                          waits=[mmq], incs=[("sDV", 1)])
                o1 = P.op("scalar", lambda e: e.activation(t_elu[:, 1], t_elu[:, 0], AF.Exp),
                          waits=[o0], incs=[("sAC", 1)])
                o2 = P.op("scalar", lambda e: e.activation(t_elu[:, 2], PS[0][0:64, 0:SPAN], AF.Relu),
                          waits=[mmq], incs=[("sAC", 1)])
                o3 = P.op("vector", lambda e, sl=sl: e.tensor_add(qb[:, sl], t_elu[:, 1], t_elu[:, 2]),
                          waits=[o1, o2], incs=[("sDV", 1)])
                o4 = P.op("vector", lambda e: e.tensor_scalar_min(t_elu[:, 0], PS[1][0:64, 0:SPAN], 0.0),
                          waits=[mmq, o3], incs=[("sDV", 1)])
                o5 = P.op("scalar", lambda e: e.activation(t_elu[:, 1], t_elu[:, 0], AF.Exp),
                          waits=[o4, o3], incs=[("sAC", 1)])
                o6 = P.op("scalar", lambda e: e.activation(t_elu[:, 2], PS[1][0:64, 0:SPAN], AF.Relu),
                          waits=[mmq, o4, o3], incs=[("sAC", 1)])
                o7 = P.op("vector", lambda e, sl=sl: e.tensor_add(kb[:, sl], t_elu[:, 1], t_elu[:, 2]),
                          waits=[o5, o6], incs=[("sDV", 1)])
                o8 = P.op("vector", lambda e, sl=sl: e.tensor_copy(vb[:, sl], PS[2][0:64, 0:SPAN]),
                          waits=[mmq], incs=[("sDV", 1)])
                o9 = P.op("vector", lambda e, sl=sl: e.tensor_copy(xwrt[:, sl], PS[4][0:8, 0:SPAN]),
                          waits=[mmq], incs=[("sDV", 1)])
                evq = P.op("sync", lambda e: e.nop(), waits=[o3, o7, o8, o9], incs=[("sSP", 1)])
                prev['evq'] = evq
            p0done = prev['evq']

            if do_moe:
                for ex in range(NEXP):
                    P.op("gpsimd", lambda e, ex=ex: e.dma_start(
                        out=W1B[ex * D_MODEL:(ex + 1) * D_MODEL, :], in_=W1[ex]),
                        incs=d16("gWB"))
                    P.op("gpsimd", lambda e, ex=ex: e.dma_start(
                        out=W2B[ex * HF:(ex + 1) * HF, :], in_=W2[ex]),
                        incs=d16("gWB"))

            NW = 12
            batch_carry = None
            pc = {'xrow_used': p0done, 'oproj_used': p0done}
            for b in range(NB):
                ms1 = P.op("vector", lambda e: e.memset(cs_f[:, :], 0.0),
                           waits=[batch_carry], incs=[("sDV", 1)])
                ms2 = P.op("vector", lambda e: e.memset(ct_f[:, :], 0.0),
                           waits=[batch_carry], incs=[("sDV", 1)])
                pc['cadd_tok'] = P.op("sync", lambda e: e.nop(), waits=[ms1, ms2],
                                      incs=[("sSP", 1)])
                for ci in range(NCH):
                    if b * NCH + ci >= cfg.get('max_chunks', 1 << 30):
                        break
                    t0 = b * TB + ci * CH
                    csl = slice(t0, t0 + CH)
                    first = (ci == 0)
                    last_chunk = (ci == NCH - 1)
                    spi = ci % CPS
                    if spi == 0:
                        ssl = slice(t0, t0 + SPAN)
                        fev = []
                        srcs = [vb, vb, kb, qb, qb, qb]
                        phz = [phiK, phiK, phiK, phiQ, phiQ, phiQ]
                        dsts = [featK, featK, featK, featQ, featQ, featQ]
                        for fi in range(6):
                            for g2 in range(4):
                                jj = 4 * fi + g2
                                bank = PS[4] if jj % 2 == 0 else PS[5]
                                mm = P.op("tensor",
                                          lambda e, fi=fi, g2=g2, bank=bank, ssl=ssl:
                                          e.matmul(bank[:, 0:SPAN],
                                                   phz[fi][:, 4 * (fi % 3) + g2, :],
                                                   srcs[fi][:, ssl], start=True, stop=True),
                                          waits=[p0done, ctok, wtok,
                                                 pc.get(('fev', jj % 2)),
                                                 pc.get('feats_used'),
                                                 pc.get('inter_ps_free'),
                                                 pc.get('wts_psum_free'),
                                                 pc.get('lraw_psum_free')],
                                          incs=[("sPE", 1)])
                                if jj % 2 == 0:
                                    ev = P.op("vector",
                                              lambda e, fi=fi, g2=g2, bank=bank:
                                              e.tensor_copy(dsts[fi][:, 4 * (fi % 3) + g2, :],
                                                            bank[:, 0:SPAN]),
                                              waits=[mm], incs=[("sDV", 1)])
                                else:
                                    ev = P.op("scalar",
                                              lambda e, fi=fi, g2=g2, bank=bank:
                                              e.activation(dsts[fi][:, 4 * (fi % 3) + g2, :],
                                                           bank[:, 0:SPAN], AF.Copy),
                                              waits=[mm], incs=[("sAC", 1)])
                                pc[('fev', jj % 2)] = ev
                                fev.append(ev)
                        pc['feats'] = P.op("sync", lambda e: e.nop(), waits=fev, incs=[("sSP", 1)])
                    feats_tok = pc['feats']

                    if cfg.get('p1_stage', 99) < 2:
                        continue
                    trs = []
                    trm = []
                    for j, srcb in enumerate((kb, vb, qb)):
                        mm = P.op("tensor", lambda e, srcb=srcb, j=j, csl=csl:
                                  e.transpose(PS3B[:, 64 * j:64 * (j + 1)], srcb[:, csl],
                                              c_identb[0:64, 0:64]),
                                  waits=[p0done, ctok, pc.get('kvqt_used'), pc.get('ps3_free')],
                                  incs=[("sPE", 1)])
                        trm.append(mm)
                        ev = P.op("scalar", lambda e, j=j: e.activation(
                            kvqt[:, j, :], PS3B[:, 64 * j:64 * (j + 1)], AF.Copy),
                            waits=[mm], incs=[("sAC", 1)])
                        trs.append(ev)
                    tr_tok = P.op("sync", lambda e: e.nop(), waits=trs, incs=[("sSP", 1)])

                    if cfg.get('p1_stage', 99) < 3:
                        continue
                    def mm_cumk(e, first=first):
                        if first:
                            e.matmul(PS[2][0:1, 256:320], c_ones128b[:, :], kvqt[:, 0, :],
                                     start=True, stop=True)
                            return e.matmul(PS[2][:, 64:128], c_triub[:, :], kvqt[:, 0, :],
                                            start=True, stop=True)
                        e.matmul(PS[2][0:1, 256:320], c_ones128b[:, :], kvqt[:, 0, :],
                                 start=True, stop=False)
                        e.matmul(PS[2][0:1, 256:320], c_onescf[0:1, 0:1], zrow[:, :],
                                 start=False, stop=True)
                        e.matmul(PS[2][:, 64:128], c_onescf[:, :], zrow[:, :],
                                 start=True, stop=False)
                        return e.matmul(PS[2][:, 64:128], c_triub[:, :], kvqt[:, 0, :],
                                        start=False, stop=True)
                    mmk = P.op("tensor", mm_cumk,
                               waits=[tr_tok, pc.get('den_done'), pc.get('zrow_done')],
                               incs=[("sPE", 1)])
                    zc = P.op("vector", lambda e: e.tensor_copy(zrow[:, :],
                                                                PS[2][0:1, 256:320]),
                              waits=[mmk], incs=[("sDV", 1)])
                    pc['zrow_done'] = zc

                    if cfg.get('p1_stage', 99) < 4:
                        continue
                    wave_ev = {}
                    wave_mm = {}
                    for w in range(NW):
                        grp = w % 4
                        fam = w // 4
                        bank = PS[w % 2]
                        Kf = featK[:, 4 * fam + grp, 128 * spi:128 * (spi + 1)]
                        Qf = featQ[:, 4 * fam + grp, 128 * spi:128 * (spi + 1)]
                        members = (LEAFG if fam == 0 else PACKG)[grp]
                        mcs = []
                        for si in range(len(members)):
                            gm = BANDIX[(fam, grp, si)]
                            if gm % 2 == 0:
                                mc = P.op("vector", lambda e, gm=gm, si=si, Qf=Qf:
                                          e.tensor_scalar(featQx[:, gm, :], Qf,
                                                          c_slotm[:, si:si + 1], None,
                                                          ALU.mult),
                                          waits=[feats_tok, ctok, pc.get('featqx_used')],
                                          incs=[("sDV", 1)])
                            else:
                                mc = P.op("scalar", lambda e, gm=gm, si=si, Qf=Qf:
                                          e.activation(featQx[:, gm, :], Qf, AF.Copy,
                                                       scale=c_slotm[:, si:si + 1]),
                                          waits=[feats_tok, ctok, pc.get('featqx_used')],
                                          incs=[("sAC", 1)])
                            mcs.append(mc)
                        def mm_sc(e, Kf=Kf, members=members, bank=bank, fam=fam, grp=grp):
                            last = None
                            for si in range(len(members)):
                                gm = BANDIX[(fam, grp, si)]
                                last = e.matmul(bank[:, 128 * si:128 * (si + 1)],
                                                Kf, featQx[:, gm, :],
                                                start=True, stop=True)
                            return last
                        scw = [feats_tok, wave_ev.get(w - 2), pc.get(('scwave', w))] + mcs
                        if w < 2:
                            scw.append(pc.get('oproj_used'))
                        mm = P.op("tensor", mm_sc, waits=scw, incs=[("sPE", 1)])
                        wave_mm[w] = mm
                        ev = P.op("vector", lambda e, w=w, bank=bank: e.tensor_tensor(
                            sig[:, 4 * w:4 * w + 4, :].rearrange("p a b -> p (a b)"),
                            bank[:, 0:512], c_triu4[:, :], ALU.mult),
                            waits=[mm, pc.get('sig_used')], incs=[("sDV", 1)])
                        wave_ev[w] = ev
                        pc[('scwave', w)] = ev
                    pc['featqx_used'] = list(wave_mm.values())
                    sc_tok = P.op("sync", lambda e: e.nop(), waits=list(wave_ev.values()),
                                  incs=[("sSP", 1)])

                    if cfg.get('p1_stage', 99) < 5:
                        continue
                    def mm_wt(e, spi=spi):
                        last = None
                        for g2 in range(4):
                            last = e.matmul(PS[4][:, 0:128],
                                            featK[:, 8 + g2, 128 * spi:128 * (spi + 1)],
                                            phiO[:, 0 + g2, :], start=(g2 == 0), stop=(g2 == 3))
                        for g2 in range(4):
                            last = e.matmul(PS[4][:, 128:256],
                                            featK[:, 4 + g2, 128 * spi:128 * (spi + 1)],
                                            phiO[:, 4 + g2, :], start=(g2 == 0), stop=(g2 == 3))
                        return last
                    mmw = P.op("tensor", mm_wt, waits=[feats_tok, pc.get('wts_used'), trm, pc.get('inter_ps_free'), pc.get(('fev', 0)), pc.get(('fev', 1))],
                               incs=[("sPE", 1)])
                    evw = P.op("scalar", lambda e: e.activation(
                        wts[:, :, :].rearrange("p a b -> p (a b)"), PS[4][:, 0:256], AF.Copy),
                        waits=[mmw], incs=[("sAC", 1)])
                    pc['wts_psum_free'] = evw

                    if cfg.get('p1_stage', 99) < 6:
                        continue
                    if not first:
                        evc = P.op("vector", lambda e: e.tensor_copy(csb[:], cs_f[:, :]),
                                   waits=[pc.get('cadd_tok'), pc.get('inter_used')], incs=[("sDV", 1)])
                        evct = P.op("scalar", lambda e: e.activation(ctb[:], ct_f[:, :], AF.Copy),
                                    waits=[pc.get('cadd_tok'), pc.get('inter_used')], incs=[("sAC", 1)])
                        evcm = P.op("vector", lambda e: e.tensor_tensor(ctm[:], ct_f[:, :],
                                                                        c_leafm[:], ALU.mult),
                                    waits=[pc.get('cadd_tok'), pc.get('inter_used')], incs=[("sDV", 1)])
                        def mm_t2(e):
                            last = None
                            for g2 in range(4):
                                last = e.matmul(PS[4][0:64, 128 * g2:128 * (g2 + 1)], csb[:, :],
                                                phiK[:, 8 + g2, :], start=True, stop=True)
                            return last
                        def mm_t1(e):
                            last = None
                            for g2 in range(4):
                                last = e.matmul(PS[5][0:64, 128 * g2:128 * (g2 + 1)], ctb[:, :],
                                                phiK[:, 4 + g2, :], start=True, stop=True)
                            return last
                        mt2 = P.op("tensor", mm_t2, waits=[evc, feats_tok, evw],
                                   incs=[("sPE", 1)])
                        mt1 = P.op("tensor", mm_t1, waits=[evct, feats_tok, evw, pc.get('lraw_psum_free')],
                                   incs=[("sPE", 1)])
                        et2 = P.op("vector", lambda e: e.tensor_copy(
                            t2b[:, :, :].rearrange("p a b -> p (a b)"), PS[4][0:64, 0:512]),
                            waits=[mt2], incs=[("sDV", 1)])
                        et1 = P.op("scalar", lambda e: e.activation(
                            t1b[:, :, :].rearrange("p a b -> p (a b)"), PS[5][0:64, 0:512], AF.Copy),
                            waits=[mt1], incs=[("sAC", 1)])
                        def mm_wg(e):
                            last = None
                            for g2 in range(4):
                                last = e.matmul(PS[4][:, 128 * g2:128 * (g2 + 1)], t2b[:, g2, :],
                                                phiK[:, 4 + g2, :], start=True, stop=True)
                            return last
                        def mm_wtg(e):
                            last = None
                            for g2 in range(4):
                                last = e.matmul(PS[5][:, 128 * g2:128 * (g2 + 1)], t1b[:, g2, :],
                                                phiK[:, 8 + g2, :], start=True, stop=True)
                            return last
                        mwg = P.op("tensor", mm_wg, waits=[et2, et1], incs=[("sPE", 1)])
                        mwtg = P.op("tensor", mm_wtg, waits=[et1, et2], incs=[("sPE", 1)])
                        ewg = P.op("vector", lambda e: e.tensor_tensor(
                            wgb[:, :, :].rearrange("p a b -> p (a b)"), PS[4][:, 0:512],
                            c_wmask[:, :, :].rearrange("p a b -> p (a b)"), ALU.mult),
                            waits=[mwg], incs=[("sDV", 1)])
                        ewtg = P.op("vector", lambda e: e.tensor_tensor(
                            wtgb[:, :, :].rearrange("p a b -> p (a b)"), PS[5][:, 0:512],
                            c_wmask[:, :, :].rearrange("p a b -> p (a b)"), ALU.mult),
                            waits=[mwtg, ewg], incs=[("sDV", 1)])
                        def mm_rtop(e):
                            last = None
                            for g2 in range(4):
                                last = e.matmul(PS[4][:, 64 * g2:64 * (g2 + 1)], wgb[:, g2, :],
                                                phiY[:, 0 + g2, :], start=True, stop=True)
                            return last
                        def mm_rbot(e):
                            last = None
                            for g2 in range(4):
                                last = e.matmul(PS[5][:, 64 * g2:64 * (g2 + 1)], wtgb[:, g2, :],
                                                phiY[:, 4 + g2, :], start=True, stop=True)
                            return last
                        mrt = P.op("tensor", mm_rtop, waits=[ewg], incs=[("sPE", 1)])
                        mrb = P.op("tensor", mm_rbot, waits=[ewtg], incs=[("sPE", 1)])
                        ert = P.op("vector", lambda e: e.tensor_copy(
                            rtop[:, :, :].rearrange("p a b -> p (a b)"), PS[4][:, 0:256]),
                            waits=[mrt], incs=[("sDV", 1)])
                        erb = P.op("scalar", lambda e: e.activation(
                            rbot[:, :, :].rearrange("p a b -> p (a b)"), PS[5][:, 0:256], AF.Copy),
                            waits=[mrb], incs=[("sAC", 1)])
                        pc['inter_ready'] = P.op("sync", lambda e: e.nop(),
                                                 waits=[ert, erb, evcm], incs=[("sSP", 1)])
                        pc['inter_ps_free'] = [ert, erb]

                    if cfg.get('p1_stage', 99) < 7:
                        continue
                    def mm_apply(e, spi=spi, first=first, csl=csl):
                        last = None
                        started = False
                        for g2 in range(4):
                            for si, n in enumerate(LEAFG[g2]):
                                w = g2
                                last = e.matmul(PS[2][:, 4 * n:4 * n + 4],
                                                sig[:, 4 * w + si, :],
                                                kvqt[:, 0, 4 * n:4 * n + 4],
                                                start=not started, stop=False)
                                started = True
                        for g2 in range(4):
                            for si, cidx in enumerate(PACKG[g2]):
                                c = TOPC[cidx]
                                w = 4 + g2
                                cb = OUTCOL[cidx]
                                last = e.matmul(PS[2][:, c['L']], sig[:, 4 * w + si, :],
                                                wts[:, 0, cb:cb + c['bs']],
                                                start=False, stop=False)
                        for g2 in range(4):
                            for si, cidx in enumerate(PACKG[g2]):
                                c = TOPC[cidx]
                                w = 8 + g2
                                cb = OUTCOL[cidx]
                                st_ = False
                                stp = (not first) is False and g2 == 3 and si == len(PACKG[3]) - 1
                                last = e.matmul(PS[2][:, c['R']], sig[:, 4 * w + si, :],
                                                wts[:, 1, cb:cb + c['bs']],
                                                start=False, stop=stp)
                        if not first:
                            e.matmul(PS[2][:, 0:64], qb[:, csl], ctm[:, :],
                                     start=False, stop=False)
                            for g2 in range(4):
                                e.matmul(PS[2][:, 0:64],
                                         featQ[:, 4 + g2, 128 * spi:128 * (spi + 1)],
                                         rtop[:, g2, :], start=False, stop=False)
                            for g2 in range(3):
                                e.matmul(PS[2][:, 0:64],
                                         featQ[:, 8 + g2, 128 * spi:128 * (spi + 1)],
                                         rbot[:, g2, :], start=False, stop=False)
                            last = e.matmul(PS[2][:, 0:64],
                                            featQ[:, 8 + 3, 128 * spi:128 * (spi + 1)],
                                            rbot[:, 3, :], start=False, stop=True)
                        return last
                    apw = [sc_tok, evw, tr_tok, pc.get('y_done')]
                    if not first:
                        apw.append(pc.get('inter_ready'))
                    mma = P.op("tensor", mm_apply, waits=apw, incs=[("sPE", 1)])
                    pc['sig_used'] = mma
                    pc['wts_used'] = mma
                    pc['feats_used'] = mma
                    pc['inter_used'] = mma

                    if cfg.get('p1_stage', 99) < 8:
                        continue
                    def mm_cu(e):
                        e.matmul(PS[6][0:64, 0:64], kvqt[:, 0, :], kvqt[:, 1, :],
                                 start=True, stop=True)
                        return e.matmul(PS[7][0:64, 0:64], kvqt[:, 1, :], kvqt[:, 0, :],
                                        start=True, stop=True)
                    cuw = [tr_tok, pc.get('cu_psum_free')]
                    mcu = P.op("tensor", mm_cu, waits=cuw, incs=[("sPE", 1)])
                    addw = [mcu, pc.get('cadd_tok')]
                    if not first:
                        addw.append(pc.get('inter_ready'))
                    ca1 = P.op("vector", lambda e: e.tensor_add(cs_f[:, :], cs_f[:, :],
                                                                PS[6][0:64, 0:64]),
                               waits=addw, incs=[("sDV", 1)])
                    ca2 = P.op("vector", lambda e: e.tensor_add(ct_f[:, :], ct_f[:, :],
                                                                PS[7][0:64, 0:64]),
                               waits=addw, incs=[("sDV", 1)])
                    pc['cu_psum_free'] = [ca1, ca2]
                    pc['cadd_tok'] = P.op("sync", lambda e: e.nop(), waits=[ca1, ca2],
                                          incs=[("sSP", 1)])

                    if cfg.get('p1_stage', 99) < 9:
                        continue
                    dden = P.op("vector", lambda e: e.scalar_tensor_tensor(
                        dsc[:, :], PS[2][:, 64:128], 1.0, kvqt[:, 2, :],
                        ALU.mult, ALU.mult, accum_out=den[:, :]),
                        waits=[mmk, tr_tok], incs=[("sDV", 1)])
                    pc['den_done'] = dden
                    b1_ = P.op("sync", lambda e: e.nop(), waits=[dden], incs=[("sSP", 1)])
                    dmax = P.op("vector", lambda e: e.tensor_scalar_max(den2[:, :], den[:, :], 1e-6),
                                waits=[b1_], incs=[("sDV", 1)])
                    b2_ = P.op("sync", lambda e: e.nop(), waits=[dmax], incs=[("sSP", 1)])
                    rcd = P.op("vector", lambda e: e.reciprocal(rden[:, :], den2[:, :]),
                               waits=[b2_], incs=[("sDV", 1)])
                    b2c = P.op("sync", lambda e: e.nop(), waits=[rcd], incs=[("sSP", 1)])
                    ydv = P.op("vector", lambda e: e.tensor_scalar(
                        ytb[:, :], PS[2][:, 0:64], rden[:, 0:1], None, ALU.mult),
                        waits=[b2c, mma], incs=[("sDV", 1)])
                    pc['y_done'] = ydv

                    if cfg.get('p1_stage', 99) < 10:
                        continue
                    myt = P.op("tensor", lambda e: e.transpose(PS3B[0:64, 0:128], ytb[:, :],
                                                               c_identb[:, :]),
                               waits=[ydv, evw], incs=[("sPE", 1)])
                    eyt = P.op("scalar", lambda e: e.activation(ytT[:, :], PS3B[0:64, 0:128],
                                                                AF.Copy),
                               waits=[myt], incs=[("sAC", 1)])
                    def mm_op(e, csl=csl):
                        e.matmul(PS[0][:, 0:512], ytT[:, :], w_o[:, 0:512], start=True, stop=True)
                        e.matmul(PS[1][:, 0:512], ytT[:, :], w_o[:, 512:1024], start=True, stop=True)
                        e.matmul(PS[2][0:8, 128:256], wowr[:, :], ytT[:, :],
                                 start=True, stop=False)
                        return e.matmul(PS[2][0:8, 128:256], c_identb[0:8, 0:8],
                                        xwrt[:, csl], start=False, stop=True)
                    mop = P.op("tensor", mm_op,
                               waits=[eyt, pc.get('oproj_used'), wave_ev[NW - 1], wave_ev[NW - 2],
                                      ydv],
                               incs=[("sPE", 1)])
                    lxr = P.op("sync", lambda e, t0=t0: e.dma_start(
                        out=xres[:, 0, :], in_=X[t0:t0 + CH, :]),
                        waits=[pc.get('xrow_used')], incs=d16("dR"))
                    ra = P.op("vector", lambda e: e.tensor_add(r1[:, 0:512], PS[0][:, 0:512],
                                                               xres[:, 0, 0:512]),
                              waits=[mop, lxr, pc.get('r1_used')], incs=[("sDV", 1)])
                    rb_ = P.op("vector", lambda e: e.tensor_add(r1[:, 512:1024], PS[1][:, 0:512],
                                                                xres[:, 0, 512:1024]),
                               waits=[mop, lxr, pc.get('r1_used')], incs=[("sDV", 1)])
                    pc['oproj_used'] = [ra, rb_]
                    pc['xrow_used'] = [ra, rb_]
                    erw = P.op("vector", lambda e: e.tensor_copy(
                        rawt8[:, :], PS[2][0:8, 128:256]),
                        waits=[mop, pc.get('rawt8_used')], incs=[("sDV", 1)])

                    if cfg.get('p1_stage', 99) < 11:
                        continue
                    bs1 = P.op("vector", lambda e: e.bn_stats(bnst[:, 0:6], r1[:, 0:512]),
                               waits=[ra, rb_], incs=[("sDV", 1)])
                    bs2 = P.op("vector", lambda e: e.bn_stats(bnst[:, 6:12], r1[:, 512:1024]),
                               waits=[ra, rb_], incs=[("sDV", 1)])
                    b3_ = P.op("sync", lambda e: e.nop(), waits=[bs1, bs2], incs=[("sSP", 1)])
                    ag = P.op("vector", lambda e: e.bn_aggr(bnag[:, :], bnst[:, :]),
                              waits=[b3_], incs=[("sDV", 1)])
                    sq = P.op("scalar", lambda e: e.activation(sqv[:, :], bnag[:, 1:2], AF.Sqrt,
                                                               bias=float(EPS)),
                              waits=[ag, ctok], incs=[("sAC", 1)])
                    rc = P.op("vector", lambda e: e.reciprocal(rstd[:, :], sqv[:, :]),
                              waits=[sq], incs=[("sDV", 1)])
                    b4_ = P.op("sync", lambda e: e.nop(), waits=[rc], incs=[("sSP", 1)])
                    nm = P.op("vector", lambda e: e.scalar_tensor_tensor(
                        nmr[:, :], bnag[:, 0:1], -1.0, rstd[:, :], ALU.mult, ALU.mult),
                        waits=[b4_], incs=[("sDV", 1)])
                    b5_ = P.op("sync", lambda e: e.nop(), waits=[nm], incs=[("sSP", 1)])
                    ln = P.op("vector", lambda e: e.tensor_scalar(ln1t[:, :], r1[:, :],
                                                                  rstd[:, 0:1], nmr[:, 0:1],
                                                                  ALU.mult, ALU.add),
                              waits=[b5_, pc.get('ln1_used')], incs=[("sDV", 1)])
                    lnb = P.op("scalar", lambda e: e.activation(ln1tb[:, :], ln1t[:, :], AF.Copy),
                               waits=[ln], incs=[("sAC", 1)])
                    st1_ = P.op("sync", lambda e, t0=t0: e.dma_start(out=LN1F[t0:t0 + CH, :],
                                                                    in_=ln1t[:, :]),
                                waits=[ln], incs=d16("dL1"))
                    st2_ = P.op("sync", lambda e, t0=t0: e.dma_start(out=LN1B[t0:t0 + CH, :],
                                                                    in_=ln1tb[:, :]),
                                waits=[lnb], incs=d16("dL2"))
                    pc['ln1_used'] = [st1_, st2_]
                    pc['r1_used'] = ln

                    if cfg.get('p1_stage', 99) < 12:
                        continue
                    mlt = P.op("tensor", lambda e: e.transpose(PS[5][:, 256:264], rawt8[:, :],
                                                               c_identf[0:8, 0:8]),
                               waits=[erw, eyt,
                                      pc.get('inter_ready') if not first else None],
                               incs=[("sPE", 1)])
                    pc['rawt8_used'] = mlt
                    elt = P.op("vector", lambda e: e.tensor_copy(lraw[:, :], PS[5][:, 256:264]),
                               waits=[mlt, pc.get('lraw_used')], incs=[("sDV", 1)])
                    pc['lraw_psum_free'] = elt
                    lg1 = P.op("scalar", lambda e: e.activation(lraw[:, :], lraw[:, :], AF.Copy,
                                                                scale=rstd[:, 0:1]),
                               waits=[elt, b5_], incs=[("sAC", 1)])
                    tt = t0 // 128
                    lg2_ = P.op("vector", lambda e, tt=tt: e.scalar_tensor_tensor(
                        logits[:, tt, :], cswr_bc[:, :], nmr[:, 0:1], lraw[:, :],
                        ALU.mult, ALU.add),
                        waits=[lg1], incs=[("sDV", 1)])
                    pc['lraw_used'] = lg2_
                    b6_ = P.op("sync", lambda e: e.nop(), waits=[lg2_], incs=[("sSP", 1)])
                    lg3 = P.op("vector", lambda e, tt=tt: e.tensor_add(
                        logits[:, tt, :], logits[:, tt, :], br_bc[:, :]),
                        waits=[b6_], incs=[("sDV", 1)])

                    pc['ps3_free'] = [elt, eyt, evw]
                    pc['kvqt_used'] = [mmk, mma, mcu, ca1, ca2, dden, ydv]
                    if last_chunk:
                        batch_carry = [pc.get('cadd_tok'), pc.get('inter_ready'), lg3]
            scan_done = P.op("sync", lambda e: e.nop(),
                             waits=[{s: cnt.get(s, 0) for s in SEMS if s != "sSP"}],
                             incs=[("sSP", 1)])
            P.emit(nc, sems)

        if do_moe:
            emit_moe(nc, cnt, sems, dict(
                T=T, NTT=NTT, CAP=CAP, NEXP=NEXP, HF=HF,
                NTILES=NTILES, GL=GL, GS16=GS16,
                n_cores=n_cores,
                logits=logits, wgt=wgt, PS=PS,
                c_iota8=c_iota8, c_tri64b=c_tri64b, c_triub=c_triub, c_onescf=c_onescf,
                c_identf=c_identf, c_identb=c_identb, c_prefm=c_prefm,
                c_ones128b=c_ones128b, c_ones128f=c_ones128f, SCR=SCR,
                LN1F=LN1F, LN1B=LN1B, OUT2=OUT2, LISTS_F=LISTS_F,
                CC_IN=CC_IN, CC_OUT=CC_OUT, OUT=OUT, TOKSRC=TOKSRC,
                CIDX0=CIDX0, SCR2=SCR2, IOT16=IOT16, TILV=TILV,
                dbg_p2=cfg.get('dbg_p2'), dbg_p3=cfg.get('dbg_p3'), dbg_wb=cfg.get('dbg_wb'), dbg_g1=cfg.get('dbg_g1'),
                W1B=W1B, W2B=W2B, B1R=B1R, B2=B2, nc=nc))
        else:
            P = Prog(cnt)
            src = LN1F if debug_dump in (None, 'ln1') else LN1F
            sts = []
            for i in range(T // 128):
                sts.append(P.op("sync", lambda e, i=i: e.dma_start(
                    out=OUT[128 * i:128 * (i + 1), :], in_=src[128 * i:128 * (i + 1), :]),
                    incs=[("dDBG", 16)]))
            P.op("sync", lambda e: e.nop(), waits=[{"dDBG": cnt["dDBG"]}], incs=[("sSP", 1)])
            P.emit(nc, sems)

    nc.compile()
    return nc


def emit_moe(nc, cnt, sems, env):
    import contextlib
    T = env['T']; NTT = env['NTT']; CAP = env['CAP']
    NEXP = env['NEXP']; HF = env['HF']
    NTILES = env['NTILES']; GL = env['GL']; GS16 = env['GS16']
    n_cores = env['n_cores']
    logits = env['logits']; wgt = env['wgt']; PS = env['PS']
    c_iota8 = env['c_iota8']; c_tri64b = env['c_tri64b']; c_onescf = env['c_onescf']
    c_identf = env['c_identf']; c_ones128b = env['c_ones128b']; c_prefm = env['c_prefm']
    LN1F = env['LN1F']; LN1B = env['LN1B']; OUT2 = env['OUT2']
    LISTS_F = env['LISTS_F']
    CC_IN = env['CC_IN']; CC_OUT = env['CC_OUT']; OUT = env['OUT']; TOKSRC = env['TOKSRC']
    W1B = env['W1B']; W2B = env['W2B']; B1R = env['B1R']; B2 = env['B2']
    IOT16 = env['IOT16']; TILV = env['TILV']
    SEMS = list(cnt.keys())
    HQ = HF // 4
    NHBQ = HF // 512

    with contextlib.ExitStack() as st2:
        sbP = lambda *a: st2.enter_context(nc.sbuf_tensor(*a))
        cbidx = sbP("cbidx", [128, 2 * (T // 16) + 16], I16)
        stF = st2.enter_context(contextlib.ExitStack())
        sb2 = lambda *a: stF.enter_context(nc.sbuf_tensor(*a))
        top8 = sb2("top8", [128, E], F32)
        idx8 = sb2("idx8", [128, E], U32)
        e01s = sb2("e01s", [128, NTT, 2], F32)
        g01s = sb2("g01s", [128, NTT, 2], F32)
        dgap = sb2("dgap", [128, 1], F32)
        e01f = sb2("e01f", [128, 2], F32)
        s0b = sb2("s0b", [128, E], F32)
        s1b = sb2("s1b", [128, E], F32)
        selsum = sb2("selsum", [128, E], F32)
        selbf = sb2("selbf", [128, E], BF16)
        cums = sb2("cums", [128, NTT, E], F32)
        sels = sb2("sels", [128, NTT, E], F32)
        csrow = sb2("csrow", [1, NTT * E], F32)
        colsf = sb2("colsf", [NTT, E], F32)
        colsb = sb2("colsb", [NTT, E], BF16)
        toffs = sb2("toffs", [NTT, E], F32)
        trow = sb2("trow", [1, NTT * E], F32)
        countsr = sb2("countsr", [1, E], F32)
        agc = sb2("agc", [n_cores, E], F32)
        agm = sb2("agm", [n_cores, E], F32)
        corow = sb2("corow", [1, E], F32)
        cobc = sb2("cobc", [128, E], F32)
        rloc = sb2("rloc", [128, E], F32)
        lrank8 = sb2("lrank8", [128, E], F32)
        grank8 = sb2("grank8", [128, E], F32)
        keep8 = sb2("keep8", [128, E], F32)
        skb = sb2("skb", [128, 2, E], F32)
        junk8 = sb2("junk8", [128, 6, E], F32)
        lr01 = sb2("lr01", [128, 2], F32)
        kp01 = sb2("kp01", [128, 2], F32)
        li32 = sb2("li32", [128, 2], mybir.dt.int32)
        lm16 = sb2("lm16", [128, 2], mybir.dt.int32)
        ld16 = sb2("ld16", [128, 2], mybir.dt.int32)
        lu32 = sb2("lu32", [128, 2], mybir.dt.int32)
        sdi32 = sb2("sdi32", [128, 2], mybir.dt.int32)
        sdall = sb2("sdall", [128, 2 * NTT], mybir.dt.int32)
        cball = sb2("cball", [128, 2 * NTT], F32)
        cidx0 = sb2("cidx0_sb", [128, 1], mybir.dt.int32)
        cidxt = sb2("cidxt", [128, 2, 2], mybir.dt.int32)
        cb16f = sb2("cb16f", [128, T // 16], F32)
        lst16 = sb2("lst16", [128, GS16 // 4], F32)
        TOKSRC_sb = sb2("toksrc_sb", [128, NTT], F32)
        zeros = sb2("zeros", [128, GL // 128], F32)
        glist = sb2("glist", [128, GS16 + 16], I16)
        I32 = mybir.dt.int32
        c_tilv = sb2("c_tilv", [128, 1], F32)
        iotsb = sb2("iotsb", [128, 256], F32)
        ccol = sb2("ccol", [8, 1], F32)
        pca = sb2("pca", [8, 1], F32)
        pci = sb2("pci", [8, 1], I32)
        pcc = sb2("pcc", [8, 1], F32)
        pcnt = sb2("pcnt", [8, 1], F32)
        pcntb = sb2("pcntb", [8, 1], BF16)
        tbcol = sb2("tbcol", [8, 1], F32)
        tbrow = sb2("tbrow", [1, 8], F32)
        tbbc = sb2("tbbc", [128, 8], F32)
        til8 = sb2("til8", [128, 8], F32)
        etcol = sb2("etcol", [128, 1], F32)
        etrow = sb2("etrow", [1, NTILES], F32)
        etbc = sb2("etbc", [128, NTILES], F32)
        etw1f = sb2("etw1f", [128, NTILES], F32)
        etw2f = sb2("etw2f", [128, NTILES], F32)
        etb1f = sb2("etb1f", [128, NTILES], F32)
        tbt = sb2("tbt", [128, 2], F32)
        tbt32 = sb2("tbt32", [128, 2], I32)
        gi32 = sb2("gi32", [128, 2], I32)
        iw1x = sb2("iw1x", [128, 2, 64], I16)
        iw2x = sb2("iw2x", [128, 2, 256], I16)
        ib1x = sb2("ib1x", [128, 2, 8], I16)
        ib2x = sb2("ib2x", [128, 2, 8], I16)
        iw1f = sb2("iw1f", [128, 2, 64], F32)
        iw2f = sb2("iw2f", [128, 2, 256], F32)
        ib1f = sb2("ib1f", [128, 2, 8], F32)
        ib2f = sb2("ib2f", [128, 2, 8], F32)
        w1full = sb2("w1full", [128, 8, HF], BF16)
        w2f = sb2("w2f", [128, HF // 128, D_MODEL], BF16)
        xg = sb2("xg", [128, 1, 8, 512], BF16)
        hsb = sb2("hsb", [128, HF // 128, 512], BF16)
        b1t = sb2("b1t", [128, 2, 1, 64], F32)
        b2g = sb2("b2g", [128, 1, 1, D_MODEL], F32)
        o2r = sb2("o2r", [128, 1, D_MODEL], F32)

        P = Prog(cnt, gate={"sSP": cnt["sSP"]})
        d16 = lambda s="dC": [(s, 16)]
        pcm = {}
        msi2 = P.op("vector", lambda e: e.memset(cbidx[:, :], 0),
                    incs=[("sDV", 1)])
        msi3 = P.op("vector", lambda e: e.memset(glist[:, :], 0), incs=[("sDV", 1)])
        msi_tok = P.op("sync", lambda e: e.nop(), waits=[msi2, msi3], incs=[("sSP", 1)])

        tl = P.op("sync", lambda e: e.dma_start(out=TOKSRC_sb[:, :], in_=TOKSRC[:, :]),
                  incs=d16("dP"))
        tvl = P.op("sync", lambda e: e.dma_start(out=c_tilv[:, :], in_=TILV[:, :]),
                   incs=d16("dP"))
        iol = P.op("sync", lambda e: e.dma_start(out=iotsb[:, :], in_=IOT16[:, :]),
                   incs=d16("dP"))
        tile_toks = []
        for tt in range(NTT):
            mx = P.op("vector", lambda e, tt=tt: e.max(top8[:, :], logits[:, tt, :]),
                      waits=[pcm.get('tb')], incs=[("sDV", 1)])
            b1 = P.op("sync", lambda e: e.nop(), waits=[mx], incs=[("sSP", 1)])
            mi = P.op("vector", lambda e, tt=tt: e.max_index(idx8[:, :], top8[:, :],
                                                             logits[:, tt, :]),
                      waits=[b1], incs=[("sDV", 1)])
            d01 = P.op("vector", lambda e: e.tensor_sub(dgap[:, :], top8[:, 0:1], top8[:, 1:2]),
                       waits=[b1], incs=[("sDV", 1)])
            g0 = P.op("scalar", lambda e, tt=tt: e.activation(g01s[:, tt, 0:1], dgap[:, :],
                                                              AF.Sigmoid),
                      waits=[d01], incs=[("sAC", 1)])
            g1 = P.op("scalar", lambda e, tt=tt: e.activation(g01s[:, tt, 1:2], dgap[:, :],
                                                              AF.Sigmoid, scale=-1.0),
                      waits=[d01], incs=[("sAC", 1)])
            b2_ = P.op("sync", lambda e: e.nop(), waits=[mi, d01], incs=[("sSP", 1)])
            ef = P.op("vector", lambda e, tt=tt: e.tensor_copy(e01s[:, tt, :], idx8[:, 0:2]),
                      waits=[b2_], incs=[("sDV", 1)])
            b3_ = P.op("sync", lambda e: e.nop(), waits=[ef], incs=[("sSP", 1)])
            sl0 = P.op("vector", lambda e, tt=tt: e.tensor_scalar(
                s0b[:, :], c_iota8[:, :], e01s[:, tt, 0:1], None, ALU.is_equal),
                waits=[b3_], incs=[("sDV", 1)])
            sl1 = P.op("vector", lambda e, tt=tt: e.tensor_scalar(
                s1b[:, :], c_iota8[:, :], e01s[:, tt, 1:2], None, ALU.is_equal),
                waits=[b3_], incs=[("sDV", 1)])
            b4_ = P.op("sync", lambda e: e.nop(), waits=[sl0, sl1], incs=[("sSP", 1)])
            sm = P.op("vector", lambda e, tt=tt: e.tensor_add(sels[:, tt, :], s0b[:, :], s1b[:, :]),
                      waits=[b4_], incs=[("sDV", 1)])
            sc = P.op("scalar", lambda e, tt=tt: e.activation(selbf[:, :], sels[:, tt, :], AF.Copy),
                      waits=[sm, pcm.get('selbf_used')], incs=[("sAC", 1)])
            mcs = P.op("tensor", lambda e, tt=tt: e.matmul(PS[4][:, 0:E], env['c_triub'][:, :],
                                                           selbf[:, :], start=True, stop=True),
                       waits=[sc, pcm.get('cums_psum_free')], incs=[("sPE", 1)])
            mcol = P.op("tensor", lambda e, tt=tt: e.matmul(PS[5][0:1, E * tt:E * (tt + 1)],
                                                            c_ones128b[:, :], selbf[:, :],
                                                            start=True, stop=True),
                        waits=[sc], incs=[("sPE", 1)])
            pcm['selbf_used'] = [mcs, mcol]
            ecs = P.op("vector", lambda e, tt=tt: e.tensor_copy(cums[:, tt, :], PS[4][:, 0:E]),
                       waits=[mcs], incs=[("sDV", 1)])
            pcm['cums_psum_free'] = ecs
            pcm['tb'] = [ecs]
            tile_toks.append([ecs, mcol, g0, g1, sm])
        ecr = P.op("vector", lambda e: e.tensor_copy(csrow[:, :], PS[5][0:1, 0:NTT * E]),
                   waits=tile_toks, incs=[("sDV", 1)])
        SCR = env['SCR']
        sc1 = P.op("sync", lambda e: e.dma_start(
            out=SCR[0:NTT * E].rearrange("(a b) -> a b", a=1), in_=csrow[:, :]),
            waits=[ecr], incs=d16("dQ"))
        sc2 = P.op("sync", lambda e: e.dma_start(
            out=colsf[:, :], in_=SCR[0:NTT * E].rearrange("(a b) -> a b", a=NTT)),
            waits=[sc1], incs=d16("dP"))
        cbf = P.op("vector", lambda e: e.tensor_copy(colsb[:, :], colsf[:, :]),
                   waits=[sc2], incs=[("sDV", 1)])
        mto = P.op("tensor", lambda e: e.matmul(PS[4][0:NTT, 0:E], c_tri64b[0:NTT, 0:NTT],
                                                colsb[:, :], start=True, stop=True),
                   waits=[cbf, pcm.get('cums_psum_free')], incs=[("sPE", 1)])
        mcnt = P.op("tensor", lambda e: e.matmul(PS[4][0:1, 64:64 + E], c_ones128b[0:NTT, :],
                                                 colsb[:, :], start=True, stop=True),
                    waits=[cbf], incs=[("sPE", 1)])
        eto = P.op("vector", lambda e: e.tensor_copy(toffs[:, :], PS[4][0:NTT, 0:E]),
                   waits=[mto], incs=[("sDV", 1)])
        ecn = P.op("vector", lambda e: e.tensor_copy(countsr[:, :], PS[4][0:1, 64:64 + E]),
                   waits=[mcnt], incs=[("sDV", 1)])
        st5 = P.op("sync", lambda e: e.dma_start(
            out=SCR[1024:1032].rearrange("(a b) -> a b", a=1), in_=countsr[:, :]),
            waits=[ecn], incs=d16("dQ"))
        lt5 = P.op("sync", lambda e: e.dma_start(
            out=ccol[:, :], in_=SCR[1024:1032].rearrange("(a b) -> a b", a=8)),
            waits=[st5], incs=d16("dP"))
        pc1 = P.op("vector", lambda e: e.tensor_scalar_add(pca[:, :], ccol[:, :], 511.0),
                   waits=[lt5], incs=[("sDV", 1)])
        pb1 = P.op("sync", lambda e: e.nop(), waits=[pc1], incs=[("sSP", 1)])
        pc3 = P.op("vector", lambda e: e.tensor_copy(pci[:, :], pca[:, :]),
                   waits=[pb1], incs=[("sDV", 1)])
        pb2 = P.op("sync", lambda e: e.nop(), waits=[pc3], incs=[("sSP", 1)])
        pc3b = P.op("vector", lambda e: e.tensor_scalar(pci[:, :], pci[:, :], ~511, None,
                                                        ALU.bitwise_and),
                    waits=[pb2], incs=[("sDV", 1)])
        pb3 = P.op("sync", lambda e: e.nop(), waits=[pc3b], incs=[("sSP", 1)])
        pc4 = P.op("vector", lambda e: e.tensor_copy(pcnt[:, :], pci[:, :]),
                   waits=[pb3], incs=[("sDV", 1)])
        pcb = P.op("vector", lambda e: e.tensor_copy(pcntb[:, :], pcnt[:, :]),
                   waits=[pc4], incs=[("sDV", 1)])
        mtb = P.op("tensor", lambda e: e.matmul(PS[6][0:8, 8:9], c_tri64b[0:8, 0:8],
                                                pcntb[:, :], start=True, stop=True),
                   waits=[pcb], incs=[("sPE", 1)])
        etb = P.op("vector", lambda e: e.tensor_copy(tbcol[:, :], PS[6][0:8, 8:9]),
                   waits=[mtb], incs=[("sDV", 1)])
        st6 = P.op("sync", lambda e: e.dma_start(
            out=SCR[1040:1048].rearrange("(a b) -> a b", a=8), in_=tbcol[:, :]),
            waits=[etb], incs=d16("dQ"))
        lt6 = P.op("sync", lambda e: e.dma_start(
            out=tbrow[:, :], in_=SCR[1040:1048].rearrange("(a b) -> a b", a=1)),
            waits=[st6], incs=d16("dP"))
        mtbb = P.op("tensor", lambda e: e.matmul(PS[6][:, 16:24], c_onescf[:, :],
                                                 tbrow[:, :], start=True, stop=True),
                    waits=[lt6], incs=[("sPE", 1)])
        etbb = P.op("vector", lambda e: e.tensor_copy(tbbc[:, :], PS[6][:, 16:24]),
                    waits=[mtbb], incs=[("sDV", 1)])
        tle = P.op("vector", lambda e: e.tensor_scalar(til8[:, :], tbbc[:, :],
                                                       c_tilv[:, 0:1], None, ALU.is_le),
                   waits=[etbb, tvl], incs=[("sDV", 1)])
        tb0 = P.op("sync", lambda e: e.nop(), waits=[tle], incs=[("sSP", 1)])
        ets = P.op("scalar", lambda e: e.activation(til8[:, :], til8[:, :], AF.Copy,
                                                    accum_out=etcol[:, :]),
                   waits=[tb0], incs=[("sAC", 1)])
        tb1 = P.op("sync", lambda e: e.nop(), waits=[ets], incs=[("sSP", 1)])
        ets2 = P.op("vector", lambda e: e.tensor_scalar_add(etcol[:, :], etcol[:, :], -1.0),
                    waits=[tb1], incs=[("sDV", 1)])
        st7 = P.op("sync", lambda e: e.dma_start(
            out=SCR[1056:1056 + NTILES].rearrange("(a b) -> a b", b=1),
            in_=etcol[0:NTILES, :]),
            waits=[ets2], incs=d16("dQ"))
        lt7 = P.op("sync", lambda e: e.dma_start(
            out=etrow[:, :], in_=SCR[1056:1056 + NTILES].rearrange("(a b) -> a b", a=1)),
            waits=[st7], incs=d16("dP"))
        metb = P.op("tensor", lambda e: e.matmul(PS[6][:, 64:64 + NTILES], c_onescf[:, :],
                                                 etrow[:, :], start=True, stop=True),
                    waits=[lt7], incs=[("sPE", 1)])
        eetb = P.op("vector", lambda e: e.tensor_copy(etbc[:, :], PS[6][:, 64:64 + NTILES]),
                    waits=[metb], incs=[("sDV", 1)])
        tb2 = P.op("sync", lambda e: e.nop(), waits=[eetb], incs=[("sSP", 1)])
        eiw1 = P.op("vector", lambda e: e.tensor_scalar(etw1f[:, :], etbc[:, :],
                                                        float(D_MODEL), None, ALU.mult),
                    waits=[tb2], incs=[("sDV", 1)])
        eiw2 = P.op("vector", lambda e: e.tensor_scalar(etw2f[:, :], etbc[:, :],
                                                        float(HF), None, ALU.mult),
                    waits=[tb2], incs=[("sDV", 1)])
        eib1 = P.op("vector", lambda e: e.tensor_scalar(etb1f[:, :], etbc[:, :],
                                                        128.0, None, ALU.mult),
                    waits=[tb2], incs=[("sDV", 1)])
        tb_tok = P.op("sync", lambda e: e.nop(), waits=[eiw1, eiw2, eib1],
                      incs=[("sSP", 1)])
        st3 = P.op("sync", lambda e: e.dma_start(
            out=SCR[0:NTT * E].rearrange("(a b) -> a b", a=NTT), in_=toffs[:, :]),
            waits=[eto, sc2], incs=d16("dQ"))
        lt3 = P.op("sync", lambda e: e.dma_start(
            out=trow[:, :], in_=SCR[0:NTT * E].rearrange("(a b) -> a b", a=1)),
            waits=[st3], incs=d16("dP"))
        st4 = P.op("sync", lambda e: e.dma_start(out=CC_IN.ap()[:, :], in_=countsr[:, :]),
                   waits=[ecn], incs=d16("dQ"))
        ccw = {"dQ": cnt["dQ"]}
        cc = P.op("gpsimd", lambda e: e.collective_compute(
            "AllGather", ALU.bypass, replica_groups=[list(range(n_cores))],
            ins=[CC_IN.ap().opt()], outs=[CC_OUT.ap().opt()]),
            waits=[st4, ccw], incs=[("sCC", 1)])
        lag = P.op("sync", lambda e: e.dma_start(out=agc[:, :], in_=CC_OUT.ap()[:, :]),
                   waits=[cc], incs=d16("dP"))
        mskc = P.op("vector", lambda e: e.tensor_scalar(agm[:, :], agc[:, :],
                                                        c_prefm[0:n_cores, 0:1], None, ALU.mult),
                    waits=[lag], incs=[("sDV", 1)])
        c_ones128f = env['c_ones128f']
        mco = P.op("tensor", lambda e: e.matmul(PS[4][0:1, 128:128 + E],
                                                c_ones128f[0:n_cores, 0:1],
                                                agm[:, :], start=True, stop=True),
                   waits=[mskc, eto, ecn], incs=[("sPE", 1)])
        eco = P.op("vector", lambda e: e.tensor_copy(corow[:, :], PS[4][0:1, 128:128 + E]),
                   waits=[mco], incs=[("sDV", 1)])
        mcob = P.op("tensor", lambda e: e.matmul(PS[4][:, 192:192 + E], c_onescf[:, :],
                                                 corow[:, :], start=True, stop=True),
                    waits=[eco], incs=[("sPE", 1)])
        ecob = P.op("vector", lambda e: e.tensor_copy(cobc[:, :], PS[4][:, 192:192 + E]),
                    waits=[mcob], incs=[("sDV", 1)])
        p2mid = P.op("sync", lambda e: e.nop(), waits=[lt3, ecob], incs=[("sSP", 1)])

        CIDX0 = env['CIDX0']; SCR2 = env['SCR2']
        lci = P.op("sync", lambda e: e.dma_start(out=cidx0[:, :], in_=CIDX0[:, :]),
                   incs=d16("dP"))
        zz = P.op("vector", lambda e: e.memset(zeros[:, :], 0.0), incs=[("sDV", 1)])
        zs = P.op("sync", lambda e: e.dma_start(
            out=LISTS_F[:, 0].rearrange("(p c) -> p c", p=128), in_=zeros[:, :]),
            waits=[zz], incs=d16("dQ"))
        zs_tok = {"dQ": cnt["dQ"]}
        sc_toks = []
        for tt in range(NTT):
            ofb = P.op("tensor", lambda e, tt=tt: e.matmul(
                PS[6][:, 0:E], c_onescf[:, :], trow[0:1, E * tt:E * (tt + 1)],
                start=True, stop=True),
                waits=[p2mid, pcm.get('ofb_free')], incs=[("sPE", 1)])
            rk = P.op("vector", lambda e, tt=tt: e.tensor_sub(rloc[:, :], cums[:, tt, :],
                                                              sels[:, tt, :]),
                      waits=[pcm.get('rk_free'), p2mid], incs=[("sDV", 1)])
            lra = P.op("vector", lambda e: e.tensor_add(lrank8[:, :], rloc[:, :], PS[6][:, 0:E]),
                       waits=[ofb, rk], incs=[("sDV", 1)])
            pcm['ofb_free'] = lra
            gra = P.op("vector", lambda e: e.tensor_add(grank8[:, :], lrank8[:, :], cobc[:, :]),
                       waits=[lra], incs=[("sDV", 1)])
            kp = P.op("vector", lambda e: e.tensor_scalar(keep8[:, :], grank8[:, :],
                                                          float(CAP), None, ALU.is_lt),
                      waits=[gra], incs=[("sDV", 1)])
            sk0 = P.op("vector", lambda e, tt=tt: e.tensor_scalar(
                skb[:, 0, :], c_iota8[:, :], e01s[:, tt, 0:1], None, ALU.is_equal),
                waits=[p2mid, pcm.get('skb_free')], incs=[("sDV", 1)])
            sk1 = P.op("vector", lambda e, tt=tt: e.tensor_scalar(
                skb[:, 1, :], c_iota8[:, :], e01s[:, tt, 1:2], None, ALU.is_equal),
                waits=[p2mid, pcm.get('skb_free')], incs=[("sDV", 1)])
            b5_ = P.op("sync", lambda e: e.nop(), waits=[kp, sk0, sk1], incs=[("sSP", 1)])
            ex = []
            for k2 in range(2):
                ex.append(P.op("vector", lambda e, k2=k2: e.scalar_tensor_tensor(
                    junk8[:, k2, :], skb[:, k2, :], 1.0, lrank8[:, :], ALU.mult, ALU.mult,
                    accum_out=lr01[:, k2:k2 + 1]),
                    waits=[b5_], incs=[("sDV", 1)]))
            for k2 in range(2):
                ex.append(P.op("vector", lambda e, k2=k2: e.scalar_tensor_tensor(
                    junk8[:, 2 + k2, :], skb[:, k2, :], 1.0, keep8[:, :], ALU.mult, ALU.mult,
                    accum_out=kp01[:, k2:k2 + 1]),
                    waits=[b5_, ex[-1]], incs=[("sDV", 1)]))
            for k2 in range(2):
                ex.append(P.op("vector", lambda e, k2=k2: e.scalar_tensor_tensor(
                    junk8[:, 4 + k2, :], skb[:, k2, :], 1.0, tbbc[:, :], ALU.mult, ALU.mult,
                    accum_out=tbt[:, k2:k2 + 1]),
                    waits=[b5_, etbb, pcm.get('tbt_used'), ex[-1]], incs=[("sDV", 1)]))
            pcm['rk_free'] = ex
            pcm['skb_free'] = ex
            b6_ = P.op("sync", lambda e: e.nop(), waits=ex, incs=[("sSP", 1)])
            wv = P.op("vector", lambda e, tt=tt: e.tensor_mul(
                wgt[:, tt, :], kp01[:, :], g01s[:, tt, :]),
                waits=[b6_], incs=[("sDV", 1)])
            cbv = P.op("vector", lambda e, tt=tt: e.scalar_tensor_tensor(
                cball[:, tt::NTT], lr01[:, :], 1.0, tbt[:, :], ALU.mult, ALU.add),
                waits=[b6_, pcm.get(('cbsc', tt - 2))], incs=[("sDV", 1)])
            li = P.op("vector", lambda e: e.tensor_copy(li32[:, :], lr01[:, :]),
                      waits=[b6_], incs=[("sDV", 1)])
            tbi = P.op("vector", lambda e: e.tensor_copy(tbt32[:, :], tbt[:, :]),
                      waits=[b6_], incs=[("sDV", 1)])
            pcm['tbt_used'] = [cbv, tbi]
            b7_ = P.op("sync", lambda e: e.nop(), waits=[li, tbi], incs=[("sSP", 1)])
            gq = P.op("vector", lambda e: e.tensor_add(gi32[:, :], li32[:, :], tbt32[:, :]),
                      waits=[b7_], incs=[("sDV", 1)])
            b7b = P.op("sync", lambda e: e.nop(), waits=[gq], incs=[("sSP", 1)])
            m2_ = P.op("vector", lambda e: e.tensor_scalar(lm16[:, :], gi32[:, :], 15, None,
                                                           ALU.bitwise_and),
                       waits=[b7b], incs=[("sDV", 1)])
            d2_ = P.op("vector", lambda e: e.tensor_scalar(ld16[:, :], gi32[:, :], 4, None,
                                                           ALU.logical_shift_right),
                       waits=[b7b], incs=[("sDV", 1)])
            b8_ = P.op("sync", lambda e: e.nop(), waits=[m2_, d2_], incs=[("sSP", 1)])
            u_ = P.op("vector", lambda e: e.scalar_tensor_tensor(
                lu32[:, :], lm16[:, :], GS16, ld16[:, :], ALU.mult, ALU.add),
                waits=[b8_], incs=[("sDV", 1)])
            b9_ = P.op("sync", lambda e: e.nop(), waits=[u_], incs=[("sSP", 1)])
            sdq = P.op("vector", lambda e, tt=tt: e.tensor_copy(
                sdall[:, tt::NTT], lu32[:, :]),
                waits=[b9_, pcm.get(('lsc', tt - 2))], incs=[("sDV", 1)])
            for k2 in range(2):
                t1 = P.op("gpsimd", lambda e, tt=tt, k2=k2: e.indirect_dma_start(
                    LISTS_F[:, :], bass.IndirectOffsetOnAxis(sdall[:, k2 * NTT + tt:k2 * NTT + tt + 1], 0),
                    TOKSRC_sb[:, tt:tt + 1], None),
                    waits=[sdq, zs_tok, tl], incs=d16("gSC"))
                pcm[('lsc', tt)] = t1 if k2 == 0 else [pcm[('lsc', tt)], t1]
                cia = P.op("vector", lambda e, tt=tt, k2=k2: e.tensor_scalar_add(
                    cidxt[:, tt % 2, k2:k2 + 1], cidx0[:, :], k2 * T + 8 * tt),
                    waits=[lci, pcm.get(('cbsc', tt - 2))], incs=[("sDV", 1)])
                t2 = P.op("gpsimd", lambda e, tt=tt, k2=k2: e.indirect_dma_start(
                    SCR2[:, :], bass.IndirectOffsetOnAxis(cidxt[:, tt % 2, k2:k2 + 1], 0),
                    cball[:, k2 * NTT + tt:k2 * NTT + tt + 1], None),
                    waits=[cbv, cia], incs=d16("gSC"))
                pcm[('cbsc', tt)] = t2 if k2 == 0 else [pcm[('cbsc', tt)], t2]
            sc_toks.append([pcm[('lsc', tt)], pcm[('cbsc', tt)], wv])
        scat_tok = {"gSC": cnt["gSC"]}
        cbc_ = []
        prev_cp = None
        for k2 in range(2):
            lds = [P.op("sync", lambda e, k2=k2, g=g: e.dma_start(
                out=cb16f[16 * g:16 * (g + 1), :],
                in_=SCR2[k2 * T:(k2 + 1) * T, 0].rearrange("(p c) -> p c", p=16)),
                waits=[scat_tok, prev_cp], incs=d16("dP")) for g in range(8)]
            cp = P.op("vector", lambda e, k2=k2: e.tensor_copy(
                cbidx[:, k2 * (T // 16):(k2 + 1) * (T // 16)], cb16f[:, :]),
                waits=[lds, msi_tok], incs=[("sDV", 1)])
            cbc_.append(cp)
            prev_cp = cp
        GQ = GS16 // 4
        lm3s = []
        prev_lm3 = None
        for k2 in range(4):
            llf = [P.op("sync", lambda e, k2=k2, g=g: e.dma_start(
                out=lst16[16 * g:16 * (g + 1), :],
                in_=bass.AP(LISTS_F.tensor, k2 * GQ, [[GS16, 16], [1, GQ]])),
                waits=[scat_tok, prev_lm3], incs=d16("dP")) for g in range(8)]
            lm1 = P.op("vector", lambda e: e.tensor_scalar_add(lst16[:, :], lst16[:, :], -1.0),
                       waits=[llf], incs=[("sDV", 1)])
            lm2 = P.op("scalar", lambda e: e.activation(lst16[:, :], lst16[:, :], AF.Relu),
                       waits=[lm1], incs=[("sAC", 1)])
            lm3 = P.op("vector", lambda e, k2=k2: e.tensor_copy(
                glist[:, k2 * GQ:(k2 + 1) * GQ], lst16[:, :]),
                waits=[lm2, msi_tok], incs=[("sDV", 1)])
            lm3s.append(lm3)
            prev_lm3 = lm3
        p2done = P.op("sync", lambda e: e.nop(), waits=[lm3s, cbc_], incs=[("sSP", 1)])

        wbtok = {"gWB": cnt.get("gWB", 0)}
        if env.get('dbg_wb'):
            for r0 in range(0, 512, 512):
                P.op("gpsimd", lambda e, r0=r0: e.dma_start(
                    out=OUT[r0:r0 + 512, 0:HF], in_=W1B[r0:r0 + 512, :]),
                    waits=[wbtok], incs=d16("gSC"))
                P.op("gpsimd", lambda e, r0=r0: e.dma_start(
                    out=OUT[1024 + r0:1024 + r0 + 512, :], in_=W2B[r0:r0 + 512, :]),
                    waits=[wbtok], incs=d16("gSC"))
            P.op("sync", lambda e: e.nop(),
                 waits=[{s: cnt.get(s, 0) for s in SEMS if s != "sSP"}], incs=[("sSP", 1)])
            P.emit(nc, sems)
            return
        if env.get('dbg_p2'):
            dbg_rows = [(countsr, 8),
                        (tbrow, 8), (etrow, NTILES), (trow, NTT * E),
                        (corow, 8)]
            dops = []
            for ri, (src, w) in enumerate(dbg_rows):
                dops.append(P.op("sync", lambda e, ri=ri, src=src, w=w: e.dma_start(
                    out=OUT[ri:ri + 1, 0:w], in_=src[0:1, 0:w]),
                    waits=[p2done, tb_tok], incs=d16("dOUT")))
            gls = sb2("glsdbg", [128, 512], F32)
            gc_ = P.op("vector", lambda e: e.tensor_copy(gls[:, :], glist[:, 0:512]),
                       waits=[p2done], incs=[("sDV", 1)])
            dops.append(P.op("sync", lambda e: e.dma_start(
                out=OUT[8:136, 0:512], in_=gls[:, :]), waits=[gc_], incs=d16("dOUT")))
            cbs = sb2("cbsdbg", [128, 2 * (T // 16)], F32)
            cc2 = P.op("vector", lambda e: e.tensor_copy(cbs[:, :], cbidx[:, 0:2 * (T // 16)]),
                       waits=[p2done], incs=[("sDV", 1)])
            dops.append(P.op("sync", lambda e: e.dma_start(
                out=OUT[136:264, 0:2 * (T // 16)], in_=cbs[:, :]),
                waits=[cc2], incs=d16("dOUT")))
            P.op("sync", lambda e: e.nop(),
                 waits=[{s: cnt.get(s, 0) for s in SEMS if s != "sSP"}], incs=[("sSP", 1)])
            P.emit(nc, sems)
            return
        for t in range(NTILES):
            xbuf = t % 2
            ib = t % 2
            iwd = [tb_tok, iol, pcm.get(('idx_used', ib))]
            fw1 = P.op("vector", lambda e, t=t, ib=ib: e.tensor_scalar(
                iw1f[:, ib, :], iotsb[:, 0:64], etw1f[:, t:t + 1], None, ALU.add),
                waits=iwd, incs=[("sDV", 1)])
            fw2 = P.op("vector", lambda e, t=t, ib=ib: e.tensor_scalar(
                iw2f[:, ib, 0:HF // 16], iotsb[:, 0:HF // 16], etw2f[:, t:t + 1], None, ALU.add),
                waits=iwd, incs=[("sDV", 1)])
            fb1 = P.op("vector", lambda e, t=t, ib=ib: e.tensor_scalar(
                ib1f[:, ib, :], iotsb[:, 0:8], etb1f[:, t:t + 1], None, ALU.add),
                waits=iwd, incs=[("sDV", 1)])
            fb2 = P.op("vector", lambda e, t=t, ib=ib: e.tensor_scalar(
                ib2f[:, ib, :], iotsb[:, 0:8], 0.0, etbc[:, t:t + 1], ALU.mult, ALU.add),
                waits=iwd, incs=[("sDV", 1)])
            jbb = P.op("sync", lambda e: e.nop(), waits=[fw1, fw2, fb1, fb2],
                       incs=[("sSP", 1)])
            jw1 = P.op("vector", lambda e, ib=ib: e.tensor_copy(iw1x[:, ib, :],
                                                                iw1f[:, ib, :]),
                       waits=[jbb], incs=[("sDV", 1)])
            jw2 = P.op("vector", lambda e, ib=ib: e.tensor_copy(iw2x[:, ib, 0:HF // 16],
                                                                iw2f[:, ib, 0:HF // 16]),
                       waits=[jbb], incs=[("sDV", 1)])
            jb1 = P.op("vector", lambda e, ib=ib: e.tensor_copy(ib1x[:, ib, :],
                                                                ib1f[:, ib, :]),
                       waits=[jbb], incs=[("sDV", 1)])
            jb2 = P.op("vector", lambda e, ib=ib: e.tensor_copy(ib2x[:, ib, :],
                                                                ib2f[:, ib, :]),
                       waits=[jbb], incs=[("sDV", 1)])
            gx = P.op("gpsimd", lambda e, t=t: e.dma_gather(
                xg[:, 0], LN1B[:, :], glist[:, 32 * t:32 * (t + 1)],
                num_idxs=512, num_idxs_reg=512, elem_size=D_MODEL, transpose=True),
                waits=[p2done, pcm.get('xg_used')],
                incs=d16("gXG0"))
            gb1 = P.op("gpsimd", lambda e, ib=ib: e.dma_gather(
                b1t[:, ib], B1R[:, :], ib1x[:, ib, :],
                num_idxs=128, num_idxs_reg=128, elem_size=64),
                waits=[jb1, pcm.get(('b1_used', ib))], incs=d16("gB1"))
            gb2 = P.op("gpsimd", lambda e, ib=ib: e.dma_gather(
                b2g[:, 0], B2[:, 0, :], ib2x[:, ib, :],
                num_idxs=128, num_idxs_reg=128, elem_size=D_MODEL),
                waits=[jb2, pcm.get('b2_used')], incs=d16("gB2"))
            NW2 = HF // 4
            HT4 = HF // 512
            if env.get('dbg_g1') and t == 0:
                ixf = sb2("ixfdbg", [128, 256], F32)
                c1_ = P.op("vector", lambda e: e.tensor_copy(ixf[:, 0:64], iw1x[:, 0, :]),
                           waits=[jw1], incs=[("sDV", 1)])
                c2_ = P.op("vector", lambda e: e.tensor_copy(ixf[:, 64:64 + HF // 16],
                                                             iw2x[:, 0, 0:HF // 16]),
                           waits=[jw2], incs=[("sDV", 1)])
                c3_ = P.op("vector", lambda e: e.tensor_copy(ixf[:, 240:248], ib1x[:, 0, :]),
                           waits=[jb1], incs=[("sDV", 1)])
                c4_ = P.op("vector", lambda e: e.tensor_copy(ixf[:, 248:256], ib2x[:, 0, :]),
                           waits=[jb2], incs=[("sDV", 1)])
                d1_ = P.op("sync", lambda e: e.dma_start(out=OUT[0:128, 0:256],
                                                         in_=ixf[:, :]),
                           waits=[c1_, c2_, c3_, c4_], incs=d16("dOUT"))
                gxx = P.op("gpsimd", lambda e: e.dma_gather(
                    xg[:, 0], LN1B[:, :], glist[:, 0:32],
                    num_idxs=512, num_idxs_reg=512, elem_size=D_MODEL, transpose=True),
                    waits=[p2done], incs=d16("gXG0"))
                gw1d = P.op("gpsimd", lambda e: e.dma_gather(
                    w1full[:, :, :], W1B[:, :], iw1x[:, 0, :],
                    num_idxs=D_MODEL, num_idxs_reg=D_MODEL, elem_size=HF),
                    waits=[jw1, wbtok], incs=d16("gW1a"))
                gw2d = P.op("gpsimd", lambda e: e.dma_gather(
                    w2f[:, 0:HF // 512, :], W2B[:, :], iw2x[:, 0, 0:HF // 64],
                    num_idxs=HF // 4, num_idxs_reg=HF // 4, elem_size=D_MODEL),
                    waits=[jw2, wbtok], incs=d16("gW2"))
                gb1d = P.op("gpsimd", lambda e: e.dma_gather(
                    b1t[:, 0], B1R[:, :], ib1x[:, 0, :],
                    num_idxs=128, num_idxs_reg=128, elem_size=64),
                    waits=[jb1], incs=d16("gB1"))
                gb2d = P.op("gpsimd", lambda e: e.dma_gather(
                    b2g[:, 0], B2[:, 0, :], ib2x[:, 0, :],
                    num_idxs=128, num_idxs_reg=128, elem_size=D_MODEL),
                    waits=[jb2], incs=d16("gB2"))
                w1f_ = sb2("w1fdbg", [128, 8, HQ], F32)
                cw1 = P.op("vector", lambda e: e.tensor_copy(w1f_[:, :, :],
                                                             w1full[:, :, 0:HQ]),
                           waits=[gw1d], incs=[("sDV", 1)])
                d2_ = P.op("sync", lambda e: e.dma_start(
                    out=OUT[128:256, 0:4 * HQ],
                    in_=w1f_[:, 0:4, :].rearrange("p a b -> p (a b)")),
                    waits=[cw1], incs=d16("dOUT"))
                w2f_ = sb2("w2fdbg", [128, HF // 512, D_MODEL], F32)
                cw2 = P.op("vector", lambda e: e.tensor_copy(w2f_[:, :, :],
                                                             w2f[:, 0:HF // 512, :]),
                           waits=[gw2d], incs=[("sDV", 1)])
                d3_ = P.op("sync", lambda e: e.dma_start(
                    out=OUT[256:384, 0:D_MODEL],
                    in_=w2f_[:, 0:1, :].rearrange("p a b -> p (a b)")),
                    waits=[cw2], incs=d16("dOUT"))
                d4_ = P.op("sync", lambda e: e.dma_start(
                    out=OUT[384:512, 0:64], in_=b1t[:, 0, 0, :]),
                    waits=[gb1d], incs=d16("dOUT"))
                d5_ = P.op("sync", lambda e: e.dma_start(
                    out=OUT[512:640, 0:D_MODEL], in_=b2g[:, 0, 0, :]),
                    waits=[gb2d], incs=d16("dOUT"))
                xgf_ = sb2("xgfdbg", [128, 8, 512], F32)
                cxg = P.op("vector", lambda e: e.tensor_copy(xgf_[:, :, :], xg[:, 0]),
                           waits=[gxx], incs=[("sDV", 1)])
                d6_ = P.op("sync", lambda e: e.dma_start(
                    out=OUT[640:768, 0:1024],
                    in_=xgf_[:, 0:2, :].rearrange("p a b -> p (a b)")),
                    waits=[cxg], incs=d16("dOUT"))
                P.op("sync", lambda e: e.nop(),
                     waits=[{s: cnt.get(s, 0) for s in SEMS if s != "sSP"}],
                     incs=[("sSP", 1)])
                P.emit(nc, sems)
                return
            lw2 = [P.op("gpsimd", lambda e, ib=ib, k=k: e.dma_gather(
                w2f[:, HT4 * k:HT4 * (k + 1), :], W2B[:, :],
                iw2x[:, ib, (NW2 // 16) * k:(NW2 // 16) * (k + 1)],
                num_idxs=NW2, num_idxs_reg=NW2, elem_size=D_MODEL),
                waits=[jw2, wbtok, pcm.get('w2_used')], incs=d16("gW2"))
                for k in range(4)]
            lw1 = P.op("gpsimd", lambda e, ib=ib: e.dma_gather(
                w1full[:, :, :], W1B[:, :], iw1x[:, ib, :],
                num_idxs=D_MODEL, num_idxs_reg=D_MODEL, elem_size=HF),
                waits=[jw1, wbtok, pcm.get('w1_used')],
                incs=d16("gW1a"))
            hbtoks = []
            for ghb in range(HF // 128):
                def mmf1(e, ghb=ghb):
                    last = None
                    for d in range(8):
                        last = e.matmul(PS[ghb % 2][:, 0:512],
                                        w1full[:, d, 128 * ghb:128 * (ghb + 1)],
                                        xg[:, 0, d, :],
                                        start=(d == 0), stop=(d == 7))
                    return last
                mmh = P.op("tensor", mmf1,
                           waits=[gx, lw1, pcm.get(('f1ps', ghb % 2)),
                                  pcm.get('ffn2_done')],
                           incs=[("sPE", 1)])
                gel = P.op("scalar", lambda e, ghb=ghb, ib=ib:
                           e.activation(hsb[:, ghb, :], PS[ghb % 2][:, 0:512], AF.Gelu,
                                        bias=b1t[:, ib, 0, ghb:ghb + 1]),
                           waits=[mmh, gb1], incs=[("sAC", 1)])
                pcm[('f1ps', ghb % 2)] = gel
                hbtoks.append(gel)
            pcm['w1_used'] = hbtoks
            pcm[('b1_used', ib)] = hbtoks
            pcm[('idx_used', ib)] = [gb1, gb2] + lw2
            f2toks = []
            b2toks = []
            for sub in range(4):
                ppair = (PS[2], PS[4]) if sub % 2 == 0 else (PS[5], PS[6])
                def mmf2(e, sub=sub, ppair=ppair):
                    last = None
                    for ht in range(HF // 128):
                        last = e.matmul(ppair[0][:, 0:512],
                                        hsb[:, ht, 128 * sub:128 * (sub + 1)],
                                        w2f[:, ht, 0:512],
                                        start=(ht == 0), stop=(ht == HF // 128 - 1))
                        last = e.matmul(ppair[1][:, 0:512],
                                        hsb[:, ht, 128 * sub:128 * (sub + 1)],
                                        w2f[:, ht, 512:1024],
                                        start=(ht == 0), stop=(ht == HF // 128 - 1))
                    return last
                mm2 = P.op("tensor", mmf2,
                           waits=hbtoks + lw2 + [pcm.get(('f2ps', sub % 2))],
                           incs=[("sPE", 1)])
                obuf = sub % 2
                ev1 = P.op("vector", lambda e, ppair=ppair:
                           e.tensor_add(o2r[:, 0, 0:512],
                                        ppair[0][:, 0:512], b2g[:, 0, 0, 0:512]),
                           waits=[mm2, gb2, pcm.get('o2r_used')],
                           incs=[("sDV", 1)])
                ev2 = P.op("vector", lambda e, ppair=ppair:
                           e.tensor_add(o2r[:, 0, 512:1024],
                                        ppair[1][:, 0:512], b2g[:, 0, 0, 512:1024]),
                           waits=[mm2, gb2, pcm.get('o2r_used')],
                           incs=[("sDV", 1)])
                pcm[('f2ps', sub % 2)] = [ev1, ev2]
                b2toks += [ev1, ev2]
                so = P.op("sync", lambda e, t=t, sub=sub:
                          e.dma_start(out=OUT2[512 * t + 128 * sub:
                                               512 * t + 128 * (sub + 1), :],
                                      in_=o2r[:, 0, :]),
                          waits=[ev1, ev2], incs=d16("dO2"))
                pcm['o2r_used'] = so
                f2toks.append(mm2)
            pcm['ffn2_done'] = f2toks
            pcm['w2_used'] = f2toks
            pcm['xg_used'] = hbtoks
            pcm['b2_used'] = b2toks
        ffn_done = P.op("sync", lambda e: e.nop(),
                        waits=[{s: cnt.get(s, 0) for s in SEMS if s != "sSP"}], incs=[("sSP", 1)])
        if env.get('dbg_p3'):
            nr = min(T, GL)
            for r0 in range(0, nr, 512):
                P.op("sync", lambda e, r0=r0: e.dma_start(
                    out=OUT[r0:r0 + 512, :], in_=OUT2[r0:r0 + 512, :]),
                    waits=[ffn_done], incs=d16("dOUT"))
            P.op("sync", lambda e: e.nop(),
                 waits=[{s: cnt.get(s, 0) for s in SEMS if s != "sSP"}], incs=[("sSP", 1)])
            P.emit(nc, sems)
            return
        P.emit(nc, sems)
        wmA = P.wm
        stF.close()

        sbC = lambda *a: st2.enter_context(nc.sbuf_tensor(*a))
        g0r = sbC("g0r", [128, 1, D_MODEL], F32)
        g1r = sbC("g1r", [128, 1, D_MODEL], F32)
        fin = sbC("fin", [128, 2, D_MODEL], F32)
        bnst2 = sbC("bnst2", [128, 12], F32)
        bnag2 = sbC("bnag2", [128, 2], F32)
        rstd2 = sbC("rstd2", [128, 1], F32)
        nmr2 = sbC("nmr2", [128, 1], F32)
        sqv2 = sbC("sqv2", [128, 1], F32)
        outr = sbC("outr", [128, 2, D_MODEL], F32)
        P = Prog(cnt, wm=wmA, gate=dict(ffn_done))
        for tt in range(NTT):
            fb = tt % 2
            lr_ = P.op("sync", lambda e, tt=tt, fb=fb: e.dma_start(
                out=outr[:, fb, :],
                in_=LN1F[128 * tt:128 * (tt + 1), :]),
                waits=[ffn_done, pcm.get(('outr_used', fb))], incs=d16("dLN%d" % fb))
            gg0 = P.op("gpsimd", lambda e, tt=tt: e.dma_gather(
                g0r[:, :, :], OUT2[:, :], cbidx[:, 8 * tt:8 * (tt + 1)],
                num_idxs=128, num_idxs_reg=128, elem_size=D_MODEL, transpose=False),
                waits=[ffn_done, pcm.get('g0_used')], incs=d16("gG0"))
            gg1 = P.op("gpsimd", lambda e, tt=tt: e.dma_gather(
                g1r[:, :, :], OUT2[:, :], cbidx[:, (T // 16) + 8 * tt:(T // 16) + 8 * (tt + 1)],
                num_idxs=128, num_idxs_reg=128, elem_size=D_MODEL, transpose=False),
                waits=[ffn_done, pcm.get('g1_used')], incs=d16("gG1"))
            c1 = P.op("vector", lambda e, tt=tt, fb=fb: e.scalar_tensor_tensor(
                fin[:, fb, :], g0r[:, 0, :], wgt[:, tt, 0:1], outr[:, fb, :],
                ALU.mult, ALU.add),
                waits=[lr_, gg0, pcm.get(('fin_used', fb))], incs=[("sDV", 1)])
            b7_ = P.op("sync", lambda e: e.nop(), waits=[c1], incs=[("sSP", 1)])
            c2 = P.op("vector", lambda e, tt=tt, fb=fb: e.scalar_tensor_tensor(
                fin[:, fb, :], g1r[:, 0, :], wgt[:, tt, 1:2], fin[:, fb, :],
                ALU.mult, ALU.add),
                waits=[b7_, gg1], incs=[("sDV", 1)])
            pcm['g0_used'] = c1
            pcm['g1_used'] = c2
            b8_ = P.op("sync", lambda e: e.nop(), waits=[c2], incs=[("sSP", 1)])
            bsa = P.op("vector", lambda e, fb=fb: e.bn_stats(bnst2[:, 0:6], fin[:, fb, 0:512]),
                       waits=[b8_], incs=[("sDV", 1)])
            bsb = P.op("vector", lambda e, fb=fb: e.bn_stats(bnst2[:, 6:12], fin[:, fb, 512:1024]),
                       waits=[b8_], incs=[("sDV", 1)])
            b9_ = P.op("sync", lambda e: e.nop(), waits=[bsa, bsb], incs=[("sSP", 1)])
            ag2 = P.op("vector", lambda e: e.bn_aggr(bnag2[:, :], bnst2[:, :]),
                       waits=[b9_], incs=[("sDV", 1)])
            sq2 = P.op("scalar", lambda e: e.activation(sqv2[:, :], bnag2[:, 1:2], AF.Sqrt,
                                                        bias=float(EPS)),
                       waits=[ag2], incs=[("sAC", 1)])
            rc2 = P.op("vector", lambda e: e.reciprocal(rstd2[:, :], sqv2[:, :]),
                       waits=[sq2], incs=[("sDV", 1)])
            ba_ = P.op("sync", lambda e: e.nop(), waits=[rc2], incs=[("sSP", 1)])
            nm2 = P.op("vector", lambda e: e.scalar_tensor_tensor(
                nmr2[:, :], bnag2[:, 0:1], -1.0, rstd2[:, :], ALU.mult, ALU.mult),
                waits=[ba_], incs=[("sDV", 1)])
            bb_ = P.op("sync", lambda e: e.nop(), waits=[nm2], incs=[("sSP", 1)])
            fo = P.op("vector", lambda e, fb=fb: e.tensor_scalar(
                outr[:, fb, :], fin[:, fb, :], rstd2[:, 0:1], nmr2[:, 0:1],
                ALU.mult, ALU.add),
                waits=[bb_], incs=[("sDV", 1)])
            pcm[('fin_used', fb)] = fo
            so_ = P.op("sync", lambda e, tt=tt, fb=fb: e.dma_start(
                out=OUT[128 * tt:128 * (tt + 1), :], in_=outr[:, fb, :]),
                waits=[fo], incs=d16("dOUT"))
            pcm[('outr_used', fb)] = [so_, c1]
        P.op("sync", lambda e: e.nop(),
             waits=[{s: cnt.get(s, 0) for s in SEMS if s != "sSP"}], incs=[("sSP", 1)])
        P.emit(nc, sems)


def b1r_pad(b1, HF):
    out = np.zeros((8 * 128, 64), np.float32)
    for e in range(8):
        out[e * 128:(e + 1) * 128, 0:HF // 128] = b1[e].reshape(HF // 128, 128).T
    return out


def iot16_const():
    a = np.zeros((16, 256), np.float32)
    for i in range(4096):
        a[i % 16, i // 16] = i
    return np.tile(a, (8, 1))


def kernel(**inputs):
    N_CORES = 8
    T = 8192
    HF = 4096
    cfg = dict(T=T, NB=2, CAP=16384, NEXP=8, H_FFN=HF,
               do_moe=True, n_cores=N_CORES)

    f32 = lambda a: np.ascontiguousarray(np.asarray(a), dtype=np.float32)
    x = f32(inputs['L0_attn_x']).reshape(16 * 4096, D_MODEL)
    Wq = f32(inputs['L0_attn_Wq']); Wk = f32(inputs['L0_attn_Wk'])
    Wv = f32(inputs['L0_attn_Wv']); Wo = f32(inputs['L0_attn_Wo'])
    UL = [f32(inputs['L0_attn_UL%d' % l]) for l in range(4)]
    VR = [f32(inputs['L0_attn_VR%d' % l]) for l in range(4)]
    UR = [f32(inputs['L0_attn_UR%d' % l]) for l in range(4)]
    VL = [f32(inputs['L0_attn_VL%d' % l]) for l in range(4)]
    Wr = f32(inputs['L0_ffn_Wr']); br = f32(inputs['L0_ffn_br']).reshape(1, 8)
    W1 = f32(inputs['L0_ffn_W1']); b1 = f32(inputs['L0_ffn_b1'])
    W2 = f32(inputs['L0_ffn_W2']); b2 = f32(inputs['L0_ffn_b2'])

    cv = build_canvases(UL, VR, UR, VL)
    pc = pure_constants(T // 128)
    slotm = np.zeros((128, 4), np.float32)
    for si in range(4):
        slotm[32 * si:32 * (si + 1), si] = 1.0

    common = {
        'wq': Wq, 'wk': Wk, 'wv': Wv, 'wo': Wo,
        'wot': np.ascontiguousarray(Wo.T), 'wr': Wr, 'br': br,
        'phi_vr': np.stack(cv['phi_vr']), 'phi_ul': np.stack(cv['phi_ul']),
        'phi_vl': np.stack(cv['phi_vl']), 'phi_lf': np.stack(cv['phi_lf']),
        'phi_ult_out': np.stack(cv['phi_ult_out']),
        'phi_urt_out': np.stack(cv['phi_urt_out']),
        'phi_ult_y': np.stack(cv['phi_ult_y']),
        'phi_urt_y': np.stack(cv['phi_urt_y']),
        'wmaskc': np.stack(cv['wmask']), 'leafm': cv['leafmask'],
        'ident': pc['ident'], 'triu4': pc['triu4'], 'tri64s': pc['tri64s'],
        'iota8': pc['iota8'], 'onesc': pc['onesc'], 'ones128': pc['ones128'],
        'toksrc': pc['toksrc'], 'cidx0': pc['cidx0'], 'slotm': slotm,
        'epsv': np.full((128, 1), 1e-5, np.float32),
        'w1': W1, 'w2': W2,
        'b1r': b1r_pad(b1, HF),
        'b2': np.ascontiguousarray(b2.reshape(8, 1, D_MODEL)),
        'iot16': iot16_const(),
        'tilv': (512.0 * np.arange(128, dtype=np.float32)).reshape(128, 1),
    }
    in_maps = []
    for c in range(N_CORES):
        prefm = np.zeros((8, 1), np.float32)
        prefm[:c, 0] = 1.0
        m = dict(common)
        m['x'] = x[T * c:T * (c + 1)]
        m['prefm'] = prefm
        in_maps.append(m)

    nc = build_program(cfg)
    res = run_bass_kernel_spmd(nc, in_maps, core_ids=list(range(N_CORES)))
    out = np.concatenate([np.asarray(res.results[c]['out']) for c in range(N_CORES)], axis=0)
    return out.reshape(16, 4096, D_MODEL).astype(np.float32, copy=False)
